# version 09
# baseline (speedup 1.0000x reference)
import math
import numpy as np
import ml_dtypes

import concourse.bass as bass
import concourse.mybir as mybir
from concourse.bass_utils import run_bass_kernel_spmd

F32 = mybir.dt.float32
BF16 = mybir.dt.bfloat16
I8 = mybir.dt.int8
I16 = mybir.dt.int16
AF = mybir.ActivationFunctionType
OP = mybir.AluOpType
AX = mybir.AxisListType
BF = ml_dtypes.bfloat16

H = 2048
S = 4096
TOK = 1024
NH = 16
HD = 128
INNER = 8192
EPS = 1e-5
TAU = 1.0 / math.sqrt(float(HD))
EXPC = 11.0

PERM_HEADS = []
for _g in range(4):
    for _p in (2 * _g, 2 * _g + 1):
        PERM_HEADS += [_p, _p + 8]
CHAN_PERM = np.concatenate([np.arange(128 * h, 128 * h + 128) for h in PERM_HEADS])



class Edge:
    __slots__ = ("sem", "inc", "count")

    def __init__(self, sem, inc):
        self.sem = sem
        self.inc = inc
        self.count = 0


class EM:

    def __init__(self, nc):
        self.nc = nc
        self.cur = None
        self.e = None
        self.edges = {}

    def edge(self, name, inc=1):
        ed = Edge(self.nc.alloc_semaphore(name), inc)
        self.edges[name] = ed
        return ed

    def on(self, eng):
        return self.cur == eng

    def sig(self, ed, ins=None, n=1):
        ed.count += n
        if ins is not None:
            ins.then_inc(ed.sem, n * ed.inc)
        return ed.count

    def wait_cur(self, ed, lag=0):
        n = ed.count - lag
        if n > 0 and self.e is not None:
            self.e.wait_ge(ed.sem, n * ed.inc)

    def wait_n(self, ed, n):
        if n > 0 and self.e is not None:
            self.e.wait_ge(ed.sem, n * ed.inc)

    def snap(self):
        return {k: v.count for k, v in self.edges.items()}

    def restore(self, s):
        for k, v in self.edges.items():
            v.count = s.get(k, 0)


ENGS = ("sync", "tensor", "vector", "scalar", "gpsimd")


PHASE_LIMIT = None
ATTN_LIMIT = None
ATTN_HEADS = None


def run_phase(block, em, C, fn):
    if PHASE_LIMIT is not None:
        n = getattr(run_phase, "_n", 0)
        run_phase._n = n + 1
        if n >= PHASE_LIMIT:
            return
    snap = em.snap()
    for i, eng in enumerate(ENGS):
        if i > 0:
            em.restore(snap)
        em.cur = eng

        def body(e, _fn=fn):
            em.e = e
            _fn(em, C)

        getattr(block, eng)(body)
    em.cur = None
    em.e = None


class Ctx:
    pass


def barrier(em, C):
    ed = em.edges["ph"]
    ins = None
    if em.e is not None:
        if em.cur == "vector":
            ins = em.e.memset(C.dumb.ap()[0:1, 0:1], 0.0)
        elif em.cur == "scalar":
            ins = em.e.memzero(C.dumb.ap()[0:1, 1:2])
        elif em.cur == "gpsimd":
            ins = em.e.memset(C.dumb.ap()[0:1, 2:3], 0.0)
        elif em.cur == "tensor":
            ins = em.e.ldweights(C.dumbW.ap())
        else:
            for nm in ("stA0", "stA1", "stB0", "stB1", "st_out"):
                em.wait_cur(em.edges[nm])
            ins = em.e.nop()
    em.sig(ed, ins)
    em.wait_n(ed, 5 * ed.count)


def _host_tables():
    inv_freq = 1.0 / 10000.0 ** (np.arange(0, H, 2, dtype=np.float64) / H)
    t = np.arange(S, dtype=np.float64)
    ph = np.outer(inv_freq, t)
    return np.cos(ph).astype(np.float32), np.sin(ph).astype(np.float32)




def build_nc():
    nc = bass.Bass("TRN2", target_bir_lowering=False)
    C = Ctx()
    C.nc = nc

    def din(name, shape, dt=F32):
        return nc.dram_tensor(name, list(shape), dt, kind="ExternalInput").ap()

    C.xT = din("xT", (H, S))
    C.xTo = din("xTo", (H, TOK))
    C.wqT = din("wqT", (H, H))
    C.wkT = din("wkT", (H, H))
    C.wvT = din("wvT", (H, H))
    C.woT = din("woT", (H, H))
    C.w1T = din("w1T", (H, INNER))
    C.w2T = din("w2T", (INNER, H))
    C.b1r = din("b1r", (1, INNER))
    C.lngr = din("lngr", (1, INNER))
    C.lnbr = din("lnbr", (1, INNER))
    C.b2r = din("b2r", (1, H))
    C.gater = din("gater", (1, NH))
    C.tabc = din("tabc", (1024, S))
    C.tabs = din("tabs", (1024, S))
    C.tabco = din("tabco", (1024, TOK))
    C.tabso = din("tabso", (1024, TOK))
    C.xo = din("xo", (TOK, H))
    C.out = nc.dram_tensor("out", [TOK, H], F32, kind="ExternalOutput").ap()

    C.xnfm_h = nc.dram_tensor("xnfm_h", [H, S], BF16, kind="Internal").ap()
    C.xnown_h = nc.dram_tensor("xnown_h", [H, TOK], BF16, kind="Internal").ap()
    C.xq1_h = nc.dram_tensor("xq1_h", [H, TOK], BF16, kind="Internal").ap()
    C.q1_h = nc.dram_tensor("q1_h", [H, INNER], I8, kind="Internal").ap()
    C.q2_h = nc.dram_tensor("q2_h", [INNER, H], I8, kind="Internal").ap()
    C.comb_h = nc.dram_tensor("comb_h", [H, TOK], BF16, kind="Internal").ap()
    C.acc_h = nc.dram_tensor("acc_h", [TOK, H], F32, kind="Internal").ap()
    C.g_h = nc.dram_tensor("g_h", [TOK, INNER], BF16, kind="Internal").ap()

    C.eye_bf_h = nc.inline_tensor(np.eye(128, dtype=BF), "eye_bf").ap()
    C.eye_f_h = nc.inline_tensor(np.eye(128, dtype=np.float32), "eye_f").ap()

    A = nc.alloc_sbuf_tensor
    C.stats = A("stats", [128, 64], F32)
    C.stats2 = A("stats2", [128, 32], F32)
    C.nrm = A("nrm", [128, 4], F32)
    C.negb = A("negb", [128, 2], F32)
    C.wsB = A("wsB", [128, 4], F32)
    C.srow = A("srow", [1, 1024], F32)
    C.brow = A("brow", [1, 1024], BF16)
    C.ones_bf = A("ones_bf", [128, 1], BF16)
    C.ones_f = A("ones_f", [128, 1], F32)
    C.onescol_bf = A("onescol_bf", [1, 128], BF16)
    C.onescol_f = A("onescol_f", [1, 128], F32)
    C.grow = A("grow", [1, NH], F32)
    C.g1m = A("g1m", [1, NH], F32)
    C.eyeS = A("eyeS", [128, 128], BF16)
    C.dumb = A("dumb", [1, 8], F32)
    C.dumbW = A("dumbW", [128, 128], BF16)
    C.bcol = A("bcol", [128, 4], BF16)

    em = EM(nc)
    C.em = em

    for name, inc in [
        ("ph", 1), ("pre_c", 1),
        ("led", 16), ("ldA0", 16), ("ldA1", 16),
        ("gled", 16), ("gledT", 16), ("gldA0", 16), ("gldA1", 16),
        ("stA0", 16), ("stA1", 16), ("stB0", 16), ("stB1", 16),
        ("st_out", 16),
        ("p0_cv", 1), ("p0_cw", 1), ("p0_mm", 1), ("p0_fin", 1), ("p0_bc_t", 1), ("p0_bc_v", 1),
        ("b_g", 1), ("b_v", 1),
        ("c_s", 1), ("c_v", 1), ("c_fin", 1), ("c_t", 1),
        ("d_g", 1), ("d_vv", 1), ("d_s", 1),
        ("w_s", 1), ("w_v", 1), ("w_fin", 1), ("w_t", 1),
        ("wq_s", 1), ("wq_g", 1), ("wq_v", 1),
        ("q_mm", 1), ("q_ev", 1),
        ("a_kv", 1), ("a_mm", 1), ("a_rope_g", 1), ("a_rope_v", 1),
        ("a_nrm_t", 1), ("a_nrm_v", 1), ("a_nrm_s", 1), ("a_nrm_g", 1),
        ("a_elu_g", 1), ("a_elu_s", 1), ("a_elu_v", 1),
        ("a_tp", 1), ("a_tpv", 1), ("a_mem", 1), ("a_memv", 1),
        ("a_ts", 1), ("a_st", 1), ("a_tv", 1),
        ("a_ep1", 1), ("a_td", 1), ("a_ep2", 1), ("a_tb", 1), ("a_ep3", 1),
        ("wo_mm", 1), ("wo_ev", 1),
        ("f1_mm", 1), ("f1_ev", 1), ("f1_s1", 1), ("f1_s2", 1), ("f1_g2", 1),
        ("f1_fin", 1),
        ("f2_v", 1), ("f2_s", 1), ("f2_fin", 1), ("f2_t", 1),
        ("f3_mm", 1), ("f3_ev", 1), ("f3_t", 1), ("f3_tv", 1),
    ]:
        em.edge(name, inc)
    for alias, base in [
        ("pre_g", "led"), ("p0_row", "led"), ("c_row", "led"), ("q_ld", "led"),
        ("a_ldw", "gled"), ("a_ldt", "gledT"), ("wo_ld", "led"), ("f1_ld", "led"),
        ("f2_ld", "led"), ("f3_lda", "led"),
        ("p0_ld0", "ldA0"), ("b_ld0", "ldA0"), ("c_ld0", "ldA0"), ("d_ld0", "ldA0"),
        ("w_ld0", "ldA0"), ("wq_ld0", "ldA0"), ("a_ldx0", "ldA0"), ("wo_ldx0", "ldA0"),
        ("f1_ldq0", "gldA0"), ("f3_ldq0", "gldA0"),
        ("p0_ld1", "ldA1"), ("b_ld1", "ldA1"), ("c_ld1", "ldA1"), ("d_ld1", "ldA1"),
        ("w_ld1", "ldA1"), ("wq_ld1", "ldA1"), ("a_ldx1", "ldA1"), ("wo_ldx1", "ldA1"),
        ("f1_ldq1", "gldA1"), ("f3_ldq1", "gldA1"),
        ("b_st0", "stA0"), ("d_sa0", "stA0"), ("wq_st0", "stA0"), ("wo_st0", "stA0"),
        ("f1_st0", "stA0"),
        ("b_st1", "stA1"), ("d_sa1", "stA1"), ("wq_st1", "stA1"), ("wo_st1", "stA1"),
        ("f1_st1", "stA1"),
        ("d_sb0", "stB0"), ("a_cb0", "stB0"),
        ("d_sb1", "stB1"), ("a_cb1", "stB1"),
    ]:
        em.edges[alias] = em.edges[base]

    return nc, C, em


def emit_all(nc, C, em):
    import contextlib
    ED = em.edges
    A = nc.alloc_sbuf_tensor

    for n in ("sf_v", "sf_s", "sf_g", "sf_t", "sf_y"):
        em.edge(n, 1)

    def fence(em, eng, ed_name, ins):
        ed = ED[ed_name]
        em.sig(ed, ins)
        if em.on(eng):
            em.wait_cur(ed)

    with contextlib.ExitStack() as OUTER, nc.Block() as block:

        sc_eye = contextlib.ExitStack()
        C.eyeSf = sc_eye.enter_context(nc.sbuf_tensor("eyeSf", [128, 128], F32))

        def ph_pre(em, C):
            pre_g, pre_c = ED["pre_g"], ED["pre_c"]
            if em.on("vector"):
                em.e.memset(C.ones_bf.ap(), 1.0)
                em.e.memset(C.ones_f.ap(), 1.0)
                em.e.memset(C.onescol_bf.ap(), 1.0)
                ins = em.e.memset(C.onescol_f.ap(), 1.0)
            else:
                ins = None
            em.sig(ED["sf_v"], ins)
            ins = em.e.dma_start(out=C.grow.ap(), in_=C.gater) if em.on("sync") else None
            em.sig(pre_g, ins)
            ins = em.e.dma_start(out=C.eyeS.ap(), in_=C.eye_bf_h) if em.on("sync") else None
            em.sig(pre_g, ins)
            ins = em.e.dma_start(out=C.eyeSf.ap(), in_=C.eye_f_h) if em.on("sync") else None
            em.sig(pre_g, ins)
            if em.on("scalar"):
                em.wait_cur(pre_g)
                ins = em.e.activation(C.srow.ap()[0:1, 0:NH], C.grow.ap(), AF.Sigmoid)
            else:
                ins = None
            em.sig(pre_c, ins)
            if em.on("vector"):
                em.wait_cur(pre_c)
                em.e.tensor_copy(C.grow.ap(), C.srow.ap()[0:1, 0:NH])
                ins = em.e.tensor_scalar(C.g1m.ap(), C.srow.ap()[0:1, 0:NH],
                                         -1.0, 1.0, OP.mult, OP.add)
            else:
                ins = None
            em.sig(pre_c, ins)
            barrier(em, C)

        run_phase(block, em, C, ph_pre)

        sc2 = OUTER.enter_context(contextlib.ExitStack())
        C.fbuf = [sc2.enter_context(nc.sbuf_tensor(f"fbuf{i}", [128, S], F32)) for i in range(2)]
        C.row_a = sc2.enter_context(nc.sbuf_tensor("row_a", [1, S], F32))
        C.row_b = sc2.enter_context(nc.sbuf_tensor("row_b", [1, S], F32))
        C.istdB = sc2.enter_context(nc.sbuf_tensor("istdB", [128, S], F32))
        C.muistdB = sc2.enter_context(nc.sbuf_tensor("muistdB", [128, S], F32))
        sc1 = contextlib.ExitStack()
        C.xb = [sc1.enter_context(nc.sbuf_tensor(f"xb{i}", [128, S], BF16)) for i in range(2)]
        C.xsq = [sc1.enter_context(nc.sbuf_tensor(f"xsq{i}", [128, S], BF16)) for i in range(2)]
        C.muT = sc1.enter_context(nc.sbuf_tensor("muT", [128, 32], F32))
        C.msqT = sc1.enter_context(nc.sbuf_tensor("msqT", [128, 32], F32))
        C.tmpA = sc1.enter_context(nc.sbuf_tensor("tmpA", [128, 32], F32))
        C.tmpB = sc1.enter_context(nc.sbuf_tensor("tmpB", [128, 32], F32))
        C.r32a = sc1.enter_context(nc.sbuf_tensor("r32a", [32, 128], F32))
        C.r32b = sc1.enter_context(nc.sbuf_tensor("r32b", [32, 128], F32))
        C.pmu = sc1.enter_context(nc.psum_tensor("pmu", [128, 512], F32))
        C.pms = sc1.enter_context(nc.psum_tensor("pms", [128, 512], F32))
        C.pg = [sc1.enter_context(nc.psum_tensor(f"pg{i}", [128, 512], F32)) for i in range(2)]
        C.pt32 = sc1.enter_context(nc.psum_tensor("pt32", [128, 512], F32))

        def ph_p0a(em, C):
            ld = [ED["p0_ld0"], ED["p0_ld1"]]
            cv, cw, mm, fin = ED["p0_cv"], ED["p0_cw"], ED["p0_mm"], ED["p0_fin"]
            bct, bcv2 = ED["p0_bc_t"], ED["p0_bc_v"]
            prow = ED["p0_row"]
            b_cv, b_cw, b_mm, b_fin = cv.count, cw.count, mm.count, fin.count
            for k in range(16):
                if em.on("sync"):
                    if k >= 2:
                        em.wait_n(mm, b_mm + k - 1)
                    ins = em.e.dma_start(out=C.fbuf[k % 2].ap(),
                                         in_=C.xT[128 * k:128 * (k + 1), :])
                else:
                    ins = None
                em.sig(ld[k % 2], ins)
                if em.on("scalar"):
                    em.wait_cur(ld[k % 2])
                    ins = em.e.activation(C.xsq[k % 2].ap(), C.fbuf[k % 2].ap(), AF.Square)
                else:
                    ins = None
                em.sig(cv, ins)
                if em.on("vector"):
                    em.wait_cur(ld[k % 2])
                    ins = em.e.tensor_copy(C.xb[k % 2].ap(), C.fbuf[k % 2].ap())
                else:
                    ins = None
                em.sig(cw, ins)
                if em.on("tensor"):
                    em.wait_n(cv, b_cv + k + 1)
                    em.wait_n(cw, b_cw + k + 1)
                    for j in range(32):
                        em.e.matmul(C.pmu.ap()[:, j:j + 1],
                                    C.xb[k % 2].ap()[:, 128 * j:128 * (j + 1)],
                                    C.ones_bf.ap(), start=(k == 0 and j == 0),
                                    stop=(k == 15 and j == 31))
                    ins = None
                    for j in range(32):
                        ins = em.e.matmul(C.pms.ap()[:, j:j + 1],
                                          C.xsq[k % 2].ap()[:, 128 * j:128 * (j + 1)],
                                          C.ones_bf.ap(), start=(k == 0 and j == 0),
                                          stop=(k == 15 and j == 31))
                else:
                    ins = None
                em.sig(mm, ins)
            if em.on("vector"):
                em.wait_n(mm, b_mm + 16)
                em.e.tensor_scalar(C.muT.ap(), C.pmu.ap()[:, 0:32], 1.0 / float(H), None, OP.mult)
                ins = em.e.tensor_scalar(C.msqT.ap(), C.pms.ap()[:, 0:32], 1.0 / float(H), None, OP.mult)
            else:
                ins = None
            fence(em, "vector", "sf_v", ins)
            if em.on("vector"):
                ins = em.e.tensor_tensor(C.tmpA.ap(), C.muT.ap(), C.muT.ap(), OP.mult)
            else:
                ins = None
            fence(em, "vector", "sf_v", ins)
            if em.on("vector"):
                ins = em.e.tensor_tensor(C.tmpB.ap(), C.msqT.ap(), C.tmpA.ap(), OP.subtract)
            else:
                ins = None
            fence(em, "vector", "sf_v", ins)
            if em.on("vector"):
                ins = em.e.tensor_scalar(C.tmpA.ap(), C.tmpB.ap(), EPS, None, OP.add)
            else:
                ins = None
            em.sig(fin, ins)
            if em.on("scalar"):
                em.wait_n(fin, b_fin + 1)
                ins = em.e.activation(C.tmpB.ap(), C.tmpA.ap(), AF.Sqrt)
            else:
                ins = None
            em.sig(fin, ins)
            if em.on("vector"):
                em.wait_n(fin, b_fin + 2)
                ins = em.e.reciprocal(C.tmpA.ap(), C.tmpB.ap())
            else:
                ins = None
            fence(em, "vector", "sf_v", ins)
            if em.on("vector"):
                ins = em.e.tensor_tensor(C.msqT.ap(), C.muT.ap(), C.tmpA.ap(), OP.mult)
            else:
                ins = None
            em.sig(fin, ins)
            b_bct, b_bcv = bct.count, bcv2.count
            if em.on("tensor"):
                em.wait_n(fin, b_fin + 3)
                ins = em.e.transpose(C.pt32.ap()[0:32, 0:128], C.tmpA.ap(), C.eyeSf.ap())
            else:
                ins = None
            em.sig(bct, ins)
            if em.on("vector"):
                em.wait_n(bct, b_bct + 1)
                ins = em.e.tensor_copy(C.r32a.ap(), C.pt32.ap()[0:32, 0:128])
            else:
                ins = None
            em.sig(bcv2, ins)
            if em.on("tensor"):
                em.wait_n(bcv2, b_bcv + 1)
                ins = em.e.transpose(C.pt32.ap()[0:32, 0:128], C.msqT.ap(), C.eyeSf.ap())
            else:
                ins = None
            em.sig(bct, ins)
            if em.on("vector"):
                em.wait_n(bct, b_bct + 2)
                ins = em.e.tensor_copy(C.r32b.ap(), C.pt32.ap()[0:32, 0:128])
            else:
                ins = None
            em.sig(bcv2, ins)
            b_prow = prow.count
            if em.on("sync"):
                em.wait_n(bcv2, b_bcv + 2)
                ins = em.e.dma_start(out=C.row_a.ap(), in_=C.r32a.ap())
                em.sig(prow, ins)
                ins = em.e.dma_start(out=C.row_b.ap(), in_=C.r32b.ap())
                em.sig(prow, ins)
            else:
                em.sig(prow, None)
                em.sig(prow, None)
            b2t, b2v = bct.count, bcv2.count
            for it in range(16):
                src = C.row_a if it < 8 else C.row_b
                dst = C.istdB if it < 8 else C.muistdB
                s = (it % 8) * 512
                if em.on("tensor"):
                    em.wait_n(prow, b_prow + 2)
                    if it >= 2:
                        em.wait_n(bcv2, b2v + it - 1)
                    ins = em.e.matmul(C.pg[it % 2].ap(), C.onescol_f.ap(),
                                      src.ap()[0:1, s:s + 512], start=True, stop=True)
                else:
                    ins = None
                em.sig(bct, ins)
                if em.on("vector"):
                    em.wait_n(bct, b2t + it + 1)
                    ins = em.e.tensor_copy(dst.ap()[:, s:s + 512], C.pg[it % 2].ap())
                else:
                    ins = None
                em.sig(bcv2, ins)
            barrier(em, C)

        run_phase(block, em, C, ph_p0a)
        sc1.close()

        sc3 = contextlib.ExitStack()
        C.tg = [sc3.enter_context(nc.sbuf_tensor(f"tg{i}", [128, S], F32)) for i in range(2)]
        C.xnb = [sc3.enter_context(nc.sbuf_tensor(f"xnb{i}", [128, S], BF16)) for i in range(2)]

        def ph_p0b(em, C):
            ld = [ED["b_ld0"], ED["b_ld1"]]
            st = [ED["b_st0"], ED["b_st1"]]
            eg, ev = ED["b_g"], ED["b_v"]
            b_g, b_v = eg.count, ev.count
            for k in range(16):
                if em.on("sync"):
                    if k >= 2:
                        em.wait_n(eg, b_g + k - 1)
                    ins = em.e.dma_start(out=C.fbuf[k % 2].ap(),
                                         in_=C.xT[128 * k:128 * (k + 1), :])
                else:
                    ins = None
                em.sig(ld[k % 2], ins)
                if em.on("gpsimd"):
                    em.wait_cur(ld[k % 2])
                    if k >= 2:
                        em.wait_n(ev, b_v + k - 1)
                    ins = em.e.tensor_tensor(C.tg[k % 2].ap(), C.fbuf[k % 2].ap(),
                                             C.istdB.ap(), OP.mult)
                else:
                    ins = None
                em.sig(eg, ins)
                if em.on("vector"):
                    em.wait_n(eg, b_g + k + 1)
                    em.wait_cur(st[k % 2])
                    ins = em.e.tensor_tensor(C.xnb[k % 2].ap(), C.tg[k % 2].ap(),
                                             C.muistdB.ap(), OP.subtract)
                else:
                    ins = None
                em.sig(ev, ins)
                if em.on("sync"):
                    em.wait_n(ev, b_v + k + 1)
                    ins = em.e.dma_start(out=C.xnfm_h[128 * k:128 * (k + 1), :],
                                         in_=C.xnb[k % 2].ap())
                else:
                    ins = None
                em.sig(st[k % 2], ins)
            barrier(em, C)

        run_phase(block, em, C, ph_p0b)
        sc3.close()
        sc2.close()

        sc4 = OUTER.enter_context(contextlib.ExitStack())
        C.muoB = sc4.enter_context(nc.sbuf_tensor("muoB", [128, TOK], F32))
        C.istdoB = sc4.enter_context(nc.sbuf_tensor("istdoB", [128, TOK], F32))
        C.c1oB = sc4.enter_context(nc.sbuf_tensor("c1oB", [128, TOK], F32))
        sc5 = contextlib.ExitStack()
        C.obuf = [sc5.enter_context(nc.sbuf_tensor(f"obuf{i}", [128, H], F32)) for i in range(2)]
        C.dmpS = [sc5.enter_context(nc.sbuf_tensor(f"dmpS{i}", [128, H], F32)) for i in range(2)]
        C.dmpV = [sc5.enter_context(nc.sbuf_tensor(f"dmpV{i}", [128, H], F32)) for i in range(2)]
        C.t8 = [sc5.enter_context(nc.sbuf_tensor(f"t8_{i}", [128, 8], F32)) for i in range(4)]
        C.ro8 = sc5.enter_context(nc.sbuf_tensor("ro8", [8, 128], F32))
        C.rowo = sc5.enter_context(nc.sbuf_tensor("rowo", [1, TOK], F32))
        C.pso = [sc5.enter_context(nc.psum_tensor(f"pso{i}", [128, 512], F32)) for i in range(2)]
        C.pt8 = sc5.enter_context(nc.psum_tensor("pt8", [128, 512], F32))

        def ph_p0c(em, C):
            ld = [ED["c_ld0"], ED["c_ld1"]]
            crow = ED["c_row"]
            es, ev, fin, et = ED["c_s"], ED["c_v"], ED["c_fin"], ED["c_t"]
            b_s, b_v = es.count, ev.count
            for t in range(8):
                if em.on("sync"):
                    if t >= 2:
                        em.wait_n(es, b_s + t - 1)
                        em.wait_n(ev, b_v + t - 1)
                    ins = em.e.dma_start(out=C.obuf[t % 2].ap(),
                                         in_=C.xo[128 * t:128 * (t + 1), :])
                else:
                    ins = None
                em.sig(ld[t % 2], ins)
                if em.on("scalar"):
                    em.wait_cur(ld[t % 2])
                    if t >= 2:
                        em.wait_cur(es, lag=1)
                    ins = em.e.activation(C.dmpS[t % 2].ap(), C.obuf[t % 2].ap(),
                                          AF.Identity,
                                          accum_out=C.t8[0].ap()[:, t:t + 1])
                else:
                    ins = None
                em.sig(es, ins)
                if em.on("vector"):
                    em.wait_cur(ld[t % 2])
                    if t >= 2:
                        em.wait_cur(ev, lag=1)
                    ins = em.e.scalar_tensor_tensor(C.dmpV[t % 2].ap(), C.obuf[t % 2].ap(),
                                                    0.0, C.obuf[t % 2].ap(),
                                                    OP.bypass, OP.mult,
                                                    accum_out=C.t8[1].ap()[:, t:t + 1])
                else:
                    ins = None
                em.sig(ev, ins)
            if em.on("vector"):
                em.wait_n(es, b_s + 8)
                em.wait_n(ev, b_v + 8)
                ins = em.e.tensor_scalar(C.stats.ap()[:, 0:8], C.t8[0].ap(),
                                         1.0 / float(H), None, OP.mult)
            else:
                ins = None
            fence(em, "vector", "sf_v", ins)
            if em.on("vector"):
                em.e.tensor_scalar(C.t8[2].ap(), C.t8[1].ap(), 1.0 / float(H), None, OP.mult)
                ins = em.e.tensor_tensor(C.t8[3].ap(), C.stats.ap()[:, 0:8],
                                         C.stats.ap()[:, 0:8], OP.mult)
            else:
                ins = None
            fence(em, "vector", "sf_v", ins)
            if em.on("vector"):
                ins = em.e.tensor_tensor(C.t8[2].ap(), C.t8[2].ap(), C.t8[3].ap(), OP.subtract)
            else:
                ins = None
            fence(em, "vector", "sf_v", ins)
            if em.on("vector"):
                ins = em.e.tensor_scalar(C.t8[3].ap(), C.t8[2].ap(), EPS, None, OP.add)
            else:
                ins = None
            em.sig(fin, ins)
            if em.on("scalar"):
                em.wait_cur(fin)
                ins = em.e.activation(C.t8[1].ap(), C.t8[3].ap(), AF.Sqrt)
            else:
                ins = None
            em.sig(fin, ins)
            if em.on("vector"):
                em.wait_cur(fin)
                ins = em.e.reciprocal(C.stats.ap()[:, 8:16], C.t8[1].ap())
            else:
                ins = None
            em.sig(fin, ins)
            b_v2 = ev.count
            for t in range(8):
                if em.on("sync"):
                    em.wait_n(es, b_s + 8)
                    em.wait_n(ev, b_v + 8)
                    if t >= 2:
                        em.wait_n(ev, b_v2 + 2 * (t - 1))
                    ins = em.e.dma_start(out=C.obuf[t % 2].ap(),
                                         in_=C.xo[128 * t:128 * (t + 1), :])
                else:
                    ins = None
                em.sig(ld[t % 2], ins)
                if em.on("vector"):
                    em.wait_cur(ld[t % 2])
                    em.wait_n(fin, ED["c_fin"].count)
                    if t >= 2:
                        em.wait_n(ev, b_v2 + 2 * (t - 1))
                    ins = em.e.tensor_scalar(C.dmpV[t % 2].ap(), C.obuf[t % 2].ap(),
                                             C.stats.ap()[:, 0 + t:1 + t],
                                             C.stats.ap()[:, 8 + t:9 + t],
                                             OP.subtract, OP.mult)
                else:
                    ins = None
                em.sig(ev, ins)
                if em.on("vector"):
                    em.wait_cur(ev)
                    ins = em.e.tensor_reduce(C.stats.ap()[:, 16 + t:17 + t],
                                             C.dmpV[t % 2].ap(), AX.X, OP.max,
                                             apply_absolute_value=True)
                else:
                    ins = None
                em.sig(ev, ins)
            if em.on("vector"):
                em.wait_n(ev, b_v2 + 16)
                ins = em.e.tensor_scalar(C.t8[3].ap(), C.t8[2].ap(), EPS, None, OP.add)
            else:
                ins = None
            fence(em, "vector", "sf_v", ins)
            if em.on("vector"):
                ins = em.e.reciprocal(C.t8[1].ap(), C.t8[3].ap())
            else:
                ins = None
            fence(em, "vector", "sf_v", ins)
            if em.on("vector"):
                ins = em.e.tensor_tensor(C.t8[3].ap(), C.t8[2].ap(), C.t8[1].ap(), OP.mult)
            else:
                ins = None
            fence(em, "vector", "sf_v", ins)
            if em.on("vector"):
                ins = em.e.tensor_scalar(C.t8[1].ap(), C.t8[3].ap(), EPS, None, OP.add)
            else:
                ins = None
            em.sig(fin, ins)
            if em.on("scalar"):
                em.wait_cur(fin)
                ins = em.e.activation(C.t8[3].ap(), C.t8[1].ap(), AF.Sqrt)
            else:
                ins = None
            em.sig(fin, ins)
            if em.on("vector"):
                em.wait_cur(fin)
                ins = em.e.reciprocal(C.t8[1].ap(), C.t8[3].ap())
            else:
                ins = None
            fence(em, "vector", "sf_v", ins)
            if em.on("vector"):
                ins = em.e.tensor_tensor(C.t8[3].ap(), C.t8[1].ap(),
                                         C.stats.ap()[:, 16:24], OP.mult)
            else:
                ins = None
            fence(em, "vector", "sf_v", ins)
            if em.on("vector"):
                ins = em.e.tensor_scalar(C.t8[2].ap(), C.t8[3].ap(), 1e-5, None, OP.max)
            else:
                ins = None
            fence(em, "vector", "sf_v", ins)
            if em.on("vector"):
                ins = em.e.reciprocal(C.t8[3].ap(), C.t8[2].ap())
            else:
                ins = None
            fence(em, "vector", "sf_v", ins)
            if em.on("vector"):
                ins = em.e.tensor_scalar(C.stats.ap()[:, 32:40], C.t8[3].ap(),
                                         127.0, None, OP.mult)
            else:
                ins = None
            fence(em, "vector", "sf_v", ins)
            if em.on("vector"):
                ins = em.e.tensor_tensor(C.stats.ap()[:, 24:32], C.t8[1].ap(),
                                         C.stats.ap()[:, 32:40], OP.mult)
            else:
                ins = None
            em.sig(fin, ins)
            b_t, b_v3 = et.count, ev.count
            for r in range(3):
                src = [C.stats.ap()[:, 0:8], C.stats.ap()[:, 8:16], C.stats.ap()[:, 24:32]][r]
                dst = [C.muoB, C.istdoB, C.c1oB][r]
                if em.on("tensor"):
                    em.wait_n(fin, ED["c_fin"].count)
                    if r >= 1:
                        em.wait_n(ev, b_v3 + 3 * r)
                    ins = em.e.transpose(C.pt8.ap()[0:8, 0:128], src, C.eyeSf.ap())
                else:
                    ins = None
                em.sig(et, ins)
                if em.on("vector"):
                    em.wait_n(et, b_t + 3 * r + 1)
                    ins = em.e.tensor_copy(C.ro8.ap(), C.pt8.ap()[0:8, 0:128])
                else:
                    ins = None
                em.sig(ev, ins)
                if em.on("sync"):
                    em.wait_n(ev, b_v3 + 3 * r + 1)
                    em.wait_n(et, b_t + 3 * r)
                    ins = em.e.dma_start(out=C.rowo.ap(), in_=C.ro8.ap())
                else:
                    ins = None
                em.sig(crow, ins)
                for half in range(2):
                    if em.on("tensor"):
                        em.wait_cur(crow)
                        ins = em.e.matmul(C.pso[half].ap(), C.onescol_f.ap(),
                                          C.rowo.ap()[0:1, 512 * half:512 * (half + 1)],
                                          start=True, stop=True)
                    else:
                        ins = None
                    em.sig(et, ins)
                    if em.on("vector"):
                        em.wait_n(et, b_t + 3 * r + 2 + half)
                        ins = em.e.tensor_copy(dst.ap()[:, 512 * half:512 * (half + 1)],
                                               C.pso[half].ap())
                    else:
                        ins = None
                    em.sig(ev, ins)
            barrier(em, C)

        run_phase(block, em, C, ph_p0c)
        sc5.close()

        sc6 = contextlib.ExitStack()
        C.fo = [sc6.enter_context(nc.sbuf_tensor(f"fo{i}", [128, TOK], F32)) for i in range(2)]
        C.ta = [sc6.enter_context(nc.sbuf_tensor(f"ta{i}", [128, TOK], F32)) for i in range(2)]
        C.xnf = [sc6.enter_context(nc.sbuf_tensor(f"xnf{i}", [128, TOK], F32)) for i in range(2)]
        C.uu = [sc6.enter_context(nc.sbuf_tensor(f"uu{i}", [128, TOK], F32)) for i in range(2)]
        C.xnbo = [sc6.enter_context(nc.sbuf_tensor(f"xnbo{i}", [128, TOK], BF16)) for i in range(2)]
        C.i16o = [sc6.enter_context(nc.sbuf_tensor(f"i16o{i}", [128, TOK], I16)) for i in range(2)]
        C.i16d = [sc6.enter_context(nc.sbuf_tensor(f"i16d{i}", [128, TOK], I16)) for i in range(2)]
        C.fdf = [sc6.enter_context(nc.sbuf_tensor(f"fdf{i}", [128, TOK], F32)) for i in range(2)]
        C.xq1b = [sc6.enter_context(nc.sbuf_tensor(f"xq1b{i}", [128, TOK], BF16)) for i in range(2)]

        def ph_p0d(em, C):
            ld = [ED["d_ld0"], ED["d_ld1"]]
            sa = [ED["d_sa0"], ED["d_sa1"]]
            sb = [ED["d_sb0"], ED["d_sb1"]]
            dg, dv, ds = ED["d_g"], ED["d_vv"], ED["d_s"]
            b_g, b_v, b_s = dg.count, dv.count, ds.count
            for k in range(16):
                if em.on("sync"):
                    if k >= 2:
                        em.wait_n(dg, b_g + 4 * (k - 2) + 1)
                    ins = em.e.dma_start(out=C.fo[k % 2].ap(),
                                         in_=C.xTo[128 * k:128 * (k + 1), :])
                else:
                    ins = None
                em.sig(ld[k % 2], ins)
                if em.on("gpsimd"):
                    em.wait_cur(ld[k % 2])
                    if k >= 2:
                        em.wait_n(dv, b_v + 3 * (k - 2) + 1)
                    ins = em.e.tensor_tensor(C.ta[k % 2].ap(), C.fo[k % 2].ap(),
                                             C.muoB.ap(), OP.subtract)
                else:
                    ins = None
                em.sig(dg, ins)
                if em.on("vector"):
                    em.wait_n(dg, b_g + 4 * k + 1)
                    if k >= 2:
                        em.wait_n(dg, b_g + 4 * (k - 2) + 2)
                        em.wait_n(dv, b_v + 3 * (k - 2) + 2)
                    ins = em.e.tensor_tensor(C.xnf[k % 2].ap(), C.ta[k % 2].ap(),
                                             C.istdoB.ap(), OP.mult)
                else:
                    ins = None
                em.sig(dv, ins)
                if em.on("gpsimd"):
                    em.wait_n(dv, b_v + 3 * k + 1)
                    em.wait_cur(sa[k % 2])
                    ins = em.e.tensor_copy(C.xnbo[k % 2].ap(), C.xnf[k % 2].ap())
                else:
                    ins = None
                em.sig(dg, ins)
                if em.on("vector"):
                    em.wait_n(dv, b_v + 3 * k + 1)
                    if k >= 2:
                        em.wait_n(ds, b_s + 2 * (k - 2) + 1)
                        em.wait_n(dg, b_g + 4 * (k - 2) + 4)
                    ins = em.e.tensor_tensor(C.uu[k % 2].ap(), C.xnf[k % 2].ap(),
                                             C.c1oB.ap(), OP.mult)
                else:
                    ins = None
                em.sig(dv, ins)
                if em.on("gpsimd"):
                    em.wait_n(dv, b_v + 3 * k + 2)
                    if k >= 2:
                        em.wait_n(ds, b_s + 2 * (k - 2) + 1)
                        em.wait_n(dg, b_g + 4 * (k - 2) + 4)
                    ins = em.e.tensor_scalar(C.uu[k % 2].ap(), C.uu[k % 2].ap(),
                                             12582912.0, -12582912.0, OP.add, OP.add)
                else:
                    ins = None
                em.sig(dg, ins)
                if em.on("scalar"):
                    em.wait_n(dg, b_g + 4 * k + 3)
                    if k >= 2:
                        em.wait_n(dv, b_v + 3 * (k - 2) + 3)
                    ins = em.e.copy(C.i16o[k % 2].ap(), C.uu[k % 2].ap())
                else:
                    ins = None
                em.sig(ds, ins)
                if em.on("gpsimd"):
                    em.wait_n(dg, b_g + 4 * k + 3)
                    em.wait_n(ds, b_s + 2 * k + 1)
                    if k >= 2:
                        em.wait_n(ds, b_s + 2 * (k - 2) + 2)
                    ins = em.e.tensor_tensor(C.fdf[k % 2].ap(), C.uu[k % 2].ap(),
                                             C.i16o[k % 2].ap(), OP.subtract)
                else:
                    ins = None
                em.sig(dg, ins)
                if em.on("scalar"):
                    em.wait_n(dg, b_g + 4 * k + 4)
                    ins = em.e.activation(C.i16d[k % 2].ap(), C.fdf[k % 2].ap(),
                                          AF.Identity, scale=2.0)
                else:
                    ins = None
                em.sig(ds, ins)
                if em.on("vector"):
                    em.wait_n(ds, b_s + 2 * k + 2)
                    em.wait_cur(sb[k % 2])
                    ins = em.e.tensor_tensor(C.xq1b[k % 2].ap(), C.i16o[k % 2].ap(),
                                             C.i16d[k % 2].ap(), OP.add)
                else:
                    ins = None
                em.sig(dv, ins)
                if em.on("sync"):
                    em.wait_n(dg, b_g + 4 * k + 2)
                    ins = em.e.dma_start(out=C.xnown_h[128 * k:128 * (k + 1), :],
                                         in_=C.xnbo[k % 2].ap())
                    em.sig(sa[k % 2], ins)
                    em.wait_n(dv, b_v + 3 * k + 3)
                    ins = em.e.dma_start(out=C.xq1_h[128 * k:128 * (k + 1), :],
                                         in_=C.xq1b[k % 2].ap())
                    em.sig(sb[k % 2], ins)
                else:
                    em.sig(sa[k % 2], None)
                    em.sig(sb[k % 2], None)
            barrier(em, C)

        run_phase(block, em, C, ph_p0d)
        sc6.close()
        sc4.close()
        sc_eye.close()

        sc7 = contextlib.ExitStack()
        C.fw = [sc7.enter_context(nc.sbuf_tensor(f"fw{i}", [128, INNER], F32)) for i in range(2)]
        C.dw = [sc7.enter_context(nc.sbuf_tensor(f"dw{i}", [128, INNER], BF16)) for i in range(2)]
        C.pW = sc7.enter_context(nc.psum_tensor("pW", [128, 512], F32))
        C.aw1 = sc7.enter_context(nc.sbuf_tensor("aw1", [128, 16], F32))
        C.aw2 = sc7.enter_context(nc.sbuf_tensor("aw2", [128, 64], F32))

        def ph_wmean(em, C):
            ld = [ED["w_ld0"], ED["w_ld1"]]
            ws_, wv, wfin, wt = ED["w_s"], ED["w_v"], ED["w_fin"], ED["w_t"]
            b_s, b_v = ws_.count, wv.count
            for k in range(16):
                if em.on("sync"):
                    if k >= 2:
                        em.wait_n(ws_, b_s + k - 1)
                    ins = em.e.dma_start(out=C.fw[k % 2].ap(),
                                         in_=C.w1T[128 * k:128 * (k + 1), :])
                else:
                    ins = None
                em.sig(ld[k % 2], ins)
                if em.on("scalar"):
                    em.wait_cur(ld[k % 2])
                    if k >= 2:
                        em.wait_cur(ws_, lag=1)
                    ins = em.e.activation(C.dw[k % 2].ap(), C.fw[k % 2].ap(), AF.Abs,
                                          accum_out=C.aw1.ap()[:, k:k + 1])
                else:
                    ins = None
                em.sig(ws_, ins)
            b_v2 = wv.count
            for k in range(64):
                if em.on("sync"):
                    em.wait_n(ws_, b_s + 16)
                    if k >= 2:
                        em.wait_n(wv, b_v2 + k - 1)
                    ins = em.e.dma_start(out=C.fw[k % 2].ap()[:, 0:H],
                                         in_=C.w2T[128 * k:128 * (k + 1), :])
                else:
                    ins = None
                em.sig(ld[k % 2], ins)
                if em.on("vector"):
                    em.wait_cur(ld[k % 2])
                    ins = em.e.tensor_reduce(C.aw2.ap()[:, k:k + 1], C.fw[k % 2].ap()[:, 0:H],
                                             AX.X, OP.add, apply_absolute_value=True)
                else:
                    ins = None
                em.sig(wv, ins)
            for w in range(2):
                aw = C.aw1 if w == 0 else C.aw2
                nel = float(H * INNER)
                if em.on("vector"):
                    em.wait_n(ws_, b_s + 16)
                    em.wait_n(wv, b_v2 + 64)
                    ins = em.e.tensor_reduce(C.nrm.ap()[:, 0:1], aw.ap(), AX.X, OP.add)
                else:
                    ins = None
                em.sig(wfin, ins)
                if em.on("tensor"):
                    em.wait_cur(wfin)
                    ins = em.e.matmul(C.pW.ap()[0:1, 0:1], C.nrm.ap()[:, 0:1],
                                      C.ones_f.ap(), start=True, stop=True)
                else:
                    ins = None
                em.sig(wt, ins)
                if em.on("vector"):
                    em.wait_cur(wt)
                    ins = em.e.tensor_scalar(C.srow.ap()[0:1, 100:101], C.pW.ap()[0:1, 0:1],
                                             1.0 / nel, 1e-5, OP.mult, OP.max)
                else:
                    ins = None
                fence(em, "vector", "sf_v", ins)
                if em.on("vector"):
                    ins = em.e.reciprocal(C.srow.ap()[0:1, 101:102], C.srow.ap()[0:1, 100:101])
                else:
                    ins = None
                em.sig(wfin, ins)
                if em.on("tensor"):
                    em.wait_cur(wfin)
                    ins = em.e.matmul(C.pW.ap()[:, 0:1], C.onescol_f.ap(),
                                      C.srow.ap()[0:1, 101:102], start=True, stop=True)
                else:
                    ins = None
                em.sig(wt, ins)
                if em.on("vector"):
                    em.wait_cur(wt)
                    ins = em.e.tensor_copy(C.wsB.ap()[:, w:w + 1], C.pW.ap()[:, 0:1])
                else:
                    ins = None
                fence(em, "vector", "sf_v", ins)
                if em.on("vector"):
                    ins = em.e.tensor_scalar(C.wsB.ap()[:, 2 + w:3 + w],
                                             C.wsB.ap()[:, w:w + 1], 2.0, None, OP.mult)
                else:
                    ins = None
                em.sig(wfin, ins)
            barrier(em, C)

        run_phase(block, em, C, ph_wmean)

        sc7b = contextlib.ExitStack()
        C.i8a = [sc7b.enter_context(nc.sbuf_tensor(f"i8a{i}", [128, INNER], I8)) for i in range(2)]
        C.i8b = [sc7b.enter_context(nc.sbuf_tensor(f"i8b{i}", [128, INNER], I8)) for i in range(2)]

        def ph_wquant(em, C):
            ld = [ED["wq_ld0"], ED["wq_ld1"]]
            st = [ED["wq_st0"], ED["wq_st1"]]
            qs, qg, qv = ED["wq_s"], ED["wq_g"], ED["wq_v"]
            for w in range(2):
                nk = 16 if w == 0 else 64
                ncol = INNER if w == 0 else H
                src = C.w1T if w == 0 else C.w2T
                dst = C.q1_h if w == 0 else C.q2_h
                b_s, b_v = qs.count, qv.count
                for k in range(-1, nk):
                    j = k + 1
                    if j < nk:
                        if em.on("sync"):
                            em.wait_n(qs, b_s)
                            em.wait_n(qv, b_v)
                            if j >= 2:
                                em.wait_n(qs, b_s + j - 1)
                                em.wait_n(qv, b_v + 3 * (j - 2) + 2)
                            ins = em.e.dma_start(out=C.fw[j % 2].ap()[:, 0:ncol],
                                                 in_=src[128 * j:128 * (j + 1), :])
                        else:
                            ins = None
                        em.sig(ld[j % 2], ins)
                    if k < 0:
                        continue
                    if em.on("scalar"):
                        em.wait_cur(ld[k % 2])
                        if k >= 2:
                            em.wait_n(qv, b_v + 3 * (k - 2) + 2)
                        ins = em.e.activation(C.fw[k % 2].ap()[:, 0:ncol],
                                              C.fw[k % 2].ap()[:, 0:ncol],
                                              AF.Identity, scale=C.wsB.ap()[:, w:w + 1])
                    else:
                        ins = None
                    em.sig(qs, ins)
                    if em.on("vector"):
                        em.wait_n(qs, b_s + k + 1)
                        if k >= 2:
                            em.wait_n(qv, b_v + 3 * (k - 2) + 2)
                        ins = em.e.tensor_scalar(C.fw[k % 2].ap()[:, 0:ncol],
                                                 C.fw[k % 2].ap()[:, 0:ncol],
                                                 12582912.0, -12582912.0, OP.add, OP.add)
                    else:
                        ins = None
                    em.sig(qv, ins)
                    if em.on("vector"):
                        em.wait_n(qv, b_v + 3 * k + 1)
                        if k >= 2:
                            em.wait_n(qv, b_v + 3 * (k - 2) + 3)
                        ins = em.e.tensor_scalar(C.i8a[k % 2].ap()[:, 0:ncol],
                                                 C.fw[k % 2].ap()[:, 0:ncol],
                                                 2.0, None, OP.mult)
                    else:
                        ins = None
                    em.sig(qv, ins)
                    if em.on("vector"):
                        em.wait_n(qv, b_v + 3 * k + 2)
                        em.wait_cur(st[k % 2])
                        ins = em.e.tensor_scalar(C.i8b[k % 2].ap()[:, 0:ncol],
                                                 C.i8a[k % 2].ap()[:, 0:ncol],
                                                 -1.0, 1.0, OP.max, OP.min)
                    else:
                        ins = None
                    em.sig(qv, ins)
                    if em.on("sync"):
                        em.wait_n(qv, b_v + 3 * k + 3)
                        ins = em.e.dma_start(out=dst[128 * k:128 * (k + 1), :],
                                             in_=C.i8b[k % 2].ap()[:, 0:ncol])
                    else:
                        ins = None
                    em.sig(st[k % 2], ins)
            barrier(em, C)

        run_phase(block, em, C, ph_wquant)
        sc7b.close()
        sc7.close()

        sc_qfm = contextlib.ExitStack()
        C.qfm = sc_qfm.enter_context(nc.sbuf_tensor("qfm", [128, 16 * 1024], BF16))
        sc8 = contextlib.ExitStack()
        C.wqbf = sc8.enter_context(nc.sbuf_tensor("wqbf", [128, 16 * H], BF16))
        C.xnq = sc8.enter_context(nc.sbuf_tensor("xnq", [128, 16 * TOK], BF16))
        C.pq = [sc8.enter_context(nc.psum_tensor(f"pq{i}", [128, 512], F32)) for i in range(2)]

        def ph_q(em, C):
            ld, qmm, qev = ED["q_ld"], ED["q_mm"], ED["q_ev"]
            b_ld, b_mm, b_ev = ld.count, qmm.count, qev.count
            ins = (em.e.dma_start(out=C.wqbf.ap().rearrange("p (k o) -> p k o", k=16),
                                  in_=C.wqT.rearrange("(k p) o -> p k o", p=128))
                   if em.on("gpsimd") else None)
            em.sig(ED["gled"], ins)
            ins = (em.e.dma_start(out=C.xnq.ap().rearrange("p (k t) -> p k t", k=16),
                                  in_=C.xnown_h.rearrange("(k p) t -> p k t", p=128))
                   if em.on("sync") else None)
            em.sig(ld, ins)
            for j in range(16):
                for half in range(2):
                    idx = 2 * j + half
                    if em.on("tensor"):
                        em.wait_cur(ld)
                        em.wait_cur(ED["gled"])
                        if idx >= 2:
                            em.wait_n(qev, b_ev + idx - 1)
                        ins = None
                        for kk in range(16):
                            ins = em.e.matmul(
                                C.pq[idx % 2].ap(),
                                C.wqbf.ap()[:, 2048 * kk + 128 * j:2048 * kk + 128 * (j + 1)],
                                C.xnq.ap()[:, 1024 * kk + 512 * half:1024 * kk + 512 * (half + 1)],
                                start=(kk == 0), stop=(kk == 15))
                    else:
                        ins = None
                    em.sig(qmm, ins)
                    if em.on("vector"):
                        em.wait_n(qmm, b_mm + idx + 1)
                        ins = em.e.tensor_copy(
                            C.qfm.ap()[:, 1024 * j + 512 * half:1024 * j + 512 * (half + 1)],
                            C.pq[idx % 2].ap())
                    else:
                        ins = None
                    em.sig(qev, ins)
            barrier(em, C)

        run_phase(block, em, C, ph_q)
        sc8.close()

        sc9 = contextlib.ExitStack()
        C.kraw = sc9.enter_context(nc.sbuf_tensor("kraw", [128, 4 * S], BF16))
        C.vtm = sc9.enter_context(nc.sbuf_tensor("vtm", [128, 32 * 512], BF16))
        C.kftm = sc9.enter_context(nc.sbuf_tensor("kftm", [128, S], BF16))
        C.scrA = sc9.enter_context(nc.sbuf_tensor("scrA", [128, S], BF16))
        C.scrB = sc9.enter_context(nc.sbuf_tensor("scrB", [128, S], BF16))
        C.kx = [sc9.enter_context(nc.sbuf_tensor(f"kx{i}", [128, S], BF16)) for i in range(2)]
        C.wkbf = sc9.enter_context(nc.sbuf_tensor("wkbf", [128, 16 * 512], BF16))
        C.wvbf = sc9.enter_context(nc.sbuf_tensor("wvbf", [128, 16 * 512], BF16))
        C.tbc = sc9.enter_context(nc.sbuf_tensor("tbc", [128, S], BF16))
        C.tbs = sc9.enter_context(nc.sbuf_tensor("tbs", [128, S], BF16))
        C.cbuf = sc9.enter_context(nc.sbuf_tensor("cbuf", [128, 512], BF16))
        C.scl = sc9.enter_context(nc.sbuf_tensor("scl", [1, 16], F32))
        C.pA = sc9.enter_context(nc.psum_tensor("pA", [128, 1024], F32))
        C.pB = sc9.enter_context(nc.psum_tensor("pB", [128, 1024], F32))
        C.pC = sc9.enter_context(nc.psum_tensor("pC", [128, 512], F32))
        C.pD = sc9.enter_context(nc.psum_tensor("pD", [128, 512], F32))
        C.pE = sc9.enter_context(nc.psum_tensor("pE", [128, 512], F32))
        C.pTb = sc9.enter_context(nc.psum_tensor("pTb", [128, 512], BF16))
        C.PT = sc9.enter_context(nc.sbuf_tensor("PT", [128, 2048], BF16))
        C.qf = sc9.enter_context(nc.sbuf_tensor("qf", [128, 1024], BF16))
        C.mem_bf = sc9.enter_context(nc.sbuf_tensor("mem_bf", [128, 132], BF16))

        def ph_attn(em, C):
            ldw = ED["a_ldw"]
            ldx = [ED["a_ldx0"], ED["a_ldx1"]]
            ldt = ED["a_ldt"]
            akv, amm = ED["a_kv"], ED["a_mm"]
            arg, arv = ED["a_rope_g"], ED["a_rope_v"]
            ant, anv, ans, ang = ED["a_nrm_t"], ED["a_nrm_v"], ED["a_nrm_s"], ED["a_nrm_g"]
            aeg, aes, aev = ED["a_elu_g"], ED["a_elu_s"], ED["a_elu_v"]
            atp, atpv = ED["a_tp"], ED["a_tpv"]
            amem, amemv = ED["a_mem"], ED["a_memv"]
            ats, ast, atv = ED["a_ts"], ED["a_st"], ED["a_tv"]
            ep1, atd, ep2, atb, ep3 = ED["a_ep1"], ED["a_td"], ED["a_ep2"], ED["a_tb"], ED["a_ep3"]
            acb = [ED["a_cb0"], ED["a_cb1"]]

            hj_done = 0
            hrec = []

            for grp in range(4 if ATTN_LIMIT is None else ATTN_LIMIT):
                b_mm = amm.count
                b_kv = akv.count
                if em.on("gpsimd"):
                    em.wait_n(amm, b_mm)
                    ins = em.e.dma_start(
                        out=C.wkbf.ap().rearrange("p (k c) -> p k c", k=16),
                        in_=C.wkT[:, 512 * grp:512 * (grp + 1)].rearrange(
                            "(k p) c -> p k c", p=128))
                    em.sig(ldw, ins)
                    ins = em.e.dma_start(
                        out=C.wvbf.ap().rearrange("p (k c) -> p k c", k=16),
                        in_=C.wvT[:, 512 * grp:512 * (grp + 1)].rearrange(
                            "(k p) c -> p k c", p=128))
                    em.sig(ldw, ins)
                else:
                    em.sig(ldw, None)
                    em.sig(ldw, None)
                for hc in range(16):
                    if em.on("sync"):
                        em.wait_n(amm, b_mm)
                        if hc < 2:
                            em.wait_cur(ats)
                            em.wait_cur(ans)
                            em.wait_cur(aeg)
                            em.wait_cur(aev)
                        if hc >= 2:
                            em.wait_n(amm, b_mm + hc - 1)
                        ins = em.e.dma_start(
                            out=C.kx[hc % 2].ap().rearrange("p (k t) -> p k t", k=16),
                            in_=C.xnfm_h[:, 256 * hc:256 * (hc + 1)].rearrange(
                                "(k p) t -> p k t", p=128))
                    else:
                        ins = None
                    em.sig(ldx[hc % 2], ins)
                    if em.on("tensor"):
                        em.wait_cur(ldw)
                        em.wait_cur(ldx[hc % 2])
                        if hc >= 1:
                            em.wait_n(akv, b_kv + 6 * hc)
                        for ct in range(4):
                            for kk in range(16):
                                em.e.matmul(
                                    C.pA.ap()[:, 256 * ct:256 * (ct + 1)],
                                    C.wkbf.ap()[:, 512 * kk + 128 * ct:512 * kk + 128 * (ct + 1)],
                                    C.kx[hc % 2].ap()[:, 256 * kk:256 * (kk + 1)],
                                    start=(kk == 0 and ct % 2 == 0),
                                    stop=(kk == 15 and ct % 2 == 1))
                        ins = None
                        for tt in range(2):
                            for kk in range(16):
                                ins = em.e.matmul(
                                    C.pB.ap()[:, 512 * tt:512 * (tt + 1)],
                                    C.kx[hc % 2].ap()[:, 256 * kk + 128 * tt:256 * kk + 128 * (tt + 1)],
                                    C.wvbf.ap()[:, 512 * kk:512 * (kk + 1)],
                                    start=(kk == 0), stop=(kk == 15))
                    else:
                        ins = None
                    em.sig(amm, ins)
                    for ct in range(4):
                        if em.on("vector"):
                            em.wait_n(amm, b_mm + hc + 1)
                            ins = em.e.tensor_copy(
                                C.kraw.ap()[:, 4096 * ct + 256 * hc:4096 * ct + 256 * (hc + 1)],
                                C.pA.ap()[:, 256 * ct:256 * (ct + 1)])
                        else:
                            ins = None
                        em.sig(akv, ins)
                    for tt in range(2):
                        if em.on("vector"):
                            ins = em.e.tensor_copy(
                                C.vtm.ap()[:, 512 * (2 * hc + tt):512 * (2 * hc + tt + 1)],
                                C.pB.ap()[:, 512 * tt:512 * (tt + 1)])
                        else:
                            ins = None
                        em.sig(akv, ins)

                prod_amm = ED["a_mm"].count
                for ct in range(4):
                    if (ATTN_LIMIT is not None and grp == ATTN_LIMIT - 1
                            and ATTN_HEADS is not None and ct >= ATTN_HEADS):
                        break
                    hh = 4 * grp + ct
                    kro = C.kx[ct % 2].ap()
                    prev2 = hrec[-2] if len(hrec) >= 2 else None
                    if ct % 2 == 0:
                        pair = 2 * grp + ct // 2
                        if em.on("gpsimd"):
                            em.wait_cur(arv)
                            em.wait_cur(aev)
                            em.wait_cur(aeg)
                            em.wait_cur(arg)
                            em.wait_cur(atp)
                            em.wait_cur(ED["sf_v"])
                            for _src, _dst in ((C.tabc, C.tbc.ap()), (C.tabs, C.tbs.ap()),
                                               (C.tabco, C.scrA.ap()[:, 2048:3072]),
                                               (C.tabso, C.scrA.ap()[:, 3072:4096])):
                                ins = em.e.dma_start(out=_dst,
                                                     in_=_src[128 * pair:128 * (pair + 1), :])
                                em.sig(ldt, ins)
                        else:
                            for _ in range(4):
                                em.sig(ldt, None)
                        rA, rB = hh, hh + 1
                        qA = C.qfm.ap()[:, 1024 * rA:1024 * (rA + 1)]
                        qB = C.qfm.ap()[:, 1024 * rB:1024 * (rB + 1)]
                        if em.on("gpsimd"):
                            em.wait_cur(ldt)
                            em.wait_cur(aev)
                            em.wait_cur(ep3)
                            em.e.tensor_tensor(C.scrA.ap()[:, 1024:2048], qB,
                                               C.scrA.ap()[:, 3072:4096], OP.mult)
                            ins = em.e.tensor_tensor(C.scrB.ap()[:, 1024:2048], qA,
                                                     C.scrA.ap()[:, 3072:4096], OP.mult)
                        else:
                            ins = None
                        em.sig(arg, ins)
                        if em.on("vector"):
                            em.wait_cur(ldt)
                            em.wait_cur(ep3)
                            em.e.tensor_tensor(C.scrA.ap()[:, 0:1024], qA,
                                               C.scrA.ap()[:, 2048:3072], OP.mult)
                            ins = em.e.tensor_tensor(C.scrB.ap()[:, 0:1024], qB,
                                                     C.scrA.ap()[:, 2048:3072], OP.mult)
                        else:
                            ins = None
                        fence(em, "vector", "sf_v", ins)
                        if em.on("vector"):
                            em.wait_cur(arg)
                            em.e.tensor_tensor(qA, C.scrA.ap()[:, 0:1024],
                                               C.scrA.ap()[:, 1024:2048], OP.subtract)
                            ins = em.e.tensor_tensor(qB, C.scrB.ap()[:, 0:1024],
                                                     C.scrB.ap()[:, 1024:2048], OP.add)
                        else:
                            ins = None
                        em.sig(arv, ins)

                    own = C.kraw.ap()[:, 4096 * ct:4096 * (ct + 1)]
                    prt = C.kraw.ap()[:, 4096 * (ct ^ 1):4096 * ((ct ^ 1) + 1)]
                    if em.on("gpsimd"):
                        em.wait_n(akv, akv.count)
                        em.wait_cur(ldt)
                        em.wait_cur(aev)
                        em.wait_cur(arv)
                        em.wait_cur(atp)
                        ins = em.e.tensor_tensor(C.scrA.ap(), own, C.tbc.ap(), OP.mult)
                    else:
                        ins = None
                    em.sig(arg, ins)
                    if em.on("vector"):
                        em.wait_n(akv, akv.count)
                        em.wait_cur(ldt)
                        em.wait_cur(aes)
                        em.wait_cur(arv)
                        ins = em.e.tensor_tensor(C.scrB.ap(), prt, C.tbs.ap(), OP.mult)
                    else:
                        ins = None
                    fence(em, "vector", "sf_v", ins)
                    if em.on("vector"):
                        em.wait_cur(arg)
                        em.wait_n(amm, prod_amm)
                        if prev2 is not None:
                            em.wait_n(ats, prev2["ats"])
                            em.wait_n(aeg, prev2["aeg"])
                            em.wait_n(aev, prev2["aev"])
                            em.wait_n(ans, prev2["ans"])
                        ins = em.e.tensor_tensor(
                            kro, C.scrA.ap(), C.scrB.ap(),
                            OP.subtract if ct % 2 == 0 else OP.add)
                    else:
                        ins = None
                    em.sig(arv, ins)

                    if em.on("scalar"):
                        em.wait_cur(arv)
                        em.wait_cur(aev)
                        ins = em.e.activation(C.scrB.ap(), kro, AF.Square)
                    else:
                        ins = None
                    em.sig(ans, ins)
                    if em.on("tensor"):
                        em.wait_cur(ans)
                        em.wait_cur(amemv)
                        em.wait_cur(anv)
                        ins = None
                        for i in range(32):
                            ins = em.e.matmul(C.pE.ap()[:, 136 + i:137 + i],
                                              C.scrB.ap()[:, 128 * i:128 * (i + 1)],
                                              C.ones_bf.ap(), start=(i == 0), stop=False)
                    else:
                        ins = None
                    em.sig(ant, ins)
                    if em.on("scalar"):
                        em.wait_cur(ant)
                        ins = em.e.activation(C.scrB.ap()[:, 0:1024],
                                              C.qfm.ap()[:, 1024 * hh:1024 * (hh + 1)],
                                              AF.Square)
                    else:
                        ins = None
                    em.sig(ans, ins)
                    if em.on("tensor"):
                        em.wait_cur(ans)
                        ins = None
                        for t in range(8):
                            ins = em.e.matmul(C.pE.ap()[:, 168 + t:169 + t],
                                              C.scrB.ap()[:, 128 * t:128 * (t + 1)],
                                              C.ones_bf.ap(), start=False, stop=(t == 7))
                    else:
                        ins = None
                    em.sig(ant, ins)
                    if em.on("vector"):
                        em.wait_cur(ant)
                        ins = em.e.tensor_reduce(C.nrm.ap()[:, 1:2], C.pE.ap()[:, 136:168],
                                                 AX.X, OP.max)
                    else:
                        ins = None
                    fence(em, "vector", "sf_v", ins)
                    if em.on("vector"):
                        ins = em.e.tensor_reduce(C.nrm.ap()[:, 2:3], C.pE.ap()[:, 168:176],
                                                 AX.X, OP.max)
                    else:
                        ins = None
                    em.sig(anv, ins)
                    if em.on("vector"):
                        em.wait_cur(anv)
                        ins = em.e.tensor_copy(C.bcol.ap()[:, 0:2], C.nrm.ap()[:, 1:3])
                    else:
                        ins = None
                    em.sig(anv, ins)
                    if em.on("tensor"):
                        em.wait_cur(anv)
                        em.wait_cur(atpv)
                        ins = em.e.transpose(C.pTb.ap()[0:2, 0:128],
                                             C.bcol.ap()[:, 0:2], C.eyeS.ap())
                    else:
                        ins = None
                    em.sig(ant, ins)
                    if em.on("vector"):
                        em.wait_cur(ant)
                        em.wait_cur(ans)
                        ins = em.e.tensor_reduce(C.bcol.ap()[0:2, 2:3],
                                                 C.pTb.ap()[0:2, 0:128], AX.X, OP.max)
                    else:
                        ins = None
                    em.sig(anv, ins)
                    if em.on("scalar"):
                        em.wait_cur(anv)
                        em.wait_cur(ant)
                        ins = em.e.activation(C.nrm.ap()[0:2, 0:1], C.bcol.ap()[0:2, 2:3],
                                              AF.Ln)
                    else:
                        ins = None
                    em.sig(ans, ins)
                    if em.on("tensor"):
                        em.wait_cur(ans)
                        em.wait_cur(anv)
                        em.wait_n(ep3, hj_done)
                        ins = em.e.matmul(C.pC.ap()[0:1, 0:1], C.nrm.ap()[0:2, 0:1],
                                          C.ones_f.ap()[0:2, 0:1], start=True, stop=True)
                    else:
                        ins = None
                    em.sig(ant, ins)
                    if em.on("scalar"):
                        em.wait_cur(ant)
                        em.wait_cur(anv)
                        ins = em.e.activation(C.scl.ap()[0:1, 3:4], C.pC.ap()[0:1, 0:1],
                                              AF.Exp, scale=0.25)
                    else:
                        ins = None
                    em.sig(ans, ins)
                    if em.on("vector"):
                        em.wait_cur(ans)
                        ins = em.e.tensor_scalar(C.scl.ap()[0:1, 4:5], C.scl.ap()[0:1, 3:4],
                                                 -TAU, EXPC, OP.mult, OP.add)
                    else:
                        ins = None
                    em.sig(anv, ins)
                    if em.on("tensor"):
                        em.wait_cur(anv)
                        em.wait_cur(ans)
                        ins = em.e.matmul(C.pC.ap()[:, 0:1], C.onescol_f.ap(),
                                          C.scl.ap()[0:1, 4:5], start=True, stop=True)
                    else:
                        ins = None
                    em.sig(ant, ins)
                    if em.on("vector"):
                        em.wait_cur(ant)
                        if prev2 is not None:
                            em.wait_n(ast, prev2["ast"])
                        ins = em.e.tensor_copy(C.negb.ap()[:, hh % 2:hh % 2 + 1],
                                               C.pC.ap()[:, 0:1])
                    else:
                        ins = None
                    em.sig(anv, ins)

                    if em.on("vector"):
                        em.wait_cur(arv)
                        ins = em.e.tensor_scalar(C.scrA.ap(), kro, 0.0, None, OP.min)
                    else:
                        ins = None
                    em.sig(aeg, ins)
                    if em.on("scalar"):
                        em.wait_cur(aeg)
                        em.wait_cur(ans)
                        em.wait_cur(ant)
                        ins = em.e.activation(C.scrB.ap(), C.scrA.ap(), AF.Exp)
                    else:
                        ins = None
                    em.sig(aes, ins)
                    if em.on("vector"):
                        em.wait_cur(aes)
                        ins = em.e.scalar_tensor_tensor(C.scrA.ap(), kro, 1.0,
                                                        C.scrB.ap(), OP.add, OP.max)
                    else:
                        ins = None
                    em.sig(aev, ins)
                    b_tp, b_tpv = atp.count, atpv.count
                    for bt in range(8):
                        if em.on("tensor"):
                            em.wait_cur(aev)
                            em.wait_cur(anv)
                            if bt >= 1:
                                em.wait_n(atpv, b_tpv + bt)
                            ins = None
                            for u in range(4):
                                i = 4 * bt + u
                                ins = em.e.transpose(C.pTb.ap()[:, 128 * u:128 * (u + 1)],
                                                     C.scrA.ap()[:, 128 * i:128 * (i + 1)],
                                                     C.eyeS.ap())
                        else:
                            ins = None
                        em.sig(atp, ins)
                        if em.on("vector"):
                            em.wait_n(atp, b_tp + bt + 1)
                            ins = em.e.tensor_copy(
                                C.kftm.ap()[:, 512 * bt:512 * (bt + 1)], C.pTb.ap())
                        else:
                            ins = None
                        em.sig(atpv, ins)
                    if em.on("tensor"):
                        em.wait_n(atpv, b_tpv + 8)
                        em.wait_cur(anv)
                        for i in range(32):
                            em.e.matmul(C.pE.ap()[:, 0:128],
                                        C.kftm.ap()[:, 128 * i:128 * (i + 1)],
                                        C.vtm.ap()[:, 512 * i + 128 * ct:512 * i + 128 * (ct + 1)],
                                        start=(i == 0), stop=False)
                        ins = None
                        for i in range(32):
                            ins = em.e.matmul(C.pE.ap()[:, 128:129],
                                              C.kftm.ap()[:, 128 * i:128 * (i + 1)],
                                              C.ones_bf.ap(), start=False, stop=(i == 31))
                    else:
                        ins = None
                    em.sig(amem, ins)
                    if em.on("vector"):
                        em.wait_cur(amem)
                        em.wait_cur(atd)
                        ins = em.e.tensor_copy(C.mem_bf.ap()[:, 0:129], C.pE.ap()[:, 0:129])
                    else:
                        ins = None
                    em.sig(amemv, ins)
                    if em.on("vector"):
                        em.wait_cur(aev)
                        ins = em.e.tensor_scalar(C.scrB.ap()[:, 0:1024],
                                                 C.qfm.ap()[:, 1024 * hh:1024 * (hh + 1)],
                                                 0.0, None, OP.min)
                    else:
                        ins = None
                    em.sig(aeg, ins)
                    if em.on("scalar"):
                        em.wait_cur(aeg)
                        ins = em.e.activation(C.scrB.ap()[:, 1024:2048],
                                              C.scrB.ap()[:, 0:1024], AF.Exp)
                    else:
                        ins = None
                    em.sig(aes, ins)
                    if em.on("vector"):
                        em.wait_cur(aes)
                        em.wait_cur(atd)
                        ins = em.e.scalar_tensor_tensor(C.qf.ap(),
                                                        C.qfm.ap()[:, 1024 * hh:1024 * (hh + 1)],
                                                        1.0, C.scrB.ap()[:, 1024:2048],
                                                        OP.add, OP.max)
                    else:
                        ins = None
                    em.sig(aev, ins)

                    for j in range(2):
                        b_ts, b_st, b_tv = ats.count, ast.count, atv.count
                        rhsq = C.qfm.ap()[:, 1024 * hh + 512 * j:1024 * hh + 512 * (j + 1)]
                        def _sslot(p):
                            bank = C.pA if (p % 4) < 2 else C.pB
                            return bank.ap()[:, 512 * (p % 2):512 * (p % 2 + 1)]

                        def _ptslot(p):
                            return C.PT.ap()[:, 512 * (p % 4):512 * (p % 4 + 1)]

                        for p in range(33):
                            if em.on("tensor"):
                                if p == 0:
                                    em.wait_n(ep3, hj_done)
                                    em.wait_cur(arv)
                                    em.wait_cur(aev)
                                if p < 32:
                                    if p >= 4:
                                        em.wait_n(ast, b_st + p - 3)
                                    ins = em.e.matmul(_sslot(p),
                                                      kro[:, 128 * p:128 * (p + 1)],
                                                      rhsq, start=True, stop=True)
                                else:
                                    ins = None
                            else:
                                ins = None
                            if p < 32:
                                em.sig(ats, ins)
                            if p >= 1:
                                pp = p - 1
                                if em.on("tensor"):
                                    em.wait_n(ast, b_st + pp + 1)
                                    em.wait_cur(amemv)
                                    em.e.matmul(
                                        C.pC.ap(),
                                        C.vtm.ap()[:, 512 * pp + 128 * ct:512 * pp + 128 * (ct + 1)],
                                        _ptslot(pp),
                                        start=(pp == 0), stop=(pp == 31))
                                    ins = em.e.matmul(
                                        C.pD.ap()[0:1, 0:512], C.ones_bf.ap(),
                                        _ptslot(pp),
                                        start=(pp == 0), stop=(pp == 31))
                                else:
                                    ins = None
                                em.sig(atv, ins)
                            if p < 32:
                                if em.on("scalar"):
                                    em.wait_n(ats, b_ts + p + 1)
                                    if p >= 4:
                                        em.wait_n(atv, b_tv + p - 3)
                                    if p == 0:
                                        em.wait_cur(anv)
                                    ins = em.e.activation(
                                        _ptslot(p), _sslot(p), AF.Exp,
                                        bias=C.negb.ap()[:, hh % 2:hh % 2 + 1], scale=TAU)
                                else:
                                    ins = None
                                em.sig(ast, ins)

                        if em.on("vector"):
                            em.wait_n(atv, b_tv + 32)
                            ins = em.e.reciprocal(C.srow.ap()[0:1, 0:512], C.pD.ap()[0:1, 0:512])
                        else:
                            ins = None
                        fence(em, "vector", "sf_v", ins)
                        if em.on("vector"):
                            ins = em.e.tensor_scalar(C.srow.ap()[0:1, 512:1024],
                                                     C.srow.ap()[0:1, 0:512],
                                                     C.g1m.ap()[0:1, PERM_HEADS[hh]:PERM_HEADS[hh] + 1],
                                                     None, OP.mult)
                        else:
                            ins = None
                        fence(em, "vector", "sf_v", ins)
                        if em.on("vector"):
                            ins = em.e.tensor_copy(C.brow.ap()[0:1, 0:512],
                                                   C.srow.ap()[0:1, 512:1024])
                        else:
                            ins = None
                        em.sig(ep1, ins)
                        if em.on("tensor"):
                            em.wait_cur(ep1)
                            em.wait_cur(amemv)
                            em.e.matmul(C.pD.ap()[0:1, 0:512], C.mem_bf.ap()[:, 128:129],
                                        C.qf.ap()[:, 512 * j:512 * (j + 1)],
                                        start=True, stop=True)
                            ins = em.e.matmul(C.pB.ap()[:, 0:512], C.mem_bf.ap()[:, 0:128],
                                              C.qf.ap()[:, 512 * j:512 * (j + 1)],
                                              start=True, stop=True)
                        else:
                            ins = None
                        em.sig(atd, ins)
                        if em.on("vector"):
                            em.wait_cur(atd)
                            ins = em.e.reciprocal(C.srow.ap()[0:1, 0:512], C.pD.ap()[0:1, 0:512])
                        else:
                            ins = None
                        fence(em, "vector", "sf_v", ins)
                        if em.on("vector"):
                            ins = em.e.tensor_scalar(C.srow.ap()[0:1, 512:1024],
                                                     C.srow.ap()[0:1, 0:512],
                                                     C.grow.ap()[0:1, PERM_HEADS[hh]:PERM_HEADS[hh] + 1],
                                                     None, OP.mult)
                        else:
                            ins = None
                        fence(em, "vector", "sf_v", ins)
                        if em.on("vector"):
                            ins = em.e.tensor_copy(C.brow.ap()[0:1, 512:1024],
                                                   C.srow.ap()[0:1, 512:1024])
                        else:
                            ins = None
                        em.sig(ep2, ins)
                        if em.on("tensor"):
                            em.wait_cur(ep2)
                            em.e.matmul(C.pA.ap()[:, 0:512], C.onescol_bf.ap(),
                                        C.brow.ap()[0:1, 0:512], start=True, stop=True)
                            ins = em.e.matmul(C.pA.ap()[:, 512:1024], C.onescol_bf.ap(),
                                              C.brow.ap()[0:1, 512:1024], start=True, stop=True)
                        else:
                            ins = None
                        em.sig(atb, ins)
                        if em.on("vector"):
                            em.wait_cur(atb)
                            em.wait_cur(aes)
                            em.e.tensor_copy(C.scrB.ap()[:, 0:512], C.pA.ap()[:, 0:512])
                            ins = em.e.tensor_copy(C.scrB.ap()[:, 512:1024],
                                                   C.pA.ap()[:, 512:1024])
                        else:
                            ins = None
                        fence(em, "vector", "sf_v", ins)
                        if em.on("vector"):
                            em.e.tensor_tensor(C.PT.ap()[:, 0:512], C.pC.ap(),
                                               C.scrB.ap()[:, 0:512], OP.mult)
                            ins = em.e.tensor_tensor(C.PT.ap()[:, 512:1024],
                                                     C.pB.ap()[:, 0:512],
                                                     C.scrB.ap()[:, 512:1024], OP.mult)
                        else:
                            ins = None
                        fence(em, "vector", "sf_v", ins)
                        if em.on("vector"):
                            em.wait_cur(acb[0])
                            ins = em.e.tensor_tensor(C.cbuf.ap(),
                                                     C.PT.ap()[:, 0:512],
                                                     C.PT.ap()[:, 512:1024], OP.add)
                        else:
                            ins = None
                        em.sig(ep3, ins)
                        if em.on("sync"):
                            em.wait_cur(ep3)
                            ins = em.e.dma_start(
                                out=C.comb_h[128 * hh:128 * (hh + 1), 512 * j:512 * (j + 1)],
                                in_=C.cbuf.ap())
                        else:
                            ins = None
                        em.sig(acb[0], ins)
                        hj_done += 1
                    hrec.append({"ats": ED["a_ts"].count, "ast": ED["a_st"].count,
                                 "atv": ED["a_tv"].count, "ans": ED["a_nrm_s"].count,
                                 "aeg": ED["a_elu_g"].count, "aev": ED["a_elu_v"].count})
            barrier(em, C)

        run_phase(block, em, C, ph_attn)
        sc9.close()

        sc10 = contextlib.ExitStack()
        C.wobf = sc10.enter_context(nc.sbuf_tensor("wobf", [128, 16 * H], BF16))
        C.combL = sc10.enter_context(nc.sbuf_tensor("combL", [128, 16 * TOK], BF16))
        C.b2B = sc10.enter_context(nc.sbuf_tensor("b2B", [128, H], BF16))
        C.rowb2 = sc10.enter_context(nc.sbuf_tensor("rowb2", [1, H], F32))
        C.xbo = [sc10.enter_context(nc.sbuf_tensor(f"xbo{i}", [128, H], F32)) for i in range(2)]
        C.aco = [sc10.enter_context(nc.sbuf_tensor(f"aco{i}", [128, H], F32)) for i in range(2)]
        C.pWo = [sc10.enter_context(nc.psum_tensor(f"pWo{i}", [128, 512], F32)) for i in range(2)]

        def ph_wo(em, C):
            ld = ED["wo_ld"]
            ldx = [ED["wo_ldx0"], ED["wo_ldx1"]]
            st = [ED["wo_st0"], ED["wo_st1"]]
            mm, ev = ED["wo_mm"], ED["wo_ev"]
            b_ld, b_mm, b_ev = ld.count, mm.count, ev.count
            ins = (em.e.dma_start(out=C.wobf.ap().rearrange("p (c o) -> p c o", c=16),
                                  in_=C.woT.rearrange("(c p) o -> p c o", p=128))
                   if em.on("gpsimd") else None)
            em.sig(ED["gled"], ins)
            ins = (em.e.dma_start(out=C.combL.ap().rearrange("p (c t) -> p c t", c=16),
                                  in_=C.comb_h.rearrange("(c p) t -> p c t", p=128))
                   if em.on("sync") else None)
            em.sig(ld, ins)
            ins = (em.e.dma_start(out=C.rowb2.ap(), in_=C.b2r) if em.on("sync") else None)
            em.sig(ld, ins)
            for sbc in range(4):
                if em.on("tensor"):
                    em.wait_n(ld, b_ld + 2)
                    if sbc >= 2:
                        em.wait_n(ev, b_ev + sbc - 1)
                    ins = em.e.matmul(C.pWo[sbc % 2].ap(), C.onescol_f.ap(),
                                      C.rowb2[0:1, 512 * sbc:512 * (sbc + 1)],
                                      start=True, stop=True)
                else:
                    ins = None
                em.sig(mm, ins)
                if em.on("vector"):
                    em.wait_n(mm, b_mm + sbc + 1)
                    ins = em.e.tensor_copy(C.b2B.ap()[:, 512 * sbc:512 * (sbc + 1)],
                                           C.pWo[sbc % 2].ap())
                else:
                    ins = None
                em.sig(ev, ins)
            b_mm, b_ev = mm.count, ev.count
            for t in range(8):
                if em.on("sync"):
                    if t >= 2:
                        em.wait_n(ev, b_ev + 4 * (t - 1))
                    ins = em.e.dma_start(out=C.xbo[t % 2].ap(),
                                         in_=C.xo[128 * t:128 * (t + 1), :])
                else:
                    ins = None
                em.sig(ldx[t % 2], ins)
                for jo in range(4):
                    idx = 4 * t + jo
                    if em.on("tensor"):
                        em.wait_n(ld, b_ld + 2)
                        em.wait_cur(ED["gled"])
                        em.wait_n(ev, b_ev + idx - 1)
                        ins = None
                        for cc in range(16):
                            ins = em.e.matmul(
                                C.pWo[idx % 2].ap(),
                                C.combL.ap()[:, 1024 * cc + 128 * t:1024 * cc + 128 * (t + 1)],
                                C.wobf.ap()[:, 2048 * cc + 512 * jo:2048 * cc + 512 * (jo + 1)],
                                start=(cc == 0), stop=(cc == 15))
                    else:
                        ins = None
                    em.sig(mm, ins)
                    if em.on("vector"):
                        em.wait_n(mm, b_mm + idx + 1)
                        em.wait_cur(ldx[t % 2])
                        em.wait_cur(st[t % 2])
                        if idx > 0:
                            em.wait_n(ev, b_ev + idx)
                        ins = em.e.tensor_tensor(C.scrW.ap(), C.pWo[idx % 2].ap(),
                                                 C.xbo[t % 2].ap()[:, 512 * jo:512 * (jo + 1)],
                                                 OP.add)
                    else:
                        ins = None
                    fence(em, "vector", "sf_v", ins)
                    if em.on("vector"):
                        ins = em.e.tensor_tensor(
                            C.aco[t % 2].ap()[:, 512 * jo:512 * (jo + 1)],
                            C.scrW.ap(), C.b2B.ap()[:, 512 * jo:512 * (jo + 1)], OP.add)
                    else:
                        ins = None
                    em.sig(ev, ins)
                if em.on("sync"):
                    em.wait_n(ev, b_ev + 4 * t + 4)
                    ins = em.e.dma_start(out=C.acc_h[128 * t:128 * (t + 1), :],
                                         in_=C.aco[t % 2].ap())
                else:
                    ins = None
                em.sig(st[t % 2], ins)
            barrier(em, C)

        C.scrW = sc10.enter_context(nc.sbuf_tensor("scrW", [128, 512], F32))
        run_phase(block, em, C, ph_wo)
        sc10.close()
        sc_qfm.close()

        sc_gs = contextlib.ExitStack()
        C.gsum = sc_gs.enter_context(nc.sbuf_tensor("gsum", [128, 128], F32))
        C.g2sum = sc_gs.enter_context(nc.sbuf_tensor("g2sum", [128, 128], F32))
        sc11 = contextlib.ExitStack()
        C.xq1s = sc11.enter_context(nc.sbuf_tensor("xq1s", [128, 16 * TOK], BF16))
        C.q1bf = [sc11.enter_context(nc.sbuf_tensor(f"q1bf{i}", [128, 16 * 512], BF16)) for i in range(2)]
        C.b1B = sc11.enter_context(nc.sbuf_tensor("b1B", [128, INNER], BF16))
        C.rowI = sc11.enter_context(nc.sbuf_tensor("rowI", [1, INNER], F32))
        C.fsc = [sc11.enter_context(nc.sbuf_tensor(f"fsc{i}", [128, 512], F32)) for i in range(2)]
        C.fsc2 = [sc11.enter_context(nc.sbuf_tensor(f"fsc2_{i}", [128, 512], F32)) for i in range(2)]
        C.gout = [sc11.enter_context(nc.sbuf_tensor(f"gout{i}", [128, 512], BF16)) for i in range(2)]
        C.pm1 = [sc11.enter_context(nc.psum_tensor(f"pm1_{i}", [128, 512], F32)) for i in range(2)]

        def ph_f1(em, C):
            ld, mm, ev = ED["f1_ld"], ED["f1_mm"], ED["f1_ev"]
            ldq = [ED["f1_ldq0"], ED["f1_ldq1"]]
            st = [ED["f1_st0"], ED["f1_st1"]]
            s1, s2, g2, fin = ED["f1_s1"], ED["f1_s2"], ED["f1_g2"], ED["f1_fin"]
            b_ld, b_mm, b_ev = ld.count, mm.count, ev.count
            b_s1, b_s2, b_g2 = s1.count, s2.count, g2.count
            if em.on("vector"):
                ins = em.e.tensor_scalar(C.stats2.ap()[:, 24:32], C.stats.ap()[:, 32:40],
                                         C.wsB.ap()[:, 0:1], None, OP.mult)
            else:
                ins = None
            fence(em, "vector", "sf_v", ins)
            if em.on("vector"):
                ins = em.e.reciprocal(C.stats.ap()[:, 40:48], C.stats2.ap()[:, 24:32])
            else:
                ins = None
            em.sig(ev, ins)
            ins = (em.e.dma_start(out=C.xq1s.ap().rearrange("p (k t) -> p k t", k=16),
                                  in_=C.xq1_h.rearrange("(k p) t -> p k t", p=128))
                   if em.on("sync") else None)
            em.sig(ld, ins)
            ins = (em.e.dma_start(out=C.rowI.ap(), in_=C.b1r) if em.on("sync") else None)
            em.sig(ld, ins)
            for sbc in range(16):
                if em.on("tensor"):
                    em.wait_n(ld, b_ld + 2)
                    if sbc >= 2:
                        em.wait_n(ev, b_ev + 1 + sbc - 1)
                    ins = em.e.matmul(C.pm1[sbc % 2].ap(), C.onescol_f.ap(),
                                      C.rowI[0:1, 512 * sbc:512 * (sbc + 1)],
                                      start=True, stop=True)
                else:
                    ins = None
                em.sig(mm, ins)
                if em.on("vector"):
                    em.wait_n(mm, b_mm + sbc + 1)
                    ins = em.e.tensor_copy(C.b1B.ap()[:, 512 * sbc:512 * (sbc + 1)],
                                           C.pm1[sbc % 2].ap())
                else:
                    ins = None
                em.sig(ev, ins)
            b_mm, b_ev = mm.count, ev.count
            for jb in range(16):
                if em.on("gpsimd"):
                    if jb >= 2:
                        em.wait_n(mm, b_mm + 8 * (jb - 1))
                    ins = em.e.dma_start(
                        out=C.q1bf[jb % 2].ap().rearrange("p (k o) -> p k o", k=16),
                        in_=C.q1_h[:, 512 * jb:512 * (jb + 1)].rearrange(
                            "(k p) o -> p k o", p=128))
                else:
                    ins = None
                em.sig(ldq[jb % 2], ins)
                for t in range(8):
                    idx = 8 * jb + t
                    if em.on("tensor"):
                        em.wait_n(ld, b_ld + 2)
                        em.wait_cur(ldq[jb % 2])
                        em.wait_n(ev, b_ev + idx - 1)
                        ins = None
                        for kk in range(16):
                            ins = em.e.matmul(
                                C.pm1[idx % 2].ap(),
                                C.xq1s.ap()[:, 1024 * kk + 128 * t:1024 * kk + 128 * (t + 1)],
                                C.q1bf[jb % 2].ap()[:, 512 * kk:512 * (kk + 1)],
                                start=(kk == 0), stop=(kk == 15))
                    else:
                        ins = None
                    em.sig(mm, ins)
                    if em.on("vector"):
                        em.wait_n(mm, b_mm + idx + 1)
                        if idx >= 2:
                            em.wait_n(s1, b_s1 + idx - 1)
                        ins = em.e.scalar_tensor_tensor(
                            C.fsc[idx % 2].ap(), C.pm1[idx % 2].ap(),
                            C.stats.ap()[:, 40 + t:41 + t],
                            C.b1B.ap()[:, 512 * jb:512 * (jb + 1)], OP.mult, OP.add)
                    else:
                        ins = None
                    em.sig(ev, ins)
                    if em.on("scalar"):
                        em.wait_n(ev, b_ev + idx + 1)
                        if idx >= 2:
                            em.wait_n(s2, b_s2 + idx - 1)
                        ins = em.e.activation(C.fsc2[idx % 2].ap(), C.fsc[idx % 2].ap(),
                                              AF.Silu)
                    else:
                        ins = None
                    em.sig(s1, ins)
                    if em.on("scalar"):
                        em.wait_cur(st[idx % 2])
                        em.wait_n(s1, b_s1 + idx + 1)
                        if idx >= 2:
                            em.wait_n(g2, b_g2 + idx - 1)
                        ins = em.e.activation(C.gout[idx % 2].ap(), C.fsc2[idx % 2].ap(),
                                              AF.Gelu,
                                              accum_out=C.gsum.ap()[:, 16 * t + jb:16 * t + jb + 1])
                    else:
                        ins = None
                    em.sig(s2, ins)
                    if em.on("vector"):
                        em.wait_n(s2, b_s2 + idx + 1)
                        em.wait_n(s1, b_s1 + idx + 1)
                        ins = em.e.scalar_tensor_tensor(
                            C.fsc[idx % 2].ap(), C.gout[idx % 2].ap(), 0.0,
                            C.gout[idx % 2].ap(), OP.bypass, OP.mult,
                            accum_out=C.g2sum.ap()[:, 16 * t + jb:16 * t + jb + 1])
                    else:
                        ins = None
                    em.sig(g2, ins)
                    if em.on("sync"):
                        em.wait_n(g2, b_g2 + idx + 1)
                        ins = em.e.dma_start(
                            out=C.g_h[128 * t:128 * (t + 1), 512 * jb:512 * (jb + 1)],
                            in_=C.gout[idx % 2].ap())
                    else:
                        ins = None
                    em.sig(st[idx % 2], ins)
            b_fin = fin.count
            for t in range(8):
                if em.on("vector"):
                    if t == 0:
                        em.wait_n(s2, b_s2 + 128)
                        em.wait_n(g2, b_g2 + 128)
                    ins = em.e.tensor_reduce(C.nrm.ap()[:, 0:1],
                                             C.gsum.ap()[:, 16 * t:16 * (t + 1)], AX.X, OP.add)
                else:
                    ins = None
                fence(em, "vector", "sf_v", ins)
                if em.on("vector"):
                    ins = em.e.tensor_scalar(C.stats.ap()[:, 48 + t:49 + t], C.nrm.ap()[:, 0:1],
                                             1.0 / float(INNER), None, OP.mult)
                else:
                    ins = None
                fence(em, "vector", "sf_v", ins)
                if em.on("vector"):
                    ins = em.e.tensor_reduce(C.nrm.ap()[:, 1:2],
                                             C.g2sum.ap()[:, 16 * t:16 * (t + 1)], AX.X, OP.add)
                else:
                    ins = None
                fence(em, "vector", "sf_v", ins)
                if em.on("vector"):
                    ins = em.e.tensor_scalar(C.nrm.ap()[:, 2:3], C.nrm.ap()[:, 1:2],
                                             1.0 / float(INNER), None, OP.mult)
                else:
                    ins = None
                fence(em, "vector", "sf_v", ins)
                if em.on("vector"):
                    ins = em.e.tensor_tensor(C.nrm.ap()[:, 1:2], C.stats.ap()[:, 48 + t:49 + t],
                                             C.stats.ap()[:, 48 + t:49 + t], OP.mult)
                else:
                    ins = None
                fence(em, "vector", "sf_v", ins)
                if em.on("vector"):
                    ins = em.e.tensor_tensor(C.nrm.ap()[:, 3:4], C.nrm.ap()[:, 2:3],
                                             C.nrm.ap()[:, 1:2], OP.subtract)
                else:
                    ins = None
                fence(em, "vector", "sf_v", ins)
                if em.on("vector"):
                    ins = em.e.tensor_scalar(C.nrm.ap()[:, 0:1], C.nrm.ap()[:, 3:4],
                                             EPS, None, OP.add)
                else:
                    ins = None
                em.sig(fin, ins)
                if em.on("scalar"):
                    em.wait_cur(fin)
                    ins = em.e.activation(C.nrm.ap()[:, 1:2], C.nrm.ap()[:, 0:1], AF.Sqrt)
                else:
                    ins = None
                em.sig(fin, ins)
                if em.on("vector"):
                    em.wait_cur(fin)
                    ins = em.e.reciprocal(C.stats.ap()[:, 56 + t:57 + t], C.nrm.ap()[:, 1:2])
                else:
                    ins = None
                em.sig(fin, ins)
            barrier(em, C)

        run_phase(block, em, C, ph_f1)
        sc11.close()

        sc12a = contextlib.ExitStack()
        C.gamB = sc12a.enter_context(nc.sbuf_tensor("gamB", [128, INNER], BF16))
        C.betB = sc12a.enter_context(nc.sbuf_tensor("betB", [128, INNER], BF16))
        sc12b = contextlib.ExitStack()
        C.rowI2 = sc12b.enter_context(nc.sbuf_tensor("rowI2", [1, INNER], F32))
        C.pf2 = [sc12b.enter_context(nc.psum_tensor(f"pf2_{i}", [128, 512], F32)) for i in range(2)]

        def ph_f2pre(em, C):
            ld, et, ev = ED["f2_ld"], ED["f2_t"], ED["f2_v"]
            b_ld, b_t, b_v = ld.count, et.count, ev.count
            for w in range(2):
                src = C.b1r if False else (C.lngr if w == 0 else C.lnbr)
                dst = C.gamB if w == 0 else C.betB
                if em.on("sync"):
                    if w == 1:
                        em.wait_n(et, b_t + 16)
                    ins = em.e.dma_start(out=C.rowI2.ap(), in_=src)
                else:
                    ins = None
                em.sig(ld, ins)
                for sbc in range(16):
                    it = 16 * w + sbc
                    if em.on("tensor"):
                        em.wait_n(ld, b_ld + w + 1)
                        if it >= 2:
                            em.wait_n(ev, b_v + it - 1)
                        ins = em.e.matmul(C.pf2[it % 2].ap(), C.onescol_f.ap(),
                                          C.rowI2[0:1, 512 * sbc:512 * (sbc + 1)],
                                          start=True, stop=True)
                    else:
                        ins = None
                    em.sig(et, ins)
                    if em.on("vector"):
                        em.wait_n(et, b_t + it + 1)
                        ins = em.e.tensor_copy(dst.ap()[:, 512 * sbc:512 * (sbc + 1)],
                                               C.pf2[it % 2].ap())
                    else:
                        ins = None
                    em.sig(ev, ins)
            barrier(em, C)

        run_phase(block, em, C, ph_f2pre)
        sc12b.close()

        sc12c = contextlib.ExitStack()
        C.bbA = sc12c.enter_context(nc.sbuf_tensor("bbA", [128, INNER], BF16))
        C.bbB = sc12c.enter_context(nc.sbuf_tensor("bbB", [128, INNER], BF16))
        C.i8z = sc12c.enter_context(nc.sbuf_tensor("i8z", [128, INNER // 8], I8))
        C.zf = sc12c.enter_context(nc.sbuf_tensor("zf", [128, INNER // 8], F32))
        C.d8z = sc12c.enter_context(nc.sbuf_tensor("d8z", [128, INNER // 8], I8))
        C.xq2f = sc12c.enter_context(nc.sbuf_tensor("xq2f", [128, 64 * 256], BF16))
        C.q2bf = sc12c.enter_context(nc.sbuf_tensor("q2bf", [128, 2 * 64 * 256], BF16))
        C.outb = [sc12c.enter_context(nc.sbuf_tensor(f"outb{i}", [128, H], F32)) for i in range(2)]
        C.accb = [sc12c.enter_context(nc.sbuf_tensor(f"accb{i}", [128, H], F32)) for i in range(2)]
        C.pm2 = [sc12c.enter_context(nc.psum_tensor(f"pm2_{i}", [128, 512], F32)) for i in range(2)]
        C.pTq = sc12c.enter_context(nc.psum_tensor("pTq", [128, 512], BF16))

        def ph_f23(em, C):
            ld, fv, fs, ffin, ftt = ED["f2_ld"], ED["f2_v"], ED["f2_s"], ED["f2_fin"], ED["f2_t"]
            f3la = ED["f3_lda"]
            f3lq = [ED["f3_ldq0"], ED["f3_ldq1"]]
            f3mm, f3ev, f3t, f3tv = (ED["f3_mm"], ED["f3_ev"], ED["f3_t"], ED["f3_tv"])
            sto = ED["st_out"]
            b_fs, b_fv = fs.count, fv.count
            for qt in range(4):
                for tt in range(2):
                    t = 2 * qt + tt
                    if em.on("sync"):
                        if qt >= 1:
                            em.wait_n(f3ev, 16 * qt)
                        ins = em.e.dma_start(out=C.accb[tt].ap(),
                                             in_=C.acc_h[128 * t:128 * (t + 1), :])
                    else:
                        ins = None
                    em.sig(f3la, ins)
                for tt in range(2):
                    t = 2 * qt + tt
                    if em.on("sync"):
                        em.wait_n(fs, fs.count)
                        ins = em.e.dma_start(out=C.bbA.ap(),
                                             in_=C.g_h[128 * t:128 * (t + 1), :])
                    else:
                        ins = None
                    em.sig(ld, ins)
                    if em.on("vector"):
                        em.wait_cur(ld)
                        em.wait_cur(f3t)
                        ins = em.e.scalar_tensor_tensor(C.bbB.ap(), C.bbA.ap(),
                                                        C.stats.ap()[:, 48 + t:49 + t],
                                                        C.gamB.ap(), OP.subtract, OP.mult)
                    else:
                        ins = None
                    fence(em, "vector", "sf_v", ins)
                    if em.on("vector"):
                        ins = em.e.scalar_tensor_tensor(C.bbA.ap(), C.bbB.ap(),
                                                        C.stats.ap()[:, 56 + t:57 + t],
                                                        C.betB.ap(), OP.mult, OP.add)
                    else:
                        ins = None
                    fence(em, "vector", "sf_v", ins)
                    if em.on("vector"):
                        ins = em.e.scalar_tensor_tensor(C.bbB.ap(), C.bbA.ap(), 0.0,
                                                        C.bbA.ap(), OP.bypass, OP.mult,
                                                        accum_out=C.stats2.ap()[:, t:t + 1])
                    else:
                        ins = None
                    fence(em, "vector", "sf_v", ins)
                    if em.on("vector"):
                        ins = em.e.tensor_reduce(C.stats2.ap()[:, 24 + t:25 + t],
                                                 C.bbA.ap(), AX.X, OP.max,
                                                 apply_absolute_value=True)
                    else:
                        ins = None
                    fence(em, "vector", "sf_v", ins)
                    if em.on("vector"):
                        ins = em.e.tensor_scalar(C.nrm.ap()[:, 0:1], C.stats2.ap()[:, t:t + 1],
                                                 1.0 / float(INNER), EPS, OP.mult, OP.add)
                    else:
                        ins = None
                    em.sig(ffin, ins)
                    if em.on("scalar"):
                        em.wait_cur(ffin)
                        ins = em.e.activation(C.nrm.ap()[:, 1:2], C.nrm.ap()[:, 0:1], AF.Sqrt)
                    else:
                        ins = None
                    em.sig(ffin, ins)
                    if em.on("vector"):
                        em.wait_cur(ffin)
                        ins = em.e.reciprocal(C.nrm.ap()[:, 2:3], C.nrm.ap()[:, 1:2])
                    else:
                        ins = None
                    fence(em, "vector", "sf_v", ins)
                    if em.on("vector"):
                        ins = em.e.tensor_tensor(C.nrm.ap()[:, 3:4], C.nrm.ap()[:, 2:3],
                                                 C.stats2.ap()[:, 24 + t:25 + t], OP.mult)
                    else:
                        ins = None
                    fence(em, "vector", "sf_v", ins)
                    if em.on("vector"):
                        ins = em.e.tensor_scalar(C.nrm.ap()[:, 0:1], C.nrm.ap()[:, 3:4],
                                                 1e-5, None, OP.max)
                    else:
                        ins = None
                    fence(em, "vector", "sf_v", ins)
                    if em.on("vector"):
                        ins = em.e.reciprocal(C.nrm.ap()[:, 1:2], C.nrm.ap()[:, 0:1])
                    else:
                        ins = None
                    fence(em, "vector", "sf_v", ins)
                    if em.on("vector"):
                        ins = em.e.tensor_scalar(C.nrm.ap()[:, 3:4], C.nrm.ap()[:, 1:2],
                                                 127.0, None, OP.mult)
                    else:
                        ins = None
                    fence(em, "vector", "sf_v", ins)
                    if em.on("vector"):
                        em.e.tensor_tensor(C.stats2.ap()[:, 8 + t:9 + t], C.nrm.ap()[:, 2:3],
                                           C.nrm.ap()[:, 3:4], OP.mult)
                        ins = em.e.tensor_scalar(C.nrm.ap()[:, 0:1], C.nrm.ap()[:, 3:4],
                                                 C.wsB.ap()[:, 1:2], None, OP.mult)
                    else:
                        ins = None
                    fence(em, "vector", "sf_v", ins)
                    if em.on("vector"):
                        ins = em.e.reciprocal(C.stats2.ap()[:, 16 + t:17 + t],
                                              C.nrm.ap()[:, 0:1])
                    else:
                        ins = None
                    em.sig(ffin, ins)
                    for hh in range(8):
                        u = 8 * t + hh
                        hsz = INNER // 8
                        hsl = slice(hsz * hh, hsz * (hh + 1))
                        if em.on("scalar"):
                            em.wait_cur(ffin)
                            if u >= 1:
                                em.wait_n(fv, b_fv + 4 * (u - 1) + 3)
                            ins = em.e.activation(C.zf.ap(), C.bbA.ap()[:, hsl],
                                                  AF.Identity,
                                                  scale=C.stats2.ap()[:, 8 + t:9 + t])
                        else:
                            ins = None
                        em.sig(fs, ins)
                        if em.on("vector"):
                            em.wait_n(fs, b_fs + 2 * u + 1)
                            if u >= 1:
                                em.wait_n(fv, b_fv + 4 * (u - 1) + 3)
                            ins = em.e.tensor_scalar(C.zf.ap(), C.zf.ap(),
                                                     12582912.0, -12582912.0,
                                                     OP.add, OP.add)
                        else:
                            ins = None
                        em.sig(fv, ins)
                        if em.on("scalar"):
                            em.wait_n(fs, b_fs + 2 * u + 1)
                            em.wait_n(fv, b_fv + 4 * u + 1)
                            ins = em.e.copy(C.i8z.ap(), C.zf.ap())
                        else:
                            ins = None
                        em.sig(fs, ins)
                        if em.on("vector"):
                            em.wait_n(fs, b_fs + 2 * u + 2)
                            em.wait_n(fv, b_fv + 4 * u + 1)
                            ins = em.e.tensor_tensor(C.zf.ap(), C.zf.ap(), C.i8z.ap(),
                                                     OP.subtract)
                        else:
                            ins = None
                        em.sig(fv, ins)
                        if em.on("vector"):
                            em.wait_n(fv, b_fv + 4 * u + 2)
                            if u >= 1:
                                em.wait_n(fv, b_fv + 4 * u)
                            ins = em.e.tensor_scalar(C.d8z.ap(), C.zf.ap(), 2.0, None,
                                                     OP.mult)
                        else:
                            ins = None
                        em.sig(fv, ins)
                        if em.on("vector"):
                            em.wait_n(fv, b_fv + 4 * u + 3)
                            ins = em.e.tensor_tensor(C.bbB.ap()[:, hsl], C.i8z.ap(),
                                                     C.d8z.ap(), OP.add)
                        else:
                            ins = None
                        em.sig(fv, ins)
                    b_t3, b_tv3 = f3t.count, f3tv.count
                    xq2v = C.xq2f.ap().rearrange("p (r c) -> p r c", c=256)
                    for bt in range(16):
                        if em.on("tensor"):
                            em.wait_cur(fv)
                            if bt >= 1:
                                em.wait_n(f3tv, b_tv3 + bt)
                            ins = None
                            for u in range(4):
                                rr = 4 * bt + u
                                ins = em.e.transpose(C.pTq.ap()[:, 128 * u:128 * (u + 1)],
                                                     C.bbB.ap()[:, 128 * rr:128 * (rr + 1)],
                                                     C.eyeS.ap())
                        else:
                            ins = None
                        em.sig(f3t, ins)
                        if em.on("vector"):
                            em.wait_n(f3t, b_t3 + bt + 1)
                            ins = em.e.tensor_copy(
                                xq2v[:, 4 * bt:4 * (bt + 1), 128 * tt:128 * (tt + 1)],
                                C.pTq.ap())
                        else:
                            ins = None
                        em.sig(f3tv, ins)
                b_mm3, b_ev3 = f3mm.count, f3ev.count
                for ob in range(8):
                    if em.on("gpsimd"):
                        em.wait_n(f3mm, b_mm3)
                        if ob >= 2:
                            em.wait_n(f3mm, b_mm3 + 2 * (ob - 1))
                        base_q = 16384 * (ob % 2)
                        for rh in range(2):
                            ins = em.e.dma_start(
                                out=C.q2bf.ap()[:, base_q + 8192 * rh:base_q + 8192 * (rh + 1)]
                                    .rearrange("p (r o) -> p r o", r=32),
                                in_=C.q2_h[4096 * rh:4096 * (rh + 1),
                                           256 * ob:256 * (ob + 1)].rearrange(
                                    "(r p) o -> p r o", p=128))
                            em.sig(f3lq[ob % 2], ins)
                    else:
                        em.sig(f3lq[ob % 2], None)
                        em.sig(f3lq[ob % 2], None)
                    for tt in range(2):
                        t = 2 * qt + tt
                        idx = 2 * ob + tt
                        if em.on("tensor"):
                            em.wait_cur(f3la)
                            em.wait_cur(f3lq[ob % 2])
                            em.wait_cur(f3tv)
                            if idx >= 2:
                                em.wait_n(f3ev, b_ev3 + idx - 1)
                            ins = None
                            for rr in range(64):
                                ins = em.e.matmul(
                                    C.pm2[idx % 2].ap()[:, 0:256],
                                    C.xq2f.ap()[:, 256 * rr + 128 * tt:256 * rr + 128 * (tt + 1)],
                                    C.q2bf.ap()[:, 16384 * (ob % 2) + 256 * rr:
                                                 16384 * (ob % 2) + 256 * (rr + 1)],
                                    start=(rr == 0), stop=(rr == 63))
                        else:
                            ins = None
                        em.sig(f3mm, ins)
                        if em.on("vector"):
                            em.wait_n(f3mm, b_mm3 + idx + 1)
                            if qt >= 1 and idx < 2:
                                em.wait_n(sto, 2 * qt)
                            ins = em.e.scalar_tensor_tensor(
                                C.outb[tt].ap()[:, 256 * ob:256 * (ob + 1)],
                                C.pm2[idx % 2].ap()[:, 0:256], C.stats2.ap()[:, 16 + t:17 + t],
                                C.accb[tt].ap()[:, 256 * ob:256 * (ob + 1)],
                                OP.mult, OP.add)
                        else:
                            ins = None
                        em.sig(f3ev, ins)
                for tt in range(2):
                    t = 2 * qt + tt
                    if em.on("sync"):
                        em.wait_n(f3ev, b_ev3 + 16)
                        ins = em.e.dma_start(out=C.out[128 * t:128 * (t + 1), :],
                                             in_=C.outb[tt].ap())
                    else:
                        ins = None
                    em.sig(sto, ins)
            if em.on("sync"):
                em.wait_n(sto, sto.count)
            barrier(em, C)

        run_phase(block, em, C, ph_f23)
        sc12c.close()
        sc12a.close()
        sc_gs.close()

    return



_CACHE = {}


def _build():
    if "nc" not in _CACHE:
        nc, C, em = build_nc()
        emit_all(nc, C, em)
        _CACHE["nc"] = nc
    return _CACHE["nc"]


def make_in_maps(inputs):
    x = np.ascontiguousarray(np.asarray(inputs["L0_attn_hidden_states"]))
    Wq = np.asarray(inputs["L0_attn_Wq"])
    Wk = np.asarray(inputs["L0_attn_Wk"])
    Wv = np.asarray(inputs["L0_attn_Wv"])
    Wo = np.asarray(inputs["L0_attn_Wo"])
    gate = np.ascontiguousarray(np.asarray(inputs["L0_attn_gate"]).reshape(1, NH))
    w1 = np.asarray(inputs["L0_ffn_w1"])
    b1 = np.ascontiguousarray(np.asarray(inputs["L0_ffn_b1"]).reshape(1, INNER))
    lng = np.ascontiguousarray(np.asarray(inputs["L0_ffn_ln_g"]).reshape(1, INNER))
    lnb = np.ascontiguousarray(np.asarray(inputs["L0_ffn_ln_b"]).reshape(1, INNER))
    w2 = np.asarray(inputs["L0_ffn_w2"])
    b2 = np.ascontiguousarray(np.asarray(inputs["L0_ffn_b2"]).reshape(1, H))

    wqT = np.ascontiguousarray(Wq.T[:, CHAN_PERM])
    wkT = np.ascontiguousarray(Wk.T[:, CHAN_PERM])
    wvT = np.ascontiguousarray(Wv.T[:, CHAN_PERM])
    woT = np.ascontiguousarray(Wo.T[CHAN_PERM, :])
    w1T = np.ascontiguousarray(w1.T)
    w2T = np.ascontiguousarray(w2.T)
    tabc, tabs = _host_tables()

    in_maps = []
    for rho in range(8):
        b, g = rho // 4, rho % 4
        xb = x[b]
        xbT = np.ascontiguousarray(xb.T)
        sl = slice(TOK * g, TOK * (g + 1))
        in_maps.append({
            "xT": xbT,
            "xTo": np.ascontiguousarray(xbT[:, sl]),
            "xo": np.ascontiguousarray(xb[sl]),
            "wqT": wqT, "wkT": wkT, "wvT": wvT, "woT": woT,
            "w1T": w1T, "w2T": w2T,
            "b1r": b1, "lngr": lng, "lnbr": lnb, "b2r": b2,
            "gater": gate,
            "tabc": tabc, "tabs": tabs,
            "tabco": np.ascontiguousarray(tabc[:, sl]),
            "tabso": np.ascontiguousarray(tabs[:, sl]),
        })

    return in_maps


def kernel(**inputs):
    in_maps = make_in_maps(inputs)
    nc = _build()
    res = run_bass_kernel_spmd(nc, in_maps, core_ids=list(range(8)))
    out = np.empty((2, S, H), np.float32)
    for rho in range(8):
        b, g = rho // 4, rho % 4
        out[b, TOK * g:TOK * (g + 1), :] = res.results[rho]["out"]
    return out


# version 10 (final)
# speedup vs baseline: 1.0274x; 1.0274x over previous
import math
import numpy as np
import ml_dtypes

import concourse.bass as bass
import concourse.mybir as mybir
from concourse.bass_utils import run_bass_kernel_spmd

F32 = mybir.dt.float32
BF16 = mybir.dt.bfloat16
I8 = mybir.dt.int8
I16 = mybir.dt.int16
AF = mybir.ActivationFunctionType
OP = mybir.AluOpType
AX = mybir.AxisListType
BF = ml_dtypes.bfloat16

H = 2048
S = 4096
TOK = 1024
NH = 16
HD = 128
INNER = 8192
EPS = 1e-5
TAU = 1.0 / math.sqrt(float(HD))
EXPC = 11.0

PERM_HEADS = []
for _g in range(4):
    for _p in (2 * _g, 2 * _g + 1):
        PERM_HEADS += [_p, _p + 8]
CHAN_PERM = np.concatenate([np.arange(128 * h, 128 * h + 128) for h in PERM_HEADS])



class Edge:
    __slots__ = ("sem", "inc", "count")

    def __init__(self, sem, inc):
        self.sem = sem
        self.inc = inc
        self.count = 0


class EM:

    def __init__(self, nc):
        self.nc = nc
        self.cur = None
        self.e = None
        self.edges = {}

    def edge(self, name, inc=1):
        ed = Edge(self.nc.alloc_semaphore(name), inc)
        self.edges[name] = ed
        return ed

    def on(self, eng):
        return self.cur == eng

    def sig(self, ed, ins=None, n=1):
        ed.count += n
        if ins is not None:
            ins.then_inc(ed.sem, n * ed.inc)
        return ed.count

    def wait_cur(self, ed, lag=0):
        n = ed.count - lag
        if n > 0 and self.e is not None:
            self.e.wait_ge(ed.sem, n * ed.inc)

    def wait_n(self, ed, n):
        if n > 0 and self.e is not None:
            self.e.wait_ge(ed.sem, n * ed.inc)

    def snap(self):
        return {k: v.count for k, v in self.edges.items()}

    def restore(self, s):
        for k, v in self.edges.items():
            v.count = s.get(k, 0)


ENGS = ("sync", "tensor", "vector", "scalar", "gpsimd")


PHASE_LIMIT = None
ATTN_LIMIT = None
ATTN_HEADS = None


def run_phase(block, em, C, fn):
    if PHASE_LIMIT is not None:
        n = getattr(run_phase, "_n", 0)
        run_phase._n = n + 1
        if n >= PHASE_LIMIT:
            return
    snap = em.snap()
    for i, eng in enumerate(ENGS):
        if i > 0:
            em.restore(snap)
        em.cur = eng

        def body(e, _fn=fn):
            em.e = e
            _fn(em, C)

        getattr(block, eng)(body)
    em.cur = None
    em.e = None


class Ctx:
    pass


def barrier(em, C):
    ed = em.edges["ph"]
    ins = None
    if em.e is not None:
        if em.cur == "vector":
            ins = em.e.memset(C.dumb.ap()[0:1, 0:1], 0.0)
        elif em.cur == "scalar":
            ins = em.e.memzero(C.dumb.ap()[0:1, 1:2])
        elif em.cur == "gpsimd":
            ins = em.e.memset(C.dumb.ap()[0:1, 2:3], 0.0)
        elif em.cur == "tensor":
            ins = em.e.ldweights(C.dumbW.ap())
        else:
            for nm in ("stA0", "stA1", "stB0", "stB1", "st_out"):
                em.wait_cur(em.edges[nm])
            ins = em.e.nop()
    em.sig(ed, ins)
    em.wait_n(ed, 5 * ed.count)


def _host_tables():
    inv_freq = 1.0 / 10000.0 ** (np.arange(0, H, 2, dtype=np.float64) / H)
    t = np.arange(S, dtype=np.float64)
    ph = np.outer(inv_freq, t)
    return np.cos(ph).astype(np.float32), np.sin(ph).astype(np.float32)




def build_nc():
    nc = bass.Bass("TRN2", target_bir_lowering=False)
    C = Ctx()
    C.nc = nc

    def din(name, shape, dt=F32):
        return nc.dram_tensor(name, list(shape), dt, kind="ExternalInput").ap()

    C.xT = din("xT", (H, S))
    C.xTo = din("xTo", (H, TOK))
    C.wqT = din("wqT", (H, H))
    C.wkT = din("wkT", (H, H))
    C.wvT = din("wvT", (H, H))
    C.woT = din("woT", (H, H))
    C.w1T = din("w1T", (H, INNER))
    C.w2T = din("w2T", (INNER, H))
    C.b1r = din("b1r", (1, INNER))
    C.lngr = din("lngr", (1, INNER))
    C.lnbr = din("lnbr", (1, INNER))
    C.b2r = din("b2r", (1, H))
    C.gater = din("gater", (1, NH))
    C.tabc = din("tabc", (1024, S))
    C.tabs = din("tabs", (1024, S))
    C.tabco = din("tabco", (1024, TOK))
    C.tabso = din("tabso", (1024, TOK))
    C.xo = din("xo", (TOK, H))
    C.out = nc.dram_tensor("out", [TOK, H], F32, kind="ExternalOutput").ap()

    C.xnfm_h = nc.dram_tensor("xnfm_h", [H, S], BF16, kind="Internal").ap()
    C.xnown_h = nc.dram_tensor("xnown_h", [H, TOK], BF16, kind="Internal").ap()
    C.xq1_h = nc.dram_tensor("xq1_h", [H, TOK], BF16, kind="Internal").ap()
    C.q1_h = nc.dram_tensor("q1_h", [H, INNER], I8, kind="Internal").ap()
    C.q2_h = nc.dram_tensor("q2_h", [INNER, H], I8, kind="Internal").ap()
    C.comb_h = nc.dram_tensor("comb_h", [H, TOK], BF16, kind="Internal").ap()
    C.acc_h = nc.dram_tensor("acc_h", [TOK, H], F32, kind="Internal").ap()
    C.g_h = nc.dram_tensor("g_h", [TOK, INNER], BF16, kind="Internal").ap()

    C.eye_bf_h = nc.inline_tensor(np.eye(128, dtype=BF), "eye_bf").ap()
    C.eye_f_h = nc.inline_tensor(np.eye(128, dtype=np.float32), "eye_f").ap()

    A = nc.alloc_sbuf_tensor
    C.stats = A("stats", [128, 64], F32)
    C.stats2 = A("stats2", [128, 32], F32)
    C.nrm = A("nrm", [128, 4], F32)
    C.negb = A("negb", [128, 2], F32)
    C.wsB = A("wsB", [128, 4], F32)
    C.srow = A("srow", [1, 1024], F32)
    C.brow = A("brow", [1, 1024], BF16)
    C.ones_bf = A("ones_bf", [128, 1], BF16)
    C.ones_f = A("ones_f", [128, 1], F32)
    C.onescol_bf = A("onescol_bf", [1, 128], BF16)
    C.onescol_f = A("onescol_f", [1, 128], F32)
    C.grow = A("grow", [1, NH], F32)
    C.g1m = A("g1m", [1, NH], F32)
    C.eyeS = A("eyeS", [128, 128], BF16)
    C.dumb = A("dumb", [1, 8], F32)
    C.dumbW = A("dumbW", [128, 128], BF16)
    C.bcol = A("bcol", [128, 4], BF16)

    em = EM(nc)
    C.em = em

    for name, inc in [
        ("ph", 1), ("pre_c", 1),
        ("led", 16), ("ldA0", 16), ("ldA1", 16),
        ("gled", 16), ("gledT", 16), ("gldA0", 16), ("gldA1", 16),
        ("stA0", 16), ("stA1", 16), ("stB0", 16), ("stB1", 16),
        ("st_out", 16),
        ("p0_cv", 1), ("p0_cw", 1), ("p0_mm", 1), ("p0_fin", 1), ("p0_bc_t", 1), ("p0_bc_v", 1),
        ("b_g", 1), ("b_v", 1),
        ("c_s", 1), ("c_v", 1), ("c_fin", 1), ("c_t", 1),
        ("d_g", 1), ("d_vv", 1), ("d_s", 1),
        ("w_s", 1), ("w_v", 1), ("w_fin", 1), ("w_t", 1),
        ("wq_s", 1), ("wq_g", 1), ("wq_v", 1),
        ("q_mm", 1), ("q_ev", 1),
        ("a_kv", 1), ("a_mm", 1), ("a_rope_g", 1), ("a_rope_v", 1),
        ("a_nrm_t", 1), ("a_nrm_v", 1), ("a_nrm_s", 1), ("a_nrm_g", 1),
        ("a_elu_g", 1), ("a_elu_s", 1), ("a_elu_v", 1),
        ("a_tp", 1), ("a_tpv", 1), ("a_mem", 1), ("a_memv", 1),
        ("a_ts", 1), ("a_st", 1), ("a_tv", 1),
        ("a_ep1", 1), ("a_td", 1), ("a_ep2", 1), ("a_tb", 1), ("a_ep3", 1),
        ("wo_mm", 1), ("wo_ev", 1),
        ("f1_mm", 1), ("f1_ev", 1), ("f1_s1", 1), ("f1_s2", 1), ("f1_g2", 1),
        ("f1_fin", 1),
        ("f2_v", 1), ("f2_s", 1), ("f2_fin", 1), ("f2_t", 1),
        ("f3_mm", 1), ("f3_ev", 1), ("f3_t", 1), ("f3_tv", 1),
    ]:
        em.edge(name, inc)
    for alias, base in [
        ("pre_g", "led"), ("p0_row", "led"), ("c_row", "led"), ("q_ld", "led"),
        ("a_ldw", "gled"), ("a_ldt", "gledT"), ("wo_ld", "led"), ("f1_ld", "led"),
        ("f2_ld", "led"), ("f3_lda", "led"),
        ("p0_ld0", "ldA0"), ("b_ld0", "ldA0"), ("c_ld0", "ldA0"), ("d_ld0", "ldA0"),
        ("w_ld0", "ldA0"), ("wq_ld0", "ldA0"), ("a_ldx0", "ldA0"), ("wo_ldx0", "ldA0"),
        ("f1_ldq0", "gldA0"), ("f3_ldq0", "gldA0"),
        ("p0_ld1", "ldA1"), ("b_ld1", "ldA1"), ("c_ld1", "ldA1"), ("d_ld1", "ldA1"),
        ("w_ld1", "ldA1"), ("wq_ld1", "ldA1"), ("a_ldx1", "ldA1"), ("wo_ldx1", "ldA1"),
        ("f1_ldq1", "gldA1"), ("f3_ldq1", "gldA1"),
        ("b_st0", "stA0"), ("d_sa0", "stA0"), ("wq_st0", "stA0"), ("wo_st0", "stA0"),
        ("f1_st0", "stA0"),
        ("b_st1", "stA1"), ("d_sa1", "stA1"), ("wq_st1", "stA1"), ("wo_st1", "stA1"),
        ("f1_st1", "stA1"),
        ("d_sb0", "stB0"), ("a_cb0", "stB0"),
        ("d_sb1", "stB1"), ("a_cb1", "stB1"),
    ]:
        em.edges[alias] = em.edges[base]

    return nc, C, em


def emit_all(nc, C, em):
    import contextlib
    ED = em.edges
    A = nc.alloc_sbuf_tensor

    for n in ("sf_v", "sf_s", "sf_g", "sf_t", "sf_y"):
        em.edge(n, 1)

    def fence(em, eng, ed_name, ins):
        ed = ED[ed_name]
        em.sig(ed, ins)
        if em.on(eng):
            em.wait_cur(ed)

    with contextlib.ExitStack() as OUTER, nc.Block() as block:

        sc_eye = contextlib.ExitStack()
        C.eyeSf = sc_eye.enter_context(nc.sbuf_tensor("eyeSf", [128, 128], F32))

        def ph_pre(em, C):
            pre_g, pre_c = ED["pre_g"], ED["pre_c"]
            if em.on("vector"):
                em.e.memset(C.ones_bf.ap(), 1.0)
                em.e.memset(C.ones_f.ap(), 1.0)
                em.e.memset(C.onescol_bf.ap(), 1.0)
                ins = em.e.memset(C.onescol_f.ap(), 1.0)
            else:
                ins = None
            em.sig(ED["sf_v"], ins)
            ins = em.e.dma_start(out=C.grow.ap(), in_=C.gater) if em.on("sync") else None
            em.sig(pre_g, ins)
            ins = em.e.dma_start(out=C.eyeS.ap(), in_=C.eye_bf_h) if em.on("sync") else None
            em.sig(pre_g, ins)
            ins = em.e.dma_start(out=C.eyeSf.ap(), in_=C.eye_f_h) if em.on("sync") else None
            em.sig(pre_g, ins)
            if em.on("scalar"):
                em.wait_cur(pre_g)
                ins = em.e.activation(C.srow.ap()[0:1, 0:NH], C.grow.ap(), AF.Sigmoid)
            else:
                ins = None
            em.sig(pre_c, ins)
            if em.on("vector"):
                em.wait_cur(pre_c)
                em.e.tensor_copy(C.grow.ap(), C.srow.ap()[0:1, 0:NH])
                ins = em.e.tensor_scalar(C.g1m.ap(), C.srow.ap()[0:1, 0:NH],
                                         -1.0, 1.0, OP.mult, OP.add)
            else:
                ins = None
            em.sig(pre_c, ins)
            barrier(em, C)

        run_phase(block, em, C, ph_pre)

        sc2 = OUTER.enter_context(contextlib.ExitStack())
        C.fbuf = [sc2.enter_context(nc.sbuf_tensor(f"fbuf{i}", [128, S], F32)) for i in range(2)]
        C.row_a = sc2.enter_context(nc.sbuf_tensor("row_a", [1, S], F32))
        C.row_b = sc2.enter_context(nc.sbuf_tensor("row_b", [1, S], F32))
        C.istdB = sc2.enter_context(nc.sbuf_tensor("istdB", [128, S], F32))
        C.muistdB = sc2.enter_context(nc.sbuf_tensor("muistdB", [128, S], F32))
        sc1 = contextlib.ExitStack()
        C.xb = [sc1.enter_context(nc.sbuf_tensor(f"xb{i}", [128, S], BF16)) for i in range(2)]
        C.xsq = [sc1.enter_context(nc.sbuf_tensor(f"xsq{i}", [128, S], BF16)) for i in range(2)]
        C.muT = sc1.enter_context(nc.sbuf_tensor("muT", [128, 32], F32))
        C.msqT = sc1.enter_context(nc.sbuf_tensor("msqT", [128, 32], F32))
        C.tmpA = sc1.enter_context(nc.sbuf_tensor("tmpA", [128, 32], F32))
        C.tmpB = sc1.enter_context(nc.sbuf_tensor("tmpB", [128, 32], F32))
        C.r32a = sc1.enter_context(nc.sbuf_tensor("r32a", [32, 128], F32))
        C.r32b = sc1.enter_context(nc.sbuf_tensor("r32b", [32, 128], F32))
        C.pmu = sc1.enter_context(nc.psum_tensor("pmu", [128, 512], F32))
        C.pms = sc1.enter_context(nc.psum_tensor("pms", [128, 512], F32))
        C.pg = [sc1.enter_context(nc.psum_tensor(f"pg{i}", [128, 512], F32)) for i in range(2)]
        C.pt32 = sc1.enter_context(nc.psum_tensor("pt32", [128, 512], F32))

        def ph_p0a(em, C):
            ld = [ED["p0_ld0"], ED["p0_ld1"]]
            cv, cw, mm, fin = ED["p0_cv"], ED["p0_cw"], ED["p0_mm"], ED["p0_fin"]
            bct, bcv2 = ED["p0_bc_t"], ED["p0_bc_v"]
            prow = ED["p0_row"]
            b_cv, b_cw, b_mm, b_fin = cv.count, cw.count, mm.count, fin.count
            for k in range(16):
                if em.on("sync"):
                    if k >= 2:
                        em.wait_n(mm, b_mm + k - 1)
                    ins = em.e.dma_start(out=C.fbuf[k % 2].ap(),
                                         in_=C.xT[128 * k:128 * (k + 1), :])
                else:
                    ins = None
                em.sig(ld[k % 2], ins)
                if em.on("scalar"):
                    em.wait_cur(ld[k % 2])
                    ins = em.e.activation(C.xsq[k % 2].ap(), C.fbuf[k % 2].ap(), AF.Square)
                else:
                    ins = None
                em.sig(cv, ins)
                if em.on("vector"):
                    em.wait_cur(ld[k % 2])
                    ins = em.e.tensor_copy(C.xb[k % 2].ap(), C.fbuf[k % 2].ap())
                else:
                    ins = None
                em.sig(cw, ins)
                if em.on("tensor"):
                    em.wait_n(cv, b_cv + k + 1)
                    em.wait_n(cw, b_cw + k + 1)
                    for j in range(32):
                        em.e.matmul(C.pmu.ap()[:, j:j + 1],
                                    C.xb[k % 2].ap()[:, 128 * j:128 * (j + 1)],
                                    C.ones_bf.ap(), start=(k == 0 and j == 0),
                                    stop=(k == 15 and j == 31))
                    ins = None
                    for j in range(32):
                        ins = em.e.matmul(C.pms.ap()[:, j:j + 1],
                                          C.xsq[k % 2].ap()[:, 128 * j:128 * (j + 1)],
                                          C.ones_bf.ap(), start=(k == 0 and j == 0),
                                          stop=(k == 15 and j == 31))
                else:
                    ins = None
                em.sig(mm, ins)
            if em.on("vector"):
                em.wait_n(mm, b_mm + 16)
                em.e.tensor_scalar(C.muT.ap(), C.pmu.ap()[:, 0:32], 1.0 / float(H), None, OP.mult)
                ins = em.e.tensor_scalar(C.msqT.ap(), C.pms.ap()[:, 0:32], 1.0 / float(H), None, OP.mult)
            else:
                ins = None
            fence(em, "vector", "sf_v", ins)
            if em.on("vector"):
                ins = em.e.tensor_tensor(C.tmpA.ap(), C.muT.ap(), C.muT.ap(), OP.mult)
            else:
                ins = None
            fence(em, "vector", "sf_v", ins)
            if em.on("vector"):
                ins = em.e.tensor_tensor(C.tmpB.ap(), C.msqT.ap(), C.tmpA.ap(), OP.subtract)
            else:
                ins = None
            fence(em, "vector", "sf_v", ins)
            if em.on("vector"):
                ins = em.e.tensor_scalar(C.tmpA.ap(), C.tmpB.ap(), EPS, None, OP.add)
            else:
                ins = None
            em.sig(fin, ins)
            if em.on("scalar"):
                em.wait_n(fin, b_fin + 1)
                ins = em.e.activation(C.tmpB.ap(), C.tmpA.ap(), AF.Sqrt)
            else:
                ins = None
            em.sig(fin, ins)
            if em.on("vector"):
                em.wait_n(fin, b_fin + 2)
                ins = em.e.reciprocal(C.tmpA.ap(), C.tmpB.ap())
            else:
                ins = None
            fence(em, "vector", "sf_v", ins)
            if em.on("vector"):
                ins = em.e.tensor_tensor(C.msqT.ap(), C.muT.ap(), C.tmpA.ap(), OP.mult)
            else:
                ins = None
            em.sig(fin, ins)
            b_bct, b_bcv = bct.count, bcv2.count
            if em.on("tensor"):
                em.wait_n(fin, b_fin + 3)
                ins = em.e.transpose(C.pt32.ap()[0:32, 0:128], C.tmpA.ap(), C.eyeSf.ap())
            else:
                ins = None
            em.sig(bct, ins)
            if em.on("vector"):
                em.wait_n(bct, b_bct + 1)
                ins = em.e.tensor_copy(C.r32a.ap(), C.pt32.ap()[0:32, 0:128])
            else:
                ins = None
            em.sig(bcv2, ins)
            if em.on("tensor"):
                em.wait_n(bcv2, b_bcv + 1)
                ins = em.e.transpose(C.pt32.ap()[0:32, 0:128], C.msqT.ap(), C.eyeSf.ap())
            else:
                ins = None
            em.sig(bct, ins)
            if em.on("vector"):
                em.wait_n(bct, b_bct + 2)
                ins = em.e.tensor_copy(C.r32b.ap(), C.pt32.ap()[0:32, 0:128])
            else:
                ins = None
            em.sig(bcv2, ins)
            b_prow = prow.count
            if em.on("sync"):
                em.wait_n(bcv2, b_bcv + 2)
                ins = em.e.dma_start(out=C.row_a.ap(), in_=C.r32a.ap())
                em.sig(prow, ins)
                ins = em.e.dma_start(out=C.row_b.ap(), in_=C.r32b.ap())
                em.sig(prow, ins)
            else:
                em.sig(prow, None)
                em.sig(prow, None)
            b2t, b2v = bct.count, bcv2.count
            for it in range(16):
                src = C.row_a if it < 8 else C.row_b
                dst = C.istdB if it < 8 else C.muistdB
                s = (it % 8) * 512
                if em.on("tensor"):
                    em.wait_n(prow, b_prow + 2)
                    if it >= 2:
                        em.wait_n(bcv2, b2v + it - 1)
                    ins = em.e.matmul(C.pg[it % 2].ap(), C.onescol_f.ap(),
                                      src.ap()[0:1, s:s + 512], start=True, stop=True)
                else:
                    ins = None
                em.sig(bct, ins)
                if em.on("vector"):
                    em.wait_n(bct, b2t + it + 1)
                    ins = em.e.tensor_copy(dst.ap()[:, s:s + 512], C.pg[it % 2].ap())
                else:
                    ins = None
                em.sig(bcv2, ins)
            barrier(em, C)

        run_phase(block, em, C, ph_p0a)
        sc1.close()

        sc3 = contextlib.ExitStack()
        C.tg = [sc3.enter_context(nc.sbuf_tensor(f"tg{i}", [128, S], F32)) for i in range(2)]
        C.xnb = [sc3.enter_context(nc.sbuf_tensor(f"xnb{i}", [128, S], BF16)) for i in range(2)]

        def ph_p0b(em, C):
            ld = [ED["b_ld0"], ED["b_ld1"]]
            st = [ED["b_st0"], ED["b_st1"]]
            eg, ev = ED["b_g"], ED["b_v"]
            b_g, b_v = eg.count, ev.count
            for k in range(16):
                if em.on("sync"):
                    if k >= 2:
                        em.wait_n(eg, b_g + k - 1)
                    ins = em.e.dma_start(out=C.fbuf[k % 2].ap(),
                                         in_=C.xT[128 * k:128 * (k + 1), :])
                else:
                    ins = None
                em.sig(ld[k % 2], ins)
                if em.on("gpsimd"):
                    em.wait_cur(ld[k % 2])
                    if k >= 2:
                        em.wait_n(ev, b_v + k - 1)
                    ins = em.e.tensor_tensor(C.tg[k % 2].ap(), C.fbuf[k % 2].ap(),
                                             C.istdB.ap(), OP.mult)
                else:
                    ins = None
                em.sig(eg, ins)
                if em.on("vector"):
                    em.wait_n(eg, b_g + k + 1)
                    em.wait_cur(st[k % 2])
                    ins = em.e.tensor_tensor(C.xnb[k % 2].ap(), C.tg[k % 2].ap(),
                                             C.muistdB.ap(), OP.subtract)
                else:
                    ins = None
                em.sig(ev, ins)
                if em.on("sync"):
                    em.wait_n(ev, b_v + k + 1)
                    ins = em.e.dma_start(out=C.xnfm_h[128 * k:128 * (k + 1), :],
                                         in_=C.xnb[k % 2].ap())
                else:
                    ins = None
                em.sig(st[k % 2], ins)
            barrier(em, C)

        run_phase(block, em, C, ph_p0b)
        sc3.close()
        sc2.close()

        sc4 = OUTER.enter_context(contextlib.ExitStack())
        C.muoB = sc4.enter_context(nc.sbuf_tensor("muoB", [128, TOK], F32))
        C.istdoB = sc4.enter_context(nc.sbuf_tensor("istdoB", [128, TOK], F32))
        C.c1oB = sc4.enter_context(nc.sbuf_tensor("c1oB", [128, TOK], F32))
        sc5 = contextlib.ExitStack()
        C.obuf = [sc5.enter_context(nc.sbuf_tensor(f"obuf{i}", [128, H], F32)) for i in range(2)]
        C.dmpS = [sc5.enter_context(nc.sbuf_tensor(f"dmpS{i}", [128, H], F32)) for i in range(2)]
        C.dmpV = [sc5.enter_context(nc.sbuf_tensor(f"dmpV{i}", [128, H], F32)) for i in range(2)]
        C.t8 = [sc5.enter_context(nc.sbuf_tensor(f"t8_{i}", [128, 8], F32)) for i in range(4)]
        C.ro8 = sc5.enter_context(nc.sbuf_tensor("ro8", [8, 128], F32))
        C.rowo = sc5.enter_context(nc.sbuf_tensor("rowo", [1, TOK], F32))
        C.pso = [sc5.enter_context(nc.psum_tensor(f"pso{i}", [128, 512], F32)) for i in range(2)]
        C.pt8 = sc5.enter_context(nc.psum_tensor("pt8", [128, 512], F32))

        def ph_p0c(em, C):
            ld = [ED["c_ld0"], ED["c_ld1"]]
            crow = ED["c_row"]
            es, ev, fin, et = ED["c_s"], ED["c_v"], ED["c_fin"], ED["c_t"]
            b_s, b_v = es.count, ev.count
            for t in range(8):
                if em.on("sync"):
                    if t >= 2:
                        em.wait_n(es, b_s + t - 1)
                        em.wait_n(ev, b_v + t - 1)
                    ins = em.e.dma_start(out=C.obuf[t % 2].ap(),
                                         in_=C.xo[128 * t:128 * (t + 1), :])
                else:
                    ins = None
                em.sig(ld[t % 2], ins)
                if em.on("scalar"):
                    em.wait_cur(ld[t % 2])
                    if t >= 2:
                        em.wait_cur(es, lag=1)
                    ins = em.e.activation(C.dmpS[t % 2].ap(), C.obuf[t % 2].ap(),
                                          AF.Identity,
                                          accum_out=C.t8[0].ap()[:, t:t + 1])
                else:
                    ins = None
                em.sig(es, ins)
                if em.on("vector"):
                    em.wait_cur(ld[t % 2])
                    if t >= 2:
                        em.wait_cur(ev, lag=1)
                    ins = em.e.scalar_tensor_tensor(C.dmpV[t % 2].ap(), C.obuf[t % 2].ap(),
                                                    0.0, C.obuf[t % 2].ap(),
                                                    OP.bypass, OP.mult,
                                                    accum_out=C.t8[1].ap()[:, t:t + 1])
                else:
                    ins = None
                em.sig(ev, ins)
            if em.on("vector"):
                em.wait_n(es, b_s + 8)
                em.wait_n(ev, b_v + 8)
                ins = em.e.tensor_scalar(C.stats.ap()[:, 0:8], C.t8[0].ap(),
                                         1.0 / float(H), None, OP.mult)
            else:
                ins = None
            fence(em, "vector", "sf_v", ins)
            if em.on("vector"):
                em.e.tensor_scalar(C.t8[2].ap(), C.t8[1].ap(), 1.0 / float(H), None, OP.mult)
                ins = em.e.tensor_tensor(C.t8[3].ap(), C.stats.ap()[:, 0:8],
                                         C.stats.ap()[:, 0:8], OP.mult)
            else:
                ins = None
            fence(em, "vector", "sf_v", ins)
            if em.on("vector"):
                ins = em.e.tensor_tensor(C.t8[2].ap(), C.t8[2].ap(), C.t8[3].ap(), OP.subtract)
            else:
                ins = None
            fence(em, "vector", "sf_v", ins)
            if em.on("vector"):
                ins = em.e.tensor_scalar(C.t8[3].ap(), C.t8[2].ap(), EPS, None, OP.add)
            else:
                ins = None
            em.sig(fin, ins)
            if em.on("scalar"):
                em.wait_cur(fin)
                ins = em.e.activation(C.t8[1].ap(), C.t8[3].ap(), AF.Sqrt)
            else:
                ins = None
            em.sig(fin, ins)
            if em.on("vector"):
                em.wait_cur(fin)
                ins = em.e.reciprocal(C.stats.ap()[:, 8:16], C.t8[1].ap())
            else:
                ins = None
            em.sig(fin, ins)
            b_v2 = ev.count
            for t in range(8):
                if em.on("sync"):
                    em.wait_n(es, b_s + 8)
                    em.wait_n(ev, b_v + 8)
                    if t >= 2:
                        em.wait_n(ev, b_v2 + 2 * (t - 1))
                    ins = em.e.dma_start(out=C.obuf[t % 2].ap(),
                                         in_=C.xo[128 * t:128 * (t + 1), :])
                else:
                    ins = None
                em.sig(ld[t % 2], ins)
                if em.on("vector"):
                    em.wait_cur(ld[t % 2])
                    em.wait_n(fin, ED["c_fin"].count)
                    if t >= 2:
                        em.wait_n(ev, b_v2 + 2 * (t - 1))
                    ins = em.e.tensor_scalar(C.dmpV[t % 2].ap(), C.obuf[t % 2].ap(),
                                             C.stats.ap()[:, 0 + t:1 + t],
                                             C.stats.ap()[:, 8 + t:9 + t],
                                             OP.subtract, OP.mult)
                else:
                    ins = None
                em.sig(ev, ins)
                if em.on("vector"):
                    em.wait_cur(ev)
                    ins = em.e.tensor_reduce(C.stats.ap()[:, 16 + t:17 + t],
                                             C.dmpV[t % 2].ap(), AX.X, OP.max,
                                             apply_absolute_value=True)
                else:
                    ins = None
                em.sig(ev, ins)
            if em.on("vector"):
                em.wait_n(ev, b_v2 + 16)
                ins = em.e.tensor_scalar(C.t8[3].ap(), C.t8[2].ap(), EPS, None, OP.add)
            else:
                ins = None
            fence(em, "vector", "sf_v", ins)
            if em.on("vector"):
                ins = em.e.reciprocal(C.t8[1].ap(), C.t8[3].ap())
            else:
                ins = None
            fence(em, "vector", "sf_v", ins)
            if em.on("vector"):
                ins = em.e.tensor_tensor(C.t8[3].ap(), C.t8[2].ap(), C.t8[1].ap(), OP.mult)
            else:
                ins = None
            fence(em, "vector", "sf_v", ins)
            if em.on("vector"):
                ins = em.e.tensor_scalar(C.t8[1].ap(), C.t8[3].ap(), EPS, None, OP.add)
            else:
                ins = None
            em.sig(fin, ins)
            if em.on("scalar"):
                em.wait_cur(fin)
                ins = em.e.activation(C.t8[3].ap(), C.t8[1].ap(), AF.Sqrt)
            else:
                ins = None
            em.sig(fin, ins)
            if em.on("vector"):
                em.wait_cur(fin)
                ins = em.e.reciprocal(C.t8[1].ap(), C.t8[3].ap())
            else:
                ins = None
            fence(em, "vector", "sf_v", ins)
            if em.on("vector"):
                ins = em.e.tensor_tensor(C.t8[3].ap(), C.t8[1].ap(),
                                         C.stats.ap()[:, 16:24], OP.mult)
            else:
                ins = None
            fence(em, "vector", "sf_v", ins)
            if em.on("vector"):
                ins = em.e.tensor_scalar(C.t8[2].ap(), C.t8[3].ap(), 1e-5, None, OP.max)
            else:
                ins = None
            fence(em, "vector", "sf_v", ins)
            if em.on("vector"):
                ins = em.e.reciprocal(C.t8[3].ap(), C.t8[2].ap())
            else:
                ins = None
            fence(em, "vector", "sf_v", ins)
            if em.on("vector"):
                ins = em.e.tensor_scalar(C.stats.ap()[:, 32:40], C.t8[3].ap(),
                                         127.0, None, OP.mult)
            else:
                ins = None
            fence(em, "vector", "sf_v", ins)
            if em.on("vector"):
                ins = em.e.tensor_tensor(C.stats.ap()[:, 24:32], C.t8[1].ap(),
                                         C.stats.ap()[:, 32:40], OP.mult)
            else:
                ins = None
            em.sig(fin, ins)
            b_t, b_v3 = et.count, ev.count
            for r in range(3):
                src = [C.stats.ap()[:, 0:8], C.stats.ap()[:, 8:16], C.stats.ap()[:, 24:32]][r]
                dst = [C.muoB, C.istdoB, C.c1oB][r]
                if em.on("tensor"):
                    em.wait_n(fin, ED["c_fin"].count)
                    if r >= 1:
                        em.wait_n(ev, b_v3 + 3 * r)
                    ins = em.e.transpose(C.pt8.ap()[0:8, 0:128], src, C.eyeSf.ap())
                else:
                    ins = None
                em.sig(et, ins)
                if em.on("vector"):
                    em.wait_n(et, b_t + 3 * r + 1)
                    ins = em.e.tensor_copy(C.ro8.ap(), C.pt8.ap()[0:8, 0:128])
                else:
                    ins = None
                em.sig(ev, ins)
                if em.on("sync"):
                    em.wait_n(ev, b_v3 + 3 * r + 1)
                    em.wait_n(et, b_t + 3 * r)
                    ins = em.e.dma_start(out=C.rowo.ap(), in_=C.ro8.ap())
                else:
                    ins = None
                em.sig(crow, ins)
                for half in range(2):
                    if em.on("tensor"):
                        em.wait_cur(crow)
                        ins = em.e.matmul(C.pso[half].ap(), C.onescol_f.ap(),
                                          C.rowo.ap()[0:1, 512 * half:512 * (half + 1)],
                                          start=True, stop=True)
                    else:
                        ins = None
                    em.sig(et, ins)
                    if em.on("vector"):
                        em.wait_n(et, b_t + 3 * r + 2 + half)
                        ins = em.e.tensor_copy(dst.ap()[:, 512 * half:512 * (half + 1)],
                                               C.pso[half].ap())
                    else:
                        ins = None
                    em.sig(ev, ins)
            barrier(em, C)

        run_phase(block, em, C, ph_p0c)
        sc5.close()

        sc6 = contextlib.ExitStack()
        C.fo = [sc6.enter_context(nc.sbuf_tensor(f"fo{i}", [128, TOK], F32)) for i in range(2)]
        C.ta = [sc6.enter_context(nc.sbuf_tensor(f"ta{i}", [128, TOK], F32)) for i in range(2)]
        C.xnf = [sc6.enter_context(nc.sbuf_tensor(f"xnf{i}", [128, TOK], F32)) for i in range(2)]
        C.uu = [sc6.enter_context(nc.sbuf_tensor(f"uu{i}", [128, TOK], F32)) for i in range(2)]
        C.xnbo = [sc6.enter_context(nc.sbuf_tensor(f"xnbo{i}", [128, TOK], BF16)) for i in range(2)]
        C.i16o = [sc6.enter_context(nc.sbuf_tensor(f"i16o{i}", [128, TOK], I16)) for i in range(2)]
        C.i16d = [sc6.enter_context(nc.sbuf_tensor(f"i16d{i}", [128, TOK], I16)) for i in range(2)]
        C.fdf = [sc6.enter_context(nc.sbuf_tensor(f"fdf{i}", [128, TOK], F32)) for i in range(2)]
        C.xq1b = [sc6.enter_context(nc.sbuf_tensor(f"xq1b{i}", [128, TOK], BF16)) for i in range(2)]

        def ph_p0d(em, C):
            ld = [ED["d_ld0"], ED["d_ld1"]]
            sa = [ED["d_sa0"], ED["d_sa1"]]
            sb = [ED["d_sb0"], ED["d_sb1"]]
            dg, dv, ds = ED["d_g"], ED["d_vv"], ED["d_s"]
            b_g, b_v, b_s = dg.count, dv.count, ds.count
            for k in range(16):
                if em.on("sync"):
                    if k >= 2:
                        em.wait_n(dg, b_g + 3 * (k - 2) + 1)
                    ins = em.e.dma_start(out=C.fo[k % 2].ap(),
                                         in_=C.xTo[128 * k:128 * (k + 1), :])
                else:
                    ins = None
                em.sig(ld[k % 2], ins)
                if em.on("gpsimd"):
                    em.wait_cur(ld[k % 2])
                    if k >= 2:
                        em.wait_n(dv, b_v + 4 * (k - 2) + 1)
                    ins = em.e.tensor_tensor(C.ta[k % 2].ap(), C.fo[k % 2].ap(),
                                             C.muoB.ap(), OP.subtract)
                else:
                    ins = None
                em.sig(dg, ins)
                if em.on("vector"):
                    em.wait_n(dg, b_g + 3 * k + 1)
                    if k >= 2:
                        em.wait_n(dg, b_g + 3 * (k - 2) + 2)
                        em.wait_n(dv, b_v + 4 * (k - 2) + 2)
                    ins = em.e.tensor_tensor(C.xnf[k % 2].ap(), C.ta[k % 2].ap(),
                                             C.istdoB.ap(), OP.mult)
                else:
                    ins = None
                em.sig(dv, ins)
                if em.on("gpsimd"):
                    em.wait_n(dv, b_v + 4 * k + 1)
                    em.wait_cur(sa[k % 2])
                    ins = em.e.tensor_copy(C.xnbo[k % 2].ap(), C.xnf[k % 2].ap())
                else:
                    ins = None
                em.sig(dg, ins)
                if em.on("vector"):
                    em.wait_n(dv, b_v + 4 * k + 1)
                    if k >= 2:
                        em.wait_n(ds, b_s + 2 * (k - 2) + 1)
                        em.wait_n(dg, b_g + 3 * (k - 2) + 3)
                        em.wait_n(dv, b_v + 4 * (k - 2) + 3)
                    ins = em.e.tensor_tensor(C.uu[k % 2].ap(), C.xnf[k % 2].ap(),
                                             C.c1oB.ap(), OP.mult)
                else:
                    ins = None
                em.sig(dv, ins)
                if em.on("vector"):
                    em.wait_n(dv, b_v + 4 * k + 2)
                    if k >= 2:
                        em.wait_n(ds, b_s + 2 * (k - 2) + 1)
                        em.wait_n(dg, b_g + 3 * (k - 2) + 3)
                    ins = em.e.tensor_scalar(C.uu[k % 2].ap(), C.uu[k % 2].ap(),
                                             12582912.0, -12582912.0, OP.add, OP.add)
                else:
                    ins = None
                em.sig(dv, ins)
                if em.on("scalar"):
                    em.wait_n(dv, b_v + 4 * k + 3)
                    if k >= 2:
                        em.wait_n(dg, b_g + 3 * (k - 2) + 3)
                        em.wait_n(dv, b_v + 4 * (k - 2) + 4)
                    ins = em.e.copy(C.i16o[k % 2].ap(), C.uu[k % 2].ap())
                else:
                    ins = None
                em.sig(ds, ins)
                if em.on("gpsimd"):
                    em.wait_n(dv, b_v + 4 * k + 3)
                    em.wait_n(ds, b_s + 2 * k + 1)
                    if k >= 2:
                        em.wait_n(ds, b_s + 2 * (k - 2) + 2)
                    ins = em.e.tensor_tensor(C.fdf[k % 2].ap(), C.uu[k % 2].ap(),
                                             C.i16o[k % 2].ap(), OP.subtract)
                else:
                    ins = None
                em.sig(dg, ins)
                if em.on("scalar"):
                    em.wait_n(dg, b_g + 3 * k + 3)
                    ins = em.e.activation(C.i16d[k % 2].ap(), C.fdf[k % 2].ap(),
                                          AF.Identity, scale=2.0)
                else:
                    ins = None
                em.sig(ds, ins)
                if em.on("vector"):
                    em.wait_n(ds, b_s + 2 * k + 2)
                    em.wait_cur(sb[k % 2])
                    ins = em.e.tensor_tensor(C.xq1b[k % 2].ap(), C.i16o[k % 2].ap(),
                                             C.i16d[k % 2].ap(), OP.add)
                else:
                    ins = None
                em.sig(dv, ins)
                if em.on("sync"):
                    em.wait_n(dg, b_g + 3 * k + 2)
                    ins = em.e.dma_start(out=C.xnown_h[128 * k:128 * (k + 1), :],
                                         in_=C.xnbo[k % 2].ap())
                    em.sig(sa[k % 2], ins)
                    em.wait_n(dv, b_v + 4 * k + 4)
                    ins = em.e.dma_start(out=C.xq1_h[128 * k:128 * (k + 1), :],
                                         in_=C.xq1b[k % 2].ap())
                    em.sig(sb[k % 2], ins)
                else:
                    em.sig(sa[k % 2], None)
                    em.sig(sb[k % 2], None)
            barrier(em, C)

        run_phase(block, em, C, ph_p0d)
        sc6.close()
        sc4.close()
        sc_eye.close()

        sc7 = contextlib.ExitStack()
        C.fw = [sc7.enter_context(nc.sbuf_tensor(f"fw{i}", [128, INNER], F32)) for i in range(2)]
        C.dw = [sc7.enter_context(nc.sbuf_tensor(f"dw{i}", [128, INNER], BF16)) for i in range(2)]
        C.pW = sc7.enter_context(nc.psum_tensor("pW", [128, 512], F32))
        C.aw1 = sc7.enter_context(nc.sbuf_tensor("aw1", [128, 16], F32))
        C.aw2 = sc7.enter_context(nc.sbuf_tensor("aw2", [128, 64], F32))

        def ph_wmean(em, C):
            ld = [ED["w_ld0"], ED["w_ld1"]]
            ws_, wv, wfin, wt = ED["w_s"], ED["w_v"], ED["w_fin"], ED["w_t"]
            b_s, b_v = ws_.count, wv.count
            for k in range(16):
                if em.on("sync"):
                    if k >= 2:
                        em.wait_n(ws_, b_s + k - 1)
                    ins = em.e.dma_start(out=C.fw[k % 2].ap(),
                                         in_=C.w1T[128 * k:128 * (k + 1), :])
                else:
                    ins = None
                em.sig(ld[k % 2], ins)
                if em.on("scalar"):
                    em.wait_cur(ld[k % 2])
                    if k >= 2:
                        em.wait_cur(ws_, lag=1)
                    ins = em.e.activation(C.dw[k % 2].ap(), C.fw[k % 2].ap(), AF.Abs,
                                          accum_out=C.aw1.ap()[:, k:k + 1])
                else:
                    ins = None
                em.sig(ws_, ins)
            b_v2 = wv.count
            for k in range(64):
                if em.on("sync"):
                    em.wait_n(ws_, b_s + 16)
                    if k >= 2:
                        em.wait_n(wv, b_v2 + k - 1)
                    ins = em.e.dma_start(out=C.fw[k % 2].ap()[:, 0:H],
                                         in_=C.w2T[128 * k:128 * (k + 1), :])
                else:
                    ins = None
                em.sig(ld[k % 2], ins)
                if em.on("vector"):
                    em.wait_cur(ld[k % 2])
                    ins = em.e.tensor_reduce(C.aw2.ap()[:, k:k + 1], C.fw[k % 2].ap()[:, 0:H],
                                             AX.X, OP.add, apply_absolute_value=True)
                else:
                    ins = None
                em.sig(wv, ins)
            for w in range(2):
                aw = C.aw1 if w == 0 else C.aw2
                nel = float(H * INNER)
                if em.on("vector"):
                    em.wait_n(ws_, b_s + 16)
                    em.wait_n(wv, b_v2 + 64)
                    ins = em.e.tensor_reduce(C.nrm.ap()[:, 0:1], aw.ap(), AX.X, OP.add)
                else:
                    ins = None
                em.sig(wfin, ins)
                if em.on("tensor"):
                    em.wait_cur(wfin)
                    ins = em.e.matmul(C.pW.ap()[0:1, 0:1], C.nrm.ap()[:, 0:1],
                                      C.ones_f.ap(), start=True, stop=True)
                else:
                    ins = None
                em.sig(wt, ins)
                if em.on("vector"):
                    em.wait_cur(wt)
                    ins = em.e.tensor_scalar(C.srow.ap()[0:1, 100:101], C.pW.ap()[0:1, 0:1],
                                             1.0 / nel, 1e-5, OP.mult, OP.max)
                else:
                    ins = None
                fence(em, "vector", "sf_v", ins)
                if em.on("vector"):
                    ins = em.e.reciprocal(C.srow.ap()[0:1, 101:102], C.srow.ap()[0:1, 100:101])
                else:
                    ins = None
                em.sig(wfin, ins)
                if em.on("tensor"):
                    em.wait_cur(wfin)
                    ins = em.e.matmul(C.pW.ap()[:, 0:1], C.onescol_f.ap(),
                                      C.srow.ap()[0:1, 101:102], start=True, stop=True)
                else:
                    ins = None
                em.sig(wt, ins)
                if em.on("vector"):
                    em.wait_cur(wt)
                    ins = em.e.tensor_copy(C.wsB.ap()[:, w:w + 1], C.pW.ap()[:, 0:1])
                else:
                    ins = None
                fence(em, "vector", "sf_v", ins)
                if em.on("vector"):
                    ins = em.e.tensor_scalar(C.wsB.ap()[:, 2 + w:3 + w],
                                             C.wsB.ap()[:, w:w + 1], 2.0, None, OP.mult)
                else:
                    ins = None
                em.sig(wfin, ins)
            barrier(em, C)

        run_phase(block, em, C, ph_wmean)

        sc7b = contextlib.ExitStack()
        C.i8a = [sc7b.enter_context(nc.sbuf_tensor(f"i8a{i}", [128, INNER], I8)) for i in range(2)]
        C.i8b = [sc7b.enter_context(nc.sbuf_tensor(f"i8b{i}", [128, INNER], I8)) for i in range(2)]

        def ph_wquant(em, C):
            ld = [ED["wq_ld0"], ED["wq_ld1"]]
            st = [ED["wq_st0"], ED["wq_st1"]]
            qs, qg, qv = ED["wq_s"], ED["wq_g"], ED["wq_v"]
            for w in range(2):
                nk = 16 if w == 0 else 64
                ncol = INNER if w == 0 else H
                src = C.w1T if w == 0 else C.w2T
                dst = C.q1_h if w == 0 else C.q2_h
                b_s, b_v = qs.count, qv.count
                for k in range(-1, nk):
                    j = k + 1
                    if j < nk:
                        if em.on("sync"):
                            em.wait_n(qs, b_s)
                            em.wait_n(qv, b_v)
                            if j >= 2:
                                em.wait_n(qs, b_s + j - 1)
                                em.wait_n(qv, b_v + 3 * (j - 2) + 2)
                            ins = em.e.dma_start(out=C.fw[j % 2].ap()[:, 0:ncol],
                                                 in_=src[128 * j:128 * (j + 1), :])
                        else:
                            ins = None
                        em.sig(ld[j % 2], ins)
                    if k < 0:
                        continue
                    if em.on("scalar"):
                        em.wait_cur(ld[k % 2])
                        if k >= 2:
                            em.wait_n(qv, b_v + 3 * (k - 2) + 2)
                        ins = em.e.activation(C.fw[k % 2].ap()[:, 0:ncol],
                                              C.fw[k % 2].ap()[:, 0:ncol],
                                              AF.Identity, scale=C.wsB.ap()[:, w:w + 1])
                    else:
                        ins = None
                    em.sig(qs, ins)
                    if em.on("vector"):
                        em.wait_n(qs, b_s + k + 1)
                        if k >= 2:
                            em.wait_n(qv, b_v + 3 * (k - 2) + 2)
                        ins = em.e.tensor_scalar(C.fw[k % 2].ap()[:, 0:ncol],
                                                 C.fw[k % 2].ap()[:, 0:ncol],
                                                 12582912.0, -12582912.0, OP.add, OP.add)
                    else:
                        ins = None
                    em.sig(qv, ins)
                    if em.on("vector"):
                        em.wait_n(qv, b_v + 3 * k + 1)
                        if k >= 2:
                            em.wait_n(qv, b_v + 3 * (k - 2) + 3)
                        ins = em.e.tensor_scalar(C.i8a[k % 2].ap()[:, 0:ncol],
                                                 C.fw[k % 2].ap()[:, 0:ncol],
                                                 2.0, None, OP.mult)
                    else:
                        ins = None
                    em.sig(qv, ins)
                    if em.on("vector"):
                        em.wait_n(qv, b_v + 3 * k + 2)
                        em.wait_cur(st[k % 2])
                        ins = em.e.tensor_scalar(C.i8b[k % 2].ap()[:, 0:ncol],
                                                 C.i8a[k % 2].ap()[:, 0:ncol],
                                                 -1.0, 1.0, OP.max, OP.min)
                    else:
                        ins = None
                    em.sig(qv, ins)
                    if em.on("sync"):
                        em.wait_n(qv, b_v + 3 * k + 3)
                        ins = em.e.dma_start(out=dst[128 * k:128 * (k + 1), :],
                                             in_=C.i8b[k % 2].ap()[:, 0:ncol])
                    else:
                        ins = None
                    em.sig(st[k % 2], ins)
            barrier(em, C)

        run_phase(block, em, C, ph_wquant)
        sc7b.close()
        sc7.close()

        sc_qfm = contextlib.ExitStack()
        C.qfm = sc_qfm.enter_context(nc.sbuf_tensor("qfm", [128, 16 * 1024], BF16))
        sc8 = contextlib.ExitStack()
        C.wqbf = sc8.enter_context(nc.sbuf_tensor("wqbf", [128, 16 * H], BF16))
        C.xnq = sc8.enter_context(nc.sbuf_tensor("xnq", [128, 16 * TOK], BF16))
        C.pq = [sc8.enter_context(nc.psum_tensor(f"pq{i}", [128, 512], F32)) for i in range(2)]

        def ph_q(em, C):
            ld, qmm, qev = ED["q_ld"], ED["q_mm"], ED["q_ev"]
            b_ld, b_mm, b_ev = ld.count, qmm.count, qev.count
            ins = (em.e.dma_start(out=C.wqbf.ap().rearrange("p (k o) -> p k o", k=16),
                                  in_=C.wqT.rearrange("(k p) o -> p k o", p=128))
                   if em.on("gpsimd") else None)
            em.sig(ED["gled"], ins)
            ins = (em.e.dma_start(out=C.xnq.ap().rearrange("p (k t) -> p k t", k=16),
                                  in_=C.xnown_h.rearrange("(k p) t -> p k t", p=128))
                   if em.on("sync") else None)
            em.sig(ld, ins)
            for j in range(16):
                for half in range(2):
                    idx = 2 * j + half
                    if em.on("tensor"):
                        em.wait_cur(ld)
                        em.wait_cur(ED["gled"])
                        if idx >= 2:
                            em.wait_n(qev, b_ev + idx - 1)
                        ins = None
                        for kk in range(16):
                            ins = em.e.matmul(
                                C.pq[idx % 2].ap(),
                                C.wqbf.ap()[:, 2048 * kk + 128 * j:2048 * kk + 128 * (j + 1)],
                                C.xnq.ap()[:, 1024 * kk + 512 * half:1024 * kk + 512 * (half + 1)],
                                start=(kk == 0), stop=(kk == 15))
                    else:
                        ins = None
                    em.sig(qmm, ins)
                    if em.on("vector"):
                        em.wait_n(qmm, b_mm + idx + 1)
                        ins = em.e.tensor_copy(
                            C.qfm.ap()[:, 1024 * j + 512 * half:1024 * j + 512 * (half + 1)],
                            C.pq[idx % 2].ap())
                    else:
                        ins = None
                    em.sig(qev, ins)
            barrier(em, C)

        run_phase(block, em, C, ph_q)
        sc8.close()

        sc9 = contextlib.ExitStack()
        C.kraw = sc9.enter_context(nc.sbuf_tensor("kraw", [128, 4 * S], BF16))
        C.vtm = sc9.enter_context(nc.sbuf_tensor("vtm", [128, 32 * 512], BF16))
        C.kftm = sc9.enter_context(nc.sbuf_tensor("kftm", [128, S], BF16))
        C.scrA = sc9.enter_context(nc.sbuf_tensor("scrA", [128, S], BF16))
        C.scrB = sc9.enter_context(nc.sbuf_tensor("scrB", [128, S], BF16))
        C.kx = [sc9.enter_context(nc.sbuf_tensor(f"kx{i}", [128, S], BF16)) for i in range(2)]
        C.wkbf = sc9.enter_context(nc.sbuf_tensor("wkbf", [128, 16 * 512], BF16))
        C.wvbf = sc9.enter_context(nc.sbuf_tensor("wvbf", [128, 16 * 512], BF16))
        C.tbc = sc9.enter_context(nc.sbuf_tensor("tbc", [128, S], BF16))
        C.tbs = sc9.enter_context(nc.sbuf_tensor("tbs", [128, S], BF16))
        C.cbuf = sc9.enter_context(nc.sbuf_tensor("cbuf", [128, 512], BF16))
        C.scl = sc9.enter_context(nc.sbuf_tensor("scl", [1, 16], F32))
        C.pA = sc9.enter_context(nc.psum_tensor("pA", [128, 1024], F32))
        C.pB = sc9.enter_context(nc.psum_tensor("pB", [128, 1024], F32))
        C.pC = sc9.enter_context(nc.psum_tensor("pC", [128, 512], F32))
        C.pD = sc9.enter_context(nc.psum_tensor("pD", [128, 512], F32))
        C.pE = sc9.enter_context(nc.psum_tensor("pE", [128, 512], F32))
        C.pTb = sc9.enter_context(nc.psum_tensor("pTb", [128, 512], BF16))
        C.PT = sc9.enter_context(nc.sbuf_tensor("PT", [128, 2048], BF16))
        C.qf = sc9.enter_context(nc.sbuf_tensor("qf", [128, 1024], BF16))
        C.mem_bf = sc9.enter_context(nc.sbuf_tensor("mem_bf", [128, 132], BF16))

        def ph_attn(em, C):
            ldw = ED["a_ldw"]
            ldx = [ED["a_ldx0"], ED["a_ldx1"]]
            ldt = ED["a_ldt"]
            akv, amm = ED["a_kv"], ED["a_mm"]
            arg, arv = ED["a_rope_g"], ED["a_rope_v"]
            ant, anv, ans, ang = ED["a_nrm_t"], ED["a_nrm_v"], ED["a_nrm_s"], ED["a_nrm_g"]
            aeg, aes, aev = ED["a_elu_g"], ED["a_elu_s"], ED["a_elu_v"]
            atp, atpv = ED["a_tp"], ED["a_tpv"]
            amem, amemv = ED["a_mem"], ED["a_memv"]
            ats, ast, atv = ED["a_ts"], ED["a_st"], ED["a_tv"]
            ep1, atd, ep2, atb, ep3 = ED["a_ep1"], ED["a_td"], ED["a_ep2"], ED["a_tb"], ED["a_ep3"]
            acb = [ED["a_cb0"], ED["a_cb1"]]

            hj_done = 0
            hrec = []

            for grp in range(4 if ATTN_LIMIT is None else ATTN_LIMIT):
                b_mm = amm.count
                b_kv = akv.count
                if em.on("gpsimd"):
                    em.wait_n(amm, b_mm)
                    ins = em.e.dma_start(
                        out=C.wkbf.ap().rearrange("p (k c) -> p k c", k=16),
                        in_=C.wkT[:, 512 * grp:512 * (grp + 1)].rearrange(
                            "(k p) c -> p k c", p=128))
                    em.sig(ldw, ins)
                    ins = em.e.dma_start(
                        out=C.wvbf.ap().rearrange("p (k c) -> p k c", k=16),
                        in_=C.wvT[:, 512 * grp:512 * (grp + 1)].rearrange(
                            "(k p) c -> p k c", p=128))
                    em.sig(ldw, ins)
                else:
                    em.sig(ldw, None)
                    em.sig(ldw, None)
                for hc in range(16):
                    if em.on("sync"):
                        em.wait_n(amm, b_mm)
                        if hc < 2:
                            em.wait_cur(ats)
                            em.wait_cur(ans)
                            em.wait_cur(aeg)
                            em.wait_cur(aev)
                        if hc >= 2:
                            em.wait_n(amm, b_mm + hc - 1)
                        ins = em.e.dma_start(
                            out=C.kx[hc % 2].ap().rearrange("p (k t) -> p k t", k=16),
                            in_=C.xnfm_h[:, 256 * hc:256 * (hc + 1)].rearrange(
                                "(k p) t -> p k t", p=128))
                    else:
                        ins = None
                    em.sig(ldx[hc % 2], ins)
                    if em.on("tensor"):
                        em.wait_cur(ldw)
                        em.wait_cur(ldx[hc % 2])
                        if hc >= 1:
                            em.wait_n(akv, b_kv + 6 * hc)
                        for ct in range(4):
                            for kk in range(16):
                                em.e.matmul(
                                    C.pA.ap()[:, 256 * ct:256 * (ct + 1)],
                                    C.wkbf.ap()[:, 512 * kk + 128 * ct:512 * kk + 128 * (ct + 1)],
                                    C.kx[hc % 2].ap()[:, 256 * kk:256 * (kk + 1)],
                                    start=(kk == 0 and ct % 2 == 0),
                                    stop=(kk == 15 and ct % 2 == 1))
                        ins = None
                        for tt in range(2):
                            for kk in range(16):
                                ins = em.e.matmul(
                                    C.pB.ap()[:, 512 * tt:512 * (tt + 1)],
                                    C.kx[hc % 2].ap()[:, 256 * kk + 128 * tt:256 * kk + 128 * (tt + 1)],
                                    C.wvbf.ap()[:, 512 * kk:512 * (kk + 1)],
                                    start=(kk == 0), stop=(kk == 15))
                    else:
                        ins = None
                    em.sig(amm, ins)
                    for ct in range(4):
                        if em.on("vector"):
                            em.wait_n(amm, b_mm + hc + 1)
                            ins = em.e.tensor_copy(
                                C.kraw.ap()[:, 4096 * ct + 256 * hc:4096 * ct + 256 * (hc + 1)],
                                C.pA.ap()[:, 256 * ct:256 * (ct + 1)])
                        else:
                            ins = None
                        em.sig(akv, ins)
                    for tt in range(2):
                        if em.on("vector"):
                            ins = em.e.tensor_copy(
                                C.vtm.ap()[:, 512 * (2 * hc + tt):512 * (2 * hc + tt + 1)],
                                C.pB.ap()[:, 512 * tt:512 * (tt + 1)])
                        else:
                            ins = None
                        em.sig(akv, ins)

                prod_amm = ED["a_mm"].count
                for ct in range(4):
                    if (ATTN_LIMIT is not None and grp == ATTN_LIMIT - 1
                            and ATTN_HEADS is not None and ct >= ATTN_HEADS):
                        break
                    hh = 4 * grp + ct
                    kro = C.kx[ct % 2].ap()
                    prev2 = hrec[-2] if len(hrec) >= 2 else None
                    if ct % 2 == 0:
                        pair = 2 * grp + ct // 2
                        if em.on("gpsimd"):
                            em.wait_cur(arv)
                            em.wait_cur(aev)
                            em.wait_cur(aeg)
                            em.wait_cur(arg)
                            em.wait_cur(atp)
                            em.wait_cur(ED["sf_v"])
                            for _src, _dst in ((C.tabc, C.tbc.ap()), (C.tabs, C.tbs.ap()),
                                               (C.tabco, C.scrA.ap()[:, 2048:3072]),
                                               (C.tabso, C.scrA.ap()[:, 3072:4096])):
                                ins = em.e.dma_start(out=_dst,
                                                     in_=_src[128 * pair:128 * (pair + 1), :])
                                em.sig(ldt, ins)
                        else:
                            for _ in range(4):
                                em.sig(ldt, None)
                        rA, rB = hh, hh + 1
                        qA = C.qfm.ap()[:, 1024 * rA:1024 * (rA + 1)]
                        qB = C.qfm.ap()[:, 1024 * rB:1024 * (rB + 1)]
                        if em.on("gpsimd"):
                            em.wait_cur(ldt)
                            em.wait_cur(aev)
                            em.wait_cur(ep3)
                            em.e.tensor_tensor(C.scrA.ap()[:, 1024:2048], qB,
                                               C.scrA.ap()[:, 3072:4096], OP.mult)
                            ins = em.e.tensor_tensor(C.scrB.ap()[:, 1024:2048], qA,
                                                     C.scrA.ap()[:, 3072:4096], OP.mult)
                        else:
                            ins = None
                        em.sig(arg, ins)
                        if em.on("vector"):
                            em.wait_cur(ldt)
                            em.wait_cur(ep3)
                            em.e.tensor_tensor(C.scrA.ap()[:, 0:1024], qA,
                                               C.scrA.ap()[:, 2048:3072], OP.mult)
                            ins = em.e.tensor_tensor(C.scrB.ap()[:, 0:1024], qB,
                                                     C.scrA.ap()[:, 2048:3072], OP.mult)
                        else:
                            ins = None
                        fence(em, "vector", "sf_v", ins)
                        if em.on("vector"):
                            em.wait_cur(arg)
                            em.e.tensor_tensor(qA, C.scrA.ap()[:, 0:1024],
                                               C.scrA.ap()[:, 1024:2048], OP.subtract)
                            ins = em.e.tensor_tensor(qB, C.scrB.ap()[:, 0:1024],
                                                     C.scrB.ap()[:, 1024:2048], OP.add)
                        else:
                            ins = None
                        em.sig(arv, ins)

                    own = C.kraw.ap()[:, 4096 * ct:4096 * (ct + 1)]
                    prt = C.kraw.ap()[:, 4096 * (ct ^ 1):4096 * ((ct ^ 1) + 1)]
                    if em.on("gpsimd"):
                        em.wait_n(akv, akv.count)
                        em.wait_cur(ldt)
                        em.wait_cur(aev)
                        em.wait_cur(arv)
                        em.wait_cur(atp)
                        ins = em.e.tensor_tensor(C.scrA.ap(), own, C.tbc.ap(), OP.mult)
                    else:
                        ins = None
                    em.sig(arg, ins)
                    if em.on("vector"):
                        em.wait_n(akv, akv.count)
                        em.wait_cur(ldt)
                        em.wait_cur(aes)
                        em.wait_cur(arv)
                        ins = em.e.tensor_tensor(C.scrB.ap(), prt, C.tbs.ap(), OP.mult)
                    else:
                        ins = None
                    fence(em, "vector", "sf_v", ins)
                    if em.on("vector"):
                        em.wait_cur(arg)
                        em.wait_n(amm, prod_amm)
                        if prev2 is not None:
                            em.wait_n(ats, prev2["ats"])
                            em.wait_n(aeg, prev2["aeg"])
                            em.wait_n(aev, prev2["aev"])
                            em.wait_n(ans, prev2["ans"])
                        ins = em.e.tensor_tensor(
                            kro, C.scrA.ap(), C.scrB.ap(),
                            OP.subtract if ct % 2 == 0 else OP.add)
                    else:
                        ins = None
                    em.sig(arv, ins)

                    if em.on("scalar"):
                        em.wait_cur(arv)
                        em.wait_cur(aev)
                        ins = em.e.activation(C.scrB.ap(), kro, AF.Square)
                    else:
                        ins = None
                    em.sig(ans, ins)
                    if em.on("tensor"):
                        em.wait_cur(ans)
                        em.wait_cur(amemv)
                        em.wait_cur(anv)
                        ins = None
                        for i in range(32):
                            ins = em.e.matmul(C.pE.ap()[:, 136 + i:137 + i],
                                              C.scrB.ap()[:, 128 * i:128 * (i + 1)],
                                              C.ones_bf.ap(), start=(i == 0), stop=False)
                    else:
                        ins = None
                    em.sig(ant, ins)
                    if em.on("scalar"):
                        em.wait_cur(ant)
                        ins = em.e.activation(C.scrB.ap()[:, 0:1024],
                                              C.qfm.ap()[:, 1024 * hh:1024 * (hh + 1)],
                                              AF.Square)
                    else:
                        ins = None
                    em.sig(ans, ins)
                    if em.on("tensor"):
                        em.wait_cur(ans)
                        ins = None
                        for t in range(8):
                            ins = em.e.matmul(C.pE.ap()[:, 168 + t:169 + t],
                                              C.scrB.ap()[:, 128 * t:128 * (t + 1)],
                                              C.ones_bf.ap(), start=False, stop=(t == 7))
                    else:
                        ins = None
                    em.sig(ant, ins)
                    if em.on("vector"):
                        em.wait_cur(ant)
                        ins = em.e.tensor_reduce(C.nrm.ap()[:, 1:2], C.pE.ap()[:, 136:168],
                                                 AX.X, OP.max)
                    else:
                        ins = None
                    fence(em, "vector", "sf_v", ins)
                    if em.on("vector"):
                        ins = em.e.tensor_reduce(C.nrm.ap()[:, 2:3], C.pE.ap()[:, 168:176],
                                                 AX.X, OP.max)
                    else:
                        ins = None
                    em.sig(anv, ins)
                    if em.on("vector"):
                        em.wait_cur(anv)
                        ins = em.e.tensor_copy(C.bcol.ap()[:, 0:2], C.nrm.ap()[:, 1:3])
                    else:
                        ins = None
                    em.sig(anv, ins)
                    if em.on("tensor"):
                        em.wait_cur(anv)
                        em.wait_cur(atpv)
                        ins = em.e.transpose(C.pTb.ap()[0:2, 0:128],
                                             C.bcol.ap()[:, 0:2], C.eyeS.ap())
                    else:
                        ins = None
                    em.sig(ant, ins)
                    if em.on("vector"):
                        em.wait_cur(ant)
                        em.wait_cur(ans)
                        ins = em.e.tensor_reduce(C.bcol.ap()[0:2, 2:3],
                                                 C.pTb.ap()[0:2, 0:128], AX.X, OP.max)
                    else:
                        ins = None
                    em.sig(anv, ins)
                    if em.on("scalar"):
                        em.wait_cur(anv)
                        em.wait_cur(ant)
                        ins = em.e.activation(C.nrm.ap()[0:2, 0:1], C.bcol.ap()[0:2, 2:3],
                                              AF.Ln)
                    else:
                        ins = None
                    em.sig(ans, ins)
                    if em.on("tensor"):
                        em.wait_cur(ans)
                        em.wait_cur(anv)
                        em.wait_n(ep3, hj_done)
                        ins = em.e.matmul(C.pC.ap()[0:1, 0:1], C.nrm.ap()[0:2, 0:1],
                                          C.ones_f.ap()[0:2, 0:1], start=True, stop=True)
                    else:
                        ins = None
                    em.sig(ant, ins)
                    if em.on("scalar"):
                        em.wait_cur(ant)
                        em.wait_cur(anv)
                        ins = em.e.activation(C.scl.ap()[0:1, 3:4], C.pC.ap()[0:1, 0:1],
                                              AF.Exp, scale=0.25)
                    else:
                        ins = None
                    em.sig(ans, ins)
                    if em.on("vector"):
                        em.wait_cur(ans)
                        ins = em.e.tensor_scalar(C.scl.ap()[0:1, 4:5], C.scl.ap()[0:1, 3:4],
                                                 -TAU, EXPC, OP.mult, OP.add)
                    else:
                        ins = None
                    em.sig(anv, ins)
                    if em.on("tensor"):
                        em.wait_cur(anv)
                        em.wait_cur(ans)
                        ins = em.e.matmul(C.pC.ap()[:, 0:1], C.onescol_f.ap(),
                                          C.scl.ap()[0:1, 4:5], start=True, stop=True)
                    else:
                        ins = None
                    em.sig(ant, ins)
                    if em.on("vector"):
                        em.wait_cur(ant)
                        if prev2 is not None:
                            em.wait_n(ast, prev2["ast"])
                        ins = em.e.tensor_copy(C.negb.ap()[:, hh % 2:hh % 2 + 1],
                                               C.pC.ap()[:, 0:1])
                    else:
                        ins = None
                    em.sig(anv, ins)

                    if em.on("vector"):
                        em.wait_cur(arv)
                        ins = em.e.tensor_scalar(C.scrA.ap(), kro, 0.0, None, OP.min)
                    else:
                        ins = None
                    em.sig(aeg, ins)
                    if em.on("scalar"):
                        em.wait_cur(aeg)
                        em.wait_cur(ans)
                        em.wait_cur(ant)
                        ins = em.e.activation(C.scrB.ap(), C.scrA.ap(), AF.Exp)
                    else:
                        ins = None
                    em.sig(aes, ins)
                    if em.on("vector"):
                        em.wait_cur(aes)
                        ins = em.e.scalar_tensor_tensor(C.scrA.ap(), kro, 1.0,
                                                        C.scrB.ap(), OP.add, OP.max)
                    else:
                        ins = None
                    em.sig(aev, ins)
                    b_tp, b_tpv = atp.count, atpv.count
                    for bt in range(8):
                        if em.on("tensor"):
                            em.wait_cur(aev)
                            em.wait_cur(anv)
                            if bt >= 1:
                                em.wait_n(atpv, b_tpv + bt)
                            ins = None
                            for u in range(4):
                                i = 4 * bt + u
                                ins = em.e.transpose(C.pTb.ap()[:, 128 * u:128 * (u + 1)],
                                                     C.scrA.ap()[:, 128 * i:128 * (i + 1)],
                                                     C.eyeS.ap())
                        else:
                            ins = None
                        em.sig(atp, ins)
                        if em.on("vector"):
                            em.wait_n(atp, b_tp + bt + 1)
                            ins = em.e.tensor_copy(
                                C.kftm.ap()[:, 512 * bt:512 * (bt + 1)], C.pTb.ap())
                        else:
                            ins = None
                        em.sig(atpv, ins)
                    if em.on("tensor"):
                        em.wait_n(atpv, b_tpv + 8)
                        em.wait_cur(anv)
                        for i in range(32):
                            em.e.matmul(C.pE.ap()[:, 0:128],
                                        C.kftm.ap()[:, 128 * i:128 * (i + 1)],
                                        C.vtm.ap()[:, 512 * i + 128 * ct:512 * i + 128 * (ct + 1)],
                                        start=(i == 0), stop=False)
                        ins = None
                        for i in range(32):
                            ins = em.e.matmul(C.pE.ap()[:, 128:129],
                                              C.kftm.ap()[:, 128 * i:128 * (i + 1)],
                                              C.ones_bf.ap(), start=False, stop=(i == 31))
                    else:
                        ins = None
                    em.sig(amem, ins)
                    if em.on("vector"):
                        em.wait_cur(amem)
                        em.wait_cur(atd)
                        ins = em.e.tensor_copy(C.mem_bf.ap()[:, 0:129], C.pE.ap()[:, 0:129])
                    else:
                        ins = None
                    em.sig(amemv, ins)
                    if em.on("vector"):
                        em.wait_cur(aev)
                        ins = em.e.tensor_scalar(C.scrB.ap()[:, 0:1024],
                                                 C.qfm.ap()[:, 1024 * hh:1024 * (hh + 1)],
                                                 0.0, None, OP.min)
                    else:
                        ins = None
                    em.sig(aeg, ins)
                    if em.on("scalar"):
                        em.wait_cur(aeg)
                        ins = em.e.activation(C.scrB.ap()[:, 1024:2048],
                                              C.scrB.ap()[:, 0:1024], AF.Exp)
                    else:
                        ins = None
                    em.sig(aes, ins)
                    if em.on("vector"):
                        em.wait_cur(aes)
                        em.wait_cur(atd)
                        ins = em.e.scalar_tensor_tensor(C.qf.ap(),
                                                        C.qfm.ap()[:, 1024 * hh:1024 * (hh + 1)],
                                                        1.0, C.scrB.ap()[:, 1024:2048],
                                                        OP.add, OP.max)
                    else:
                        ins = None
                    em.sig(aev, ins)

                    for j in range(2):
                        b_ts, b_st, b_tv = ats.count, ast.count, atv.count
                        rhsq = C.qfm.ap()[:, 1024 * hh + 512 * j:1024 * hh + 512 * (j + 1)]
                        def _sslot(p):
                            bank = C.pA if (p % 4) < 2 else C.pB
                            return bank.ap()[:, 512 * (p % 2):512 * (p % 2 + 1)]

                        def _ptslot(p):
                            return C.PT.ap()[:, 512 * (p % 4):512 * (p % 4 + 1)]

                        for p in range(33):
                            if em.on("tensor"):
                                if p == 0:
                                    em.wait_n(ep3, hj_done)
                                    em.wait_cur(arv)
                                    em.wait_cur(aev)
                                if p < 32:
                                    if p >= 4:
                                        em.wait_n(ast, b_st + p - 3)
                                    ins = em.e.matmul(_sslot(p),
                                                      kro[:, 128 * p:128 * (p + 1)],
                                                      rhsq, start=True, stop=True)
                                else:
                                    ins = None
                            else:
                                ins = None
                            if p < 32:
                                em.sig(ats, ins)
                            if p >= 1:
                                pp = p - 1
                                if em.on("tensor"):
                                    em.wait_n(ast, b_st + pp + 1)
                                    em.wait_cur(amemv)
                                    em.e.matmul(
                                        C.pC.ap(),
                                        C.vtm.ap()[:, 512 * pp + 128 * ct:512 * pp + 128 * (ct + 1)],
                                        _ptslot(pp),
                                        start=(pp == 0), stop=(pp == 31))
                                    ins = em.e.matmul(
                                        C.pD.ap()[0:1, 0:512], C.ones_bf.ap(),
                                        _ptslot(pp),
                                        start=(pp == 0), stop=(pp == 31))
                                else:
                                    ins = None
                                em.sig(atv, ins)
                            if p < 32:
                                if em.on("scalar"):
                                    em.wait_n(ats, b_ts + p + 1)
                                    if p >= 4:
                                        em.wait_n(atv, b_tv + p - 3)
                                    if p == 0:
                                        em.wait_cur(anv)
                                    ins = em.e.activation(
                                        _ptslot(p), _sslot(p), AF.Exp,
                                        bias=C.negb.ap()[:, hh % 2:hh % 2 + 1], scale=TAU)
                                else:
                                    ins = None
                                em.sig(ast, ins)

                        if em.on("vector"):
                            em.wait_n(atv, b_tv + 32)
                            ins = em.e.reciprocal(C.srow.ap()[0:1, 0:512], C.pD.ap()[0:1, 0:512])
                        else:
                            ins = None
                        fence(em, "vector", "sf_v", ins)
                        if em.on("vector"):
                            ins = em.e.tensor_scalar(C.srow.ap()[0:1, 512:1024],
                                                     C.srow.ap()[0:1, 0:512],
                                                     C.g1m.ap()[0:1, PERM_HEADS[hh]:PERM_HEADS[hh] + 1],
                                                     None, OP.mult)
                        else:
                            ins = None
                        fence(em, "vector", "sf_v", ins)
                        if em.on("vector"):
                            ins = em.e.tensor_copy(C.brow.ap()[0:1, 0:512],
                                                   C.srow.ap()[0:1, 512:1024])
                        else:
                            ins = None
                        em.sig(ep1, ins)
                        if em.on("tensor"):
                            em.wait_cur(ep1)
                            em.wait_cur(amemv)
                            em.e.matmul(C.pD.ap()[0:1, 0:512], C.mem_bf.ap()[:, 128:129],
                                        C.qf.ap()[:, 512 * j:512 * (j + 1)],
                                        start=True, stop=True)
                            ins = em.e.matmul(C.pB.ap()[:, 0:512], C.mem_bf.ap()[:, 0:128],
                                              C.qf.ap()[:, 512 * j:512 * (j + 1)],
                                              start=True, stop=True)
                        else:
                            ins = None
                        em.sig(atd, ins)
                        if em.on("vector"):
                            em.wait_cur(atd)
                            ins = em.e.reciprocal(C.srow.ap()[0:1, 0:512], C.pD.ap()[0:1, 0:512])
                        else:
                            ins = None
                        fence(em, "vector", "sf_v", ins)
                        if em.on("vector"):
                            ins = em.e.tensor_scalar(C.srow.ap()[0:1, 512:1024],
                                                     C.srow.ap()[0:1, 0:512],
                                                     C.grow.ap()[0:1, PERM_HEADS[hh]:PERM_HEADS[hh] + 1],
                                                     None, OP.mult)
                        else:
                            ins = None
                        fence(em, "vector", "sf_v", ins)
                        if em.on("vector"):
                            ins = em.e.tensor_copy(C.brow.ap()[0:1, 512:1024],
                                                   C.srow.ap()[0:1, 512:1024])
                        else:
                            ins = None
                        em.sig(ep2, ins)
                        if em.on("tensor"):
                            em.wait_cur(ep2)
                            em.e.matmul(C.pA.ap()[:, 0:512], C.onescol_bf.ap(),
                                        C.brow.ap()[0:1, 0:512], start=True, stop=True)
                            ins = em.e.matmul(C.pA.ap()[:, 512:1024], C.onescol_bf.ap(),
                                              C.brow.ap()[0:1, 512:1024], start=True, stop=True)
                        else:
                            ins = None
                        em.sig(atb, ins)
                        if em.on("vector"):
                            em.wait_cur(atb)
                            em.wait_cur(aes)
                            em.e.tensor_copy(C.scrB.ap()[:, 0:512], C.pA.ap()[:, 0:512])
                            ins = em.e.tensor_copy(C.scrB.ap()[:, 512:1024],
                                                   C.pA.ap()[:, 512:1024])
                        else:
                            ins = None
                        fence(em, "vector", "sf_v", ins)
                        if em.on("vector"):
                            em.e.tensor_tensor(C.PT.ap()[:, 0:512], C.pC.ap(),
                                               C.scrB.ap()[:, 0:512], OP.mult)
                            ins = em.e.tensor_tensor(C.PT.ap()[:, 512:1024],
                                                     C.pB.ap()[:, 0:512],
                                                     C.scrB.ap()[:, 512:1024], OP.mult)
                        else:
                            ins = None
                        fence(em, "vector", "sf_v", ins)
                        if em.on("vector"):
                            em.wait_cur(acb[0])
                            ins = em.e.tensor_tensor(C.cbuf.ap(),
                                                     C.PT.ap()[:, 0:512],
                                                     C.PT.ap()[:, 512:1024], OP.add)
                        else:
                            ins = None
                        em.sig(ep3, ins)
                        if em.on("sync"):
                            em.wait_cur(ep3)
                            ins = em.e.dma_start(
                                out=C.comb_h[128 * hh:128 * (hh + 1), 512 * j:512 * (j + 1)],
                                in_=C.cbuf.ap())
                        else:
                            ins = None
                        em.sig(acb[0], ins)
                        hj_done += 1
                    hrec.append({"ats": ED["a_ts"].count, "ast": ED["a_st"].count,
                                 "atv": ED["a_tv"].count, "ans": ED["a_nrm_s"].count,
                                 "aeg": ED["a_elu_g"].count, "aev": ED["a_elu_v"].count})
            barrier(em, C)

        run_phase(block, em, C, ph_attn)
        sc9.close()

        sc10 = contextlib.ExitStack()
        C.wobf = sc10.enter_context(nc.sbuf_tensor("wobf", [128, 16 * H], BF16))
        C.combL = sc10.enter_context(nc.sbuf_tensor("combL", [128, 16 * TOK], BF16))
        C.b2B = sc10.enter_context(nc.sbuf_tensor("b2B", [128, H], BF16))
        C.rowb2 = sc10.enter_context(nc.sbuf_tensor("rowb2", [1, H], F32))
        C.xbo = [sc10.enter_context(nc.sbuf_tensor(f"xbo{i}", [128, H], F32)) for i in range(2)]
        C.aco = [sc10.enter_context(nc.sbuf_tensor(f"aco{i}", [128, H], F32)) for i in range(2)]
        C.pWo = [sc10.enter_context(nc.psum_tensor(f"pWo{i}", [128, 512], F32)) for i in range(2)]

        def ph_wo(em, C):
            ld = ED["wo_ld"]
            ldx = [ED["wo_ldx0"], ED["wo_ldx1"]]
            st = [ED["wo_st0"], ED["wo_st1"]]
            mm, ev = ED["wo_mm"], ED["wo_ev"]
            b_ld, b_mm, b_ev = ld.count, mm.count, ev.count
            ins = (em.e.dma_start(out=C.wobf.ap().rearrange("p (c o) -> p c o", c=16),
                                  in_=C.woT.rearrange("(c p) o -> p c o", p=128))
                   if em.on("gpsimd") else None)
            em.sig(ED["gled"], ins)
            ins = (em.e.dma_start(out=C.combL.ap().rearrange("p (c t) -> p c t", c=16),
                                  in_=C.comb_h.rearrange("(c p) t -> p c t", p=128))
                   if em.on("sync") else None)
            em.sig(ld, ins)
            ins = (em.e.dma_start(out=C.rowb2.ap(), in_=C.b2r) if em.on("sync") else None)
            em.sig(ld, ins)
            for sbc in range(4):
                if em.on("tensor"):
                    em.wait_n(ld, b_ld + 2)
                    if sbc >= 2:
                        em.wait_n(ev, b_ev + sbc - 1)
                    ins = em.e.matmul(C.pWo[sbc % 2].ap(), C.onescol_f.ap(),
                                      C.rowb2[0:1, 512 * sbc:512 * (sbc + 1)],
                                      start=True, stop=True)
                else:
                    ins = None
                em.sig(mm, ins)
                if em.on("vector"):
                    em.wait_n(mm, b_mm + sbc + 1)
                    ins = em.e.tensor_copy(C.b2B.ap()[:, 512 * sbc:512 * (sbc + 1)],
                                           C.pWo[sbc % 2].ap())
                else:
                    ins = None
                em.sig(ev, ins)
            b_mm, b_ev = mm.count, ev.count
            for t in range(8):
                if em.on("sync"):
                    if t >= 2:
                        em.wait_n(ev, b_ev + 4 * (t - 1))
                    ins = em.e.dma_start(out=C.xbo[t % 2].ap(),
                                         in_=C.xo[128 * t:128 * (t + 1), :])
                else:
                    ins = None
                em.sig(ldx[t % 2], ins)
                for jo in range(4):
                    idx = 4 * t + jo
                    if em.on("tensor"):
                        em.wait_n(ld, b_ld + 2)
                        em.wait_cur(ED["gled"])
                        em.wait_n(ev, b_ev + idx - 1)
                        ins = None
                        for cc in range(16):
                            ins = em.e.matmul(
                                C.pWo[idx % 2].ap(),
                                C.combL.ap()[:, 1024 * cc + 128 * t:1024 * cc + 128 * (t + 1)],
                                C.wobf.ap()[:, 2048 * cc + 512 * jo:2048 * cc + 512 * (jo + 1)],
                                start=(cc == 0), stop=(cc == 15))
                    else:
                        ins = None
                    em.sig(mm, ins)
                    if em.on("vector"):
                        em.wait_n(mm, b_mm + idx + 1)
                        em.wait_cur(ldx[t % 2])
                        em.wait_cur(st[t % 2])
                        if idx > 0:
                            em.wait_n(ev, b_ev + idx)
                        ins = em.e.tensor_tensor(C.scrW.ap(), C.pWo[idx % 2].ap(),
                                                 C.xbo[t % 2].ap()[:, 512 * jo:512 * (jo + 1)],
                                                 OP.add)
                    else:
                        ins = None
                    fence(em, "vector", "sf_v", ins)
                    if em.on("vector"):
                        ins = em.e.tensor_tensor(
                            C.aco[t % 2].ap()[:, 512 * jo:512 * (jo + 1)],
                            C.scrW.ap(), C.b2B.ap()[:, 512 * jo:512 * (jo + 1)], OP.add)
                    else:
                        ins = None
                    em.sig(ev, ins)
                if em.on("sync"):
                    em.wait_n(ev, b_ev + 4 * t + 4)
                    ins = em.e.dma_start(out=C.acc_h[128 * t:128 * (t + 1), :],
                                         in_=C.aco[t % 2].ap())
                else:
                    ins = None
                em.sig(st[t % 2], ins)
            barrier(em, C)

        C.scrW = sc10.enter_context(nc.sbuf_tensor("scrW", [128, 512], F32))
        run_phase(block, em, C, ph_wo)
        sc10.close()
        sc_qfm.close()

        sc_gs = contextlib.ExitStack()
        C.gsum = sc_gs.enter_context(nc.sbuf_tensor("gsum", [128, 128], F32))
        C.g2sum = sc_gs.enter_context(nc.sbuf_tensor("g2sum", [128, 128], F32))
        sc11 = contextlib.ExitStack()
        C.xq1s = sc11.enter_context(nc.sbuf_tensor("xq1s", [128, 16 * TOK], BF16))
        C.q1bf = [sc11.enter_context(nc.sbuf_tensor(f"q1bf{i}", [128, 16 * 512], BF16)) for i in range(2)]
        C.b1B = sc11.enter_context(nc.sbuf_tensor("b1B", [128, INNER], BF16))
        C.rowI = sc11.enter_context(nc.sbuf_tensor("rowI", [1, INNER], F32))
        C.fsc = [sc11.enter_context(nc.sbuf_tensor(f"fsc{i}", [128, 512], F32)) for i in range(2)]
        C.fsc2 = [sc11.enter_context(nc.sbuf_tensor(f"fsc2_{i}", [128, 512], F32)) for i in range(2)]
        C.gout = [sc11.enter_context(nc.sbuf_tensor(f"gout{i}", [128, 512], BF16)) for i in range(2)]
        C.pm1 = [sc11.enter_context(nc.psum_tensor(f"pm1_{i}", [128, 512], F32)) for i in range(2)]

        def ph_f1(em, C):
            ld, mm, ev = ED["f1_ld"], ED["f1_mm"], ED["f1_ev"]
            ldq = [ED["f1_ldq0"], ED["f1_ldq1"]]
            st = [ED["f1_st0"], ED["f1_st1"]]
            s1, s2, g2, fin = ED["f1_s1"], ED["f1_s2"], ED["f1_g2"], ED["f1_fin"]
            b_ld, b_mm, b_ev = ld.count, mm.count, ev.count
            b_s1, b_s2, b_g2 = s1.count, s2.count, g2.count
            if em.on("vector"):
                ins = em.e.tensor_scalar(C.stats2.ap()[:, 24:32], C.stats.ap()[:, 32:40],
                                         C.wsB.ap()[:, 0:1], None, OP.mult)
            else:
                ins = None
            fence(em, "vector", "sf_v", ins)
            if em.on("vector"):
                ins = em.e.reciprocal(C.stats.ap()[:, 40:48], C.stats2.ap()[:, 24:32])
            else:
                ins = None
            em.sig(ev, ins)
            ins = (em.e.dma_start(out=C.xq1s.ap().rearrange("p (k t) -> p k t", k=16),
                                  in_=C.xq1_h.rearrange("(k p) t -> p k t", p=128))
                   if em.on("sync") else None)
            em.sig(ld, ins)
            ins = (em.e.dma_start(out=C.rowI.ap(), in_=C.b1r) if em.on("sync") else None)
            em.sig(ld, ins)
            for sbc in range(16):
                if em.on("tensor"):
                    em.wait_n(ld, b_ld + 2)
                    if sbc >= 2:
                        em.wait_n(ev, b_ev + 1 + sbc - 1)
                    ins = em.e.matmul(C.pm1[sbc % 2].ap(), C.onescol_f.ap(),
                                      C.rowI[0:1, 512 * sbc:512 * (sbc + 1)],
                                      start=True, stop=True)
                else:
                    ins = None
                em.sig(mm, ins)
                if em.on("vector"):
                    em.wait_n(mm, b_mm + sbc + 1)
                    ins = em.e.tensor_copy(C.b1B.ap()[:, 512 * sbc:512 * (sbc + 1)],
                                           C.pm1[sbc % 2].ap())
                else:
                    ins = None
                em.sig(ev, ins)
            b_mm, b_ev = mm.count, ev.count
            for jb in range(16):
                if em.on("gpsimd"):
                    if jb >= 2:
                        em.wait_n(mm, b_mm + 8 * (jb - 1))
                    ins = em.e.dma_start(
                        out=C.q1bf[jb % 2].ap().rearrange("p (k o) -> p k o", k=16),
                        in_=C.q1_h[:, 512 * jb:512 * (jb + 1)].rearrange(
                            "(k p) o -> p k o", p=128))
                else:
                    ins = None
                em.sig(ldq[jb % 2], ins)
                for t in range(8):
                    idx = 8 * jb + t
                    if em.on("tensor"):
                        em.wait_n(ld, b_ld + 2)
                        em.wait_cur(ldq[jb % 2])
                        em.wait_n(ev, b_ev + idx - 1)
                        ins = None
                        for kk in range(16):
                            ins = em.e.matmul(
                                C.pm1[idx % 2].ap(),
                                C.xq1s.ap()[:, 1024 * kk + 128 * t:1024 * kk + 128 * (t + 1)],
                                C.q1bf[jb % 2].ap()[:, 512 * kk:512 * (kk + 1)],
                                start=(kk == 0), stop=(kk == 15))
                    else:
                        ins = None
                    em.sig(mm, ins)
                    if em.on("vector"):
                        em.wait_n(mm, b_mm + idx + 1)
                        if idx >= 2:
                            em.wait_n(s1, b_s1 + idx - 1)
                        ins = em.e.scalar_tensor_tensor(
                            C.fsc[idx % 2].ap(), C.pm1[idx % 2].ap(),
                            C.stats.ap()[:, 40 + t:41 + t],
                            C.b1B.ap()[:, 512 * jb:512 * (jb + 1)], OP.mult, OP.add)
                    else:
                        ins = None
                    em.sig(ev, ins)
                    if em.on("scalar"):
                        em.wait_n(ev, b_ev + idx + 1)
                        if idx >= 2:
                            em.wait_n(s2, b_s2 + idx - 1)
                        ins = em.e.activation(C.fsc2[idx % 2].ap(), C.fsc[idx % 2].ap(),
                                              AF.Silu)
                    else:
                        ins = None
                    em.sig(s1, ins)
                    if em.on("scalar"):
                        em.wait_cur(st[idx % 2])
                        em.wait_n(s1, b_s1 + idx + 1)
                        if idx >= 2:
                            em.wait_n(g2, b_g2 + idx - 1)
                        ins = em.e.activation(C.gout[idx % 2].ap(), C.fsc2[idx % 2].ap(),
                                              AF.Gelu,
                                              accum_out=C.gsum.ap()[:, 16 * t + jb:16 * t + jb + 1])
                    else:
                        ins = None
                    em.sig(s2, ins)
                    if em.on("vector"):
                        em.wait_n(s2, b_s2 + idx + 1)
                        em.wait_n(s1, b_s1 + idx + 1)
                        ins = em.e.scalar_tensor_tensor(
                            C.fsc[idx % 2].ap(), C.gout[idx % 2].ap(), 0.0,
                            C.gout[idx % 2].ap(), OP.bypass, OP.mult,
                            accum_out=C.g2sum.ap()[:, 16 * t + jb:16 * t + jb + 1])
                    else:
                        ins = None
                    em.sig(g2, ins)
                    if em.on("sync"):
                        em.wait_n(g2, b_g2 + idx + 1)
                        ins = em.e.dma_start(
                            out=C.g_h[128 * t:128 * (t + 1), 512 * jb:512 * (jb + 1)],
                            in_=C.gout[idx % 2].ap())
                    else:
                        ins = None
                    em.sig(st[idx % 2], ins)
            b_fin = fin.count
            for t in range(8):
                if em.on("vector"):
                    if t == 0:
                        em.wait_n(s2, b_s2 + 128)
                        em.wait_n(g2, b_g2 + 128)
                    ins = em.e.tensor_reduce(C.nrm.ap()[:, 0:1],
                                             C.gsum.ap()[:, 16 * t:16 * (t + 1)], AX.X, OP.add)
                else:
                    ins = None
                fence(em, "vector", "sf_v", ins)
                if em.on("vector"):
                    ins = em.e.tensor_scalar(C.stats.ap()[:, 48 + t:49 + t], C.nrm.ap()[:, 0:1],
                                             1.0 / float(INNER), None, OP.mult)
                else:
                    ins = None
                fence(em, "vector", "sf_v", ins)
                if em.on("vector"):
                    ins = em.e.tensor_reduce(C.nrm.ap()[:, 1:2],
                                             C.g2sum.ap()[:, 16 * t:16 * (t + 1)], AX.X, OP.add)
                else:
                    ins = None
                fence(em, "vector", "sf_v", ins)
                if em.on("vector"):
                    ins = em.e.tensor_scalar(C.nrm.ap()[:, 2:3], C.nrm.ap()[:, 1:2],
                                             1.0 / float(INNER), None, OP.mult)
                else:
                    ins = None
                fence(em, "vector", "sf_v", ins)
                if em.on("vector"):
                    ins = em.e.tensor_tensor(C.nrm.ap()[:, 1:2], C.stats.ap()[:, 48 + t:49 + t],
                                             C.stats.ap()[:, 48 + t:49 + t], OP.mult)
                else:
                    ins = None
                fence(em, "vector", "sf_v", ins)
                if em.on("vector"):
                    ins = em.e.tensor_tensor(C.nrm.ap()[:, 3:4], C.nrm.ap()[:, 2:3],
                                             C.nrm.ap()[:, 1:2], OP.subtract)
                else:
                    ins = None
                fence(em, "vector", "sf_v", ins)
                if em.on("vector"):
                    ins = em.e.tensor_scalar(C.nrm.ap()[:, 0:1], C.nrm.ap()[:, 3:4],
                                             EPS, None, OP.add)
                else:
                    ins = None
                em.sig(fin, ins)
                if em.on("scalar"):
                    em.wait_cur(fin)
                    ins = em.e.activation(C.nrm.ap()[:, 1:2], C.nrm.ap()[:, 0:1], AF.Sqrt)
                else:
                    ins = None
                em.sig(fin, ins)
                if em.on("vector"):
                    em.wait_cur(fin)
                    ins = em.e.reciprocal(C.stats.ap()[:, 56 + t:57 + t], C.nrm.ap()[:, 1:2])
                else:
                    ins = None
                em.sig(fin, ins)
            barrier(em, C)

        run_phase(block, em, C, ph_f1)
        sc11.close()

        sc12a = contextlib.ExitStack()
        C.gamB = sc12a.enter_context(nc.sbuf_tensor("gamB", [128, INNER], BF16))
        C.betB = sc12a.enter_context(nc.sbuf_tensor("betB", [128, INNER], BF16))
        sc12b = contextlib.ExitStack()
        C.rowI2 = sc12b.enter_context(nc.sbuf_tensor("rowI2", [1, INNER], F32))
        C.pf2 = [sc12b.enter_context(nc.psum_tensor(f"pf2_{i}", [128, 512], F32)) for i in range(2)]

        def ph_f2pre(em, C):
            ld, et, ev = ED["f2_ld"], ED["f2_t"], ED["f2_v"]
            b_ld, b_t, b_v = ld.count, et.count, ev.count
            for w in range(2):
                src = C.b1r if False else (C.lngr if w == 0 else C.lnbr)
                dst = C.gamB if w == 0 else C.betB
                if em.on("sync"):
                    if w == 1:
                        em.wait_n(et, b_t + 16)
                    ins = em.e.dma_start(out=C.rowI2.ap(), in_=src)
                else:
                    ins = None
                em.sig(ld, ins)
                for sbc in range(16):
                    it = 16 * w + sbc
                    if em.on("tensor"):
                        em.wait_n(ld, b_ld + w + 1)
                        if it >= 2:
                            em.wait_n(ev, b_v + it - 1)
                        ins = em.e.matmul(C.pf2[it % 2].ap(), C.onescol_f.ap(),
                                          C.rowI2[0:1, 512 * sbc:512 * (sbc + 1)],
                                          start=True, stop=True)
                    else:
                        ins = None
                    em.sig(et, ins)
                    if em.on("vector"):
                        em.wait_n(et, b_t + it + 1)
                        ins = em.e.tensor_copy(dst.ap()[:, 512 * sbc:512 * (sbc + 1)],
                                               C.pf2[it % 2].ap())
                    else:
                        ins = None
                    em.sig(ev, ins)
            barrier(em, C)

        run_phase(block, em, C, ph_f2pre)
        sc12b.close()

        sc12c = contextlib.ExitStack()
        C.bbA = sc12c.enter_context(nc.sbuf_tensor("bbA", [128, INNER], BF16))
        C.bbB = sc12c.enter_context(nc.sbuf_tensor("bbB", [128, INNER], BF16))
        C.i8z = sc12c.enter_context(nc.sbuf_tensor("i8z", [128, INNER // 16], I8))
        C.zf = sc12c.enter_context(nc.sbuf_tensor("zf", [128, INNER // 16], F32))
        C.d8z = sc12c.enter_context(nc.sbuf_tensor("d8z", [128, INNER // 16], I8))
        C.xq2f = sc12c.enter_context(nc.sbuf_tensor("xq2f", [128, 64 * 256], BF16))
        C.q2bf = sc12c.enter_context(nc.sbuf_tensor("q2bf", [128, 2 * 64 * 256], BF16))
        C.outb = [sc12c.enter_context(nc.sbuf_tensor(f"outb{i}", [128, H], F32)) for i in range(2)]
        C.accb = [sc12c.enter_context(nc.sbuf_tensor(f"accb{i}", [128, H], F32)) for i in range(2)]
        C.pm2 = [sc12c.enter_context(nc.psum_tensor(f"pm2_{i}", [128, 512], F32)) for i in range(2)]
        C.pTq = sc12c.enter_context(nc.psum_tensor("pTq", [128, 512], BF16))

        def ph_f23(em, C):
            ld, fv, fs, ffin, ftt = ED["f2_ld"], ED["f2_v"], ED["f2_s"], ED["f2_fin"], ED["f2_t"]
            f3la = ED["f3_lda"]
            f3lq = [ED["f3_ldq0"], ED["f3_ldq1"]]
            f3mm, f3ev, f3t, f3tv = (ED["f3_mm"], ED["f3_ev"], ED["f3_t"], ED["f3_tv"])
            sto = ED["st_out"]
            b_fs, b_fv = fs.count, fv.count
            for qt in range(4):
                for tt in range(2):
                    t = 2 * qt + tt
                    if em.on("sync"):
                        if qt >= 1:
                            em.wait_n(f3ev, 16 * qt)
                        ins = em.e.dma_start(out=C.accb[tt].ap(),
                                             in_=C.acc_h[128 * t:128 * (t + 1), :])
                    else:
                        ins = None
                    em.sig(f3la, ins)
                for tt in range(2):
                    t = 2 * qt + tt
                    if em.on("sync"):
                        em.wait_n(fs, fs.count)
                        ins = em.e.dma_start(out=C.bbA.ap(),
                                             in_=C.g_h[128 * t:128 * (t + 1), :])
                    else:
                        ins = None
                    em.sig(ld, ins)
                    if em.on("vector"):
                        em.wait_cur(ld)
                        em.wait_cur(f3t)
                        ins = em.e.scalar_tensor_tensor(C.bbB.ap(), C.bbA.ap(),
                                                        C.stats.ap()[:, 48 + t:49 + t],
                                                        C.gamB.ap(), OP.subtract, OP.mult)
                    else:
                        ins = None
                    fence(em, "vector", "sf_v", ins)
                    if em.on("vector"):
                        ins = em.e.scalar_tensor_tensor(C.bbA.ap(), C.bbB.ap(),
                                                        C.stats.ap()[:, 56 + t:57 + t],
                                                        C.betB.ap(), OP.mult, OP.add)
                    else:
                        ins = None
                    fence(em, "vector", "sf_v", ins)
                    if em.on("vector"):
                        ins = em.e.scalar_tensor_tensor(C.bbB.ap(), C.bbA.ap(), 0.0,
                                                        C.bbA.ap(), OP.bypass, OP.mult,
                                                        accum_out=C.stats2.ap()[:, t:t + 1])
                    else:
                        ins = None
                    fence(em, "vector", "sf_v", ins)
                    if em.on("vector"):
                        ins = em.e.tensor_reduce(C.stats2.ap()[:, 24 + t:25 + t],
                                                 C.bbA.ap(), AX.X, OP.max,
                                                 apply_absolute_value=True)
                    else:
                        ins = None
                    fence(em, "vector", "sf_v", ins)
                    if em.on("vector"):
                        ins = em.e.tensor_scalar(C.nrm.ap()[:, 0:1], C.stats2.ap()[:, t:t + 1],
                                                 1.0 / float(INNER), EPS, OP.mult, OP.add)
                    else:
                        ins = None
                    em.sig(ffin, ins)
                    if em.on("scalar"):
                        em.wait_cur(ffin)
                        ins = em.e.activation(C.nrm.ap()[:, 1:2], C.nrm.ap()[:, 0:1], AF.Sqrt)
                    else:
                        ins = None
                    em.sig(ffin, ins)
                    if em.on("vector"):
                        em.wait_cur(ffin)
                        ins = em.e.reciprocal(C.nrm.ap()[:, 2:3], C.nrm.ap()[:, 1:2])
                    else:
                        ins = None
                    fence(em, "vector", "sf_v", ins)
                    if em.on("vector"):
                        ins = em.e.tensor_tensor(C.nrm.ap()[:, 3:4], C.nrm.ap()[:, 2:3],
                                                 C.stats2.ap()[:, 24 + t:25 + t], OP.mult)
                    else:
                        ins = None
                    fence(em, "vector", "sf_v", ins)
                    if em.on("vector"):
                        ins = em.e.tensor_scalar(C.nrm.ap()[:, 0:1], C.nrm.ap()[:, 3:4],
                                                 1e-5, None, OP.max)
                    else:
                        ins = None
                    fence(em, "vector", "sf_v", ins)
                    if em.on("vector"):
                        ins = em.e.reciprocal(C.nrm.ap()[:, 1:2], C.nrm.ap()[:, 0:1])
                    else:
                        ins = None
                    fence(em, "vector", "sf_v", ins)
                    if em.on("vector"):
                        ins = em.e.tensor_scalar(C.nrm.ap()[:, 3:4], C.nrm.ap()[:, 1:2],
                                                 127.0, None, OP.mult)
                    else:
                        ins = None
                    fence(em, "vector", "sf_v", ins)
                    if em.on("vector"):
                        em.e.tensor_tensor(C.stats2.ap()[:, 8 + t:9 + t], C.nrm.ap()[:, 2:3],
                                           C.nrm.ap()[:, 3:4], OP.mult)
                        ins = em.e.tensor_scalar(C.nrm.ap()[:, 0:1], C.nrm.ap()[:, 3:4],
                                                 C.wsB.ap()[:, 1:2], None, OP.mult)
                    else:
                        ins = None
                    fence(em, "vector", "sf_v", ins)
                    if em.on("vector"):
                        ins = em.e.reciprocal(C.stats2.ap()[:, 16 + t:17 + t],
                                              C.nrm.ap()[:, 0:1])
                    else:
                        ins = None
                    em.sig(ffin, ins)
                    for hh in range(16):
                        u = 16 * t + hh
                        hsz = INNER // 16
                        hsl = slice(hsz * hh, hsz * (hh + 1))
                        if em.on("scalar"):
                            em.wait_cur(ffin)
                            if u >= 1:
                                em.wait_n(fv, b_fv + 4 * (u - 1) + 3)
                            ins = em.e.activation(C.zf.ap(), C.bbA.ap()[:, hsl],
                                                  AF.Identity,
                                                  scale=C.stats2.ap()[:, 8 + t:9 + t])
                        else:
                            ins = None
                        em.sig(fs, ins)
                        if em.on("vector"):
                            em.wait_n(fs, b_fs + 2 * u + 1)
                            if u >= 1:
                                em.wait_n(fv, b_fv + 4 * (u - 1) + 3)
                            ins = em.e.tensor_scalar(C.zf.ap(), C.zf.ap(),
                                                     12582912.0, -12582912.0,
                                                     OP.add, OP.add)
                        else:
                            ins = None
                        em.sig(fv, ins)
                        if em.on("scalar"):
                            em.wait_n(fs, b_fs + 2 * u + 1)
                            em.wait_n(fv, b_fv + 4 * u + 1)
                            ins = em.e.copy(C.i8z.ap(), C.zf.ap())
                        else:
                            ins = None
                        em.sig(fs, ins)
                        if em.on("vector"):
                            em.wait_n(fs, b_fs + 2 * u + 2)
                            em.wait_n(fv, b_fv + 4 * u + 1)
                            ins = em.e.tensor_tensor(C.zf.ap(), C.zf.ap(), C.i8z.ap(),
                                                     OP.subtract)
                        else:
                            ins = None
                        em.sig(fv, ins)
                        if em.on("vector"):
                            em.wait_n(fv, b_fv + 4 * u + 2)
                            if u >= 1:
                                em.wait_n(fv, b_fv + 4 * u)
                            ins = em.e.tensor_scalar(C.d8z.ap(), C.zf.ap(), 2.0, None,
                                                     OP.mult)
                        else:
                            ins = None
                        em.sig(fv, ins)
                        if em.on("vector"):
                            em.wait_n(fv, b_fv + 4 * u + 3)
                            ins = em.e.tensor_tensor(C.bbB.ap()[:, hsl], C.i8z.ap(),
                                                     C.d8z.ap(), OP.add)
                        else:
                            ins = None
                        em.sig(fv, ins)
                    b_t3, b_tv3 = f3t.count, f3tv.count
                    xq2v = C.xq2f.ap().rearrange("p (r c) -> p r c", c=256)
                    for bt in range(16):
                        if em.on("tensor"):
                            em.wait_cur(fv)
                            if bt >= 1:
                                em.wait_n(f3tv, b_tv3 + bt)
                            ins = None
                            for u in range(4):
                                rr = 4 * bt + u
                                ins = em.e.transpose(C.pTq.ap()[:, 128 * u:128 * (u + 1)],
                                                     C.bbB.ap()[:, 128 * rr:128 * (rr + 1)],
                                                     C.eyeS.ap())
                        else:
                            ins = None
                        em.sig(f3t, ins)
                        if em.on("vector"):
                            em.wait_n(f3t, b_t3 + bt + 1)
                            ins = em.e.tensor_copy(
                                xq2v[:, 4 * bt:4 * (bt + 1), 128 * tt:128 * (tt + 1)],
                                C.pTq.ap())
                        else:
                            ins = None
                        em.sig(f3tv, ins)
                b_mm3, b_ev3 = f3mm.count, f3ev.count
                for ob in range(8):
                    if em.on("gpsimd"):
                        em.wait_n(f3mm, b_mm3)
                        if ob >= 2:
                            em.wait_n(f3mm, b_mm3 + 2 * (ob - 1))
                        base_q = 16384 * (ob % 2)
                        for rh in range(2):
                            ins = em.e.dma_start(
                                out=C.q2bf.ap()[:, base_q + 8192 * rh:base_q + 8192 * (rh + 1)]
                                    .rearrange("p (r o) -> p r o", r=32),
                                in_=C.q2_h[4096 * rh:4096 * (rh + 1),
                                           256 * ob:256 * (ob + 1)].rearrange(
                                    "(r p) o -> p r o", p=128))
                            em.sig(f3lq[ob % 2], ins)
                    else:
                        em.sig(f3lq[ob % 2], None)
                        em.sig(f3lq[ob % 2], None)
                    for tt in range(2):
                        t = 2 * qt + tt
                        idx = 2 * ob + tt
                        if em.on("tensor"):
                            em.wait_cur(f3la)
                            em.wait_cur(f3lq[ob % 2])
                            em.wait_cur(f3tv)
                            if idx >= 2:
                                em.wait_n(f3ev, b_ev3 + idx - 1)
                            ins = None
                            for rr in range(64):
                                ins = em.e.matmul(
                                    C.pm2[idx % 2].ap()[:, 0:256],
                                    C.xq2f.ap()[:, 256 * rr + 128 * tt:256 * rr + 128 * (tt + 1)],
                                    C.q2bf.ap()[:, 16384 * (ob % 2) + 256 * rr:
                                                 16384 * (ob % 2) + 256 * (rr + 1)],
                                    start=(rr == 0), stop=(rr == 63))
                        else:
                            ins = None
                        em.sig(f3mm, ins)
                        if em.on("vector"):
                            em.wait_n(f3mm, b_mm3 + idx + 1)
                            if qt >= 1 and idx < 2:
                                em.wait_n(sto, 2 * qt)
                            ins = em.e.scalar_tensor_tensor(
                                C.outb[tt].ap()[:, 256 * ob:256 * (ob + 1)],
                                C.pm2[idx % 2].ap()[:, 0:256], C.stats2.ap()[:, 16 + t:17 + t],
                                C.accb[tt].ap()[:, 256 * ob:256 * (ob + 1)],
                                OP.mult, OP.add)
                        else:
                            ins = None
                        em.sig(f3ev, ins)
                for tt in range(2):
                    t = 2 * qt + tt
                    if em.on("sync"):
                        em.wait_n(f3ev, b_ev3 + 16)
                        ins = em.e.dma_start(out=C.out[128 * t:128 * (t + 1), :],
                                             in_=C.outb[tt].ap())
                    else:
                        ins = None
                    em.sig(sto, ins)
            if em.on("sync"):
                em.wait_n(sto, sto.count)
            barrier(em, C)

        run_phase(block, em, C, ph_f23)
        sc12c.close()
        sc12a.close()
        sc_gs.close()

    return



_CACHE = {}


def _build():
    if "nc" not in _CACHE:
        nc, C, em = build_nc()
        emit_all(nc, C, em)
        _CACHE["nc"] = nc
    return _CACHE["nc"]


def make_in_maps(inputs):
    x = np.ascontiguousarray(np.asarray(inputs["L0_attn_hidden_states"]))
    Wq = np.asarray(inputs["L0_attn_Wq"])
    Wk = np.asarray(inputs["L0_attn_Wk"])
    Wv = np.asarray(inputs["L0_attn_Wv"])
    Wo = np.asarray(inputs["L0_attn_Wo"])
    gate = np.ascontiguousarray(np.asarray(inputs["L0_attn_gate"]).reshape(1, NH))
    w1 = np.asarray(inputs["L0_ffn_w1"])
    b1 = np.ascontiguousarray(np.asarray(inputs["L0_ffn_b1"]).reshape(1, INNER))
    lng = np.ascontiguousarray(np.asarray(inputs["L0_ffn_ln_g"]).reshape(1, INNER))
    lnb = np.ascontiguousarray(np.asarray(inputs["L0_ffn_ln_b"]).reshape(1, INNER))
    w2 = np.asarray(inputs["L0_ffn_w2"])
    b2 = np.ascontiguousarray(np.asarray(inputs["L0_ffn_b2"]).reshape(1, H))

    wqT = np.ascontiguousarray(Wq.T[:, CHAN_PERM])
    wkT = np.ascontiguousarray(Wk.T[:, CHAN_PERM])
    wvT = np.ascontiguousarray(Wv.T[:, CHAN_PERM])
    woT = np.ascontiguousarray(Wo.T[CHAN_PERM, :])
    w1T = np.ascontiguousarray(w1.T)
    w2T = np.ascontiguousarray(w2.T)
    tabc, tabs = _host_tables()

    in_maps = []
    for rho in range(8):
        b, g = rho // 4, rho % 4
        xb = x[b]
        xbT = np.ascontiguousarray(xb.T)
        sl = slice(TOK * g, TOK * (g + 1))
        in_maps.append({
            "xT": xbT,
            "xTo": np.ascontiguousarray(xbT[:, sl]),
            "xo": np.ascontiguousarray(xb[sl]),
            "wqT": wqT, "wkT": wkT, "wvT": wvT, "woT": woT,
            "w1T": w1T, "w2T": w2T,
            "b1r": b1, "lngr": lng, "lnbr": lnb, "b2r": b2,
            "gater": gate,
            "tabc": tabc, "tabs": tabs,
            "tabco": np.ascontiguousarray(tabc[:, sl]),
            "tabso": np.ascontiguousarray(tabs[:, sl]),
        })

    return in_maps


def kernel(**inputs):
    in_maps = make_in_maps(inputs)
    nc = _build()
    res = run_bass_kernel_spmd(nc, in_maps, core_ids=list(range(8)))
    out = np.empty((2, S, H), np.float32)
    for rho in range(8):
        b, g = rho // 4, rho % 4
        out[b, TOK * g:TOK * (g + 1), :] = res.results[rho]["out"]
    return out
